# version 10
# baseline (speedup 1.0000x reference)
from contextlib import ExitStack

import numpy as np
import ml_dtypes

import concourse.bass as bass
import concourse.mybir as mybir
from concourse.bass_utils import run_bass_kernel_spmd
from concourse.bass_isa import ReduceOp
from concourse import library_config

F32 = mybir.dt.float32
BF16 = mybir.dt.bfloat16
FP8 = mybir.dt.float8e4
I32 = mybir.dt.int32
AF = mybir.ActivationFunctionType
ALU = mybir.AluOpType
AX = mybir.AxisListType

D = 2048
TPC = 1024
NSLOT = 8
CONV_DIM = 6144
IN_OUT = 8224
NH_M = 32
DST = 64
HP = 64
NH_A = 16
I_FFN = 8192

N_SUB_DEFAULT = 6


def gtile(c, s):
    return 8 * s + (c + s) % 8


class Sem:
    def __init__(self, nc, ctx, name):
        self.h = ctx.enter_context(nc.semaphore(name))
        self.n = 0

    def inc(self, inst, k=1):
        inst.then_inc(self.h, k)
        self.n += k
        return self.n

    def wait(self, eng, n=None):
        eng.wait_ge(self.h, self.n if n is None else n)


def build(n_sub=N_SUB_DEFAULT):
    nc = bass.Bass("TRN2", target_bir_lowering=False)

    def inp(name, shape, dt):
        return nc.dram_tensor(name, shape, dt, kind="ExternalInput").ap()

    x0 = inp("x0", [TPC, D], F32)
    x0h = inp("x0h", [128, D], F32)
    halo_scale = inp("halo_scale", [1, 24], F32)
    selb = inp("selb", [64, 64], F32)
    masks = inp("masks", [8, 128, 8 * 128], BF16)
    rope_cos = inp("rope_cos", [TPC, 64], F32)
    rope_sin = inp("rope_sin", [TPC, 64], F32)
    ident_f = inp("ident_f", [128, 128], F32)
    ident_b = inp("ident_b", [128, 128], BF16)
    ident_b64 = inp("ident_b64", [128, 64], BF16)
    ident_q = inp("ident_q", [128, 128], FP8)
    triu_f = inp("triu_f", [128, 128], F32)
    tril_i = inp("tril_i", [128, 128], mybir.dt.int8)

    w0T = inp("w0T", [D, D], BF16)
    b0row = inp("b0row", [1, D], BF16)
    WinT = inp("WinT", [64 * 128, D], F32)
    WinA = inp("WinA", [128, 512], F32)
    WoutT = inp("WoutT", [D, D], F32)
    convw = inp("convw", [128, 192], F32)
    convb = inp("convb", [128, 48], F32)
    zbF = inp("zbF", [128, 16], F32)
    Drow = inp("Drow", [1, NH_M], F32)
    wqkv = inp("wqkv", [12 * 128, 16 * 512], F32)
    wo_a = inp("wo_a", [128, 16 * D], F32)
    grow = inp("grow", [1, NH_A], F32)
    WgT = inp("WgT", [I_FFN, D], I32)
    WuT = inp("WuT", [I_FFN, D], I32)
    dwT = inp("dwT", [I_FFN, D], I32)
    gs_in = inp("gs_in", [64, 128], F32)
    us_in = inp("us_in", [64, 128], F32)
    ds_in = inp("ds_in", [16, 128], F32)

    out = nc.dram_tensor("out", [TPC, D], F32, kind="ExternalOutput").ap()

    def idram(name, shape, dt):
        return nc.dram_tensor(name, shape, dt, kind="Internal").ap()

    def idram_sh(name, shape, dt):
        return nc.dram_tensor(name, shape, dt, kind="Internal",
                              addr_space="Shared").ap()

    x_spill = idram("x_spill", [TPC, D], F32)
    scal_d = idram("scal_d", [1, 4], F32)
    Wg_c = idram("Wg_c", [I_FFN, D], BF16)
    Wu_c = idram("Wu_c", [I_FFN, D], BF16)
    dw_c = idram("dw_c", [I_FFN, D], BF16)
    xs_src = [idram(f"xs_src{i}", [1, 16], F32) for i in range(2)]
    xs_dst = [idram_sh(f"xs_dst{i}", [8, 1, 16], F32) for i in range(2)]
    cd_src = idram("cd_src", [8, NH_M], F32)
    cd_dst = idram_sh("cd_dst", [64, NH_M], F32)
    st_src = idram("st_src", [DST, NSLOT * 2048], BF16)
    st_dst = idram_sh("st_dst", [8 * DST, NSLOT * 2048], BF16)
    yp_d = [idram(f"yp_d{i}", [128, 1024], BF16) for i in range(16)]
    svz_d = [idram(f"svz_d{i}", [128, 1048], BF16) for i in range(16)]
    kt_src = [idram(f"kt_src{i}", [1024, 1024], BF16) for i in range(2)]
    kt_dst = [idram_sh(f"kt_dst{i}", [8 * 1024, 1024], BF16) for i in range(2)]
    v_src = [idram(f"v_src{i}", [128, 8 * 1024], BF16) for i in range(2)]
    v_dst = [idram_sh(f"v_dst{i}", [8 * 128, 8 * 1024], BF16)
             for i in range(2)]

    with ExitStack() as ctx:
        def sb(name, shape, dt):
            return ctx.enter_context(nc.sbuf_tensor(name, shape, dt))

        xr = sb("xr", [128, NSLOT * D], F32)
        stat_nm = sb("stat_nm", [128, 16], F32)
        stat_rs = sb("stat_rs", [128, 16], F32)
        stat_ab = sb("stat_ab", [128, 16], F32)
        colb = sb("colb", [128, 8], F32)
        onesb_col = sb("onesb_col", [128, 1], BF16)
        onesb_row = sb("onesb_row", [1, 128], BF16)
        idf = sb("idf", [128, 128], F32)
        idb = sb("idb", [128, 128], BF16)
        idb64 = sb("idb64", [128, 64], BF16)
        idq = sb("idq", [128, 128], FP8)
        triuf = sb("triuf", [128, 128], F32)
        trili = sb("trili", [128, 128], mybir.dt.int8)
        scr = sb("scr", [128, D], F32)
        small = sb("small", [1, 144], F32)
        cst = sb("cst", [128, 4], F32)
        tpx = sb("tpx", [128, 128], F32)
        of32g = sb("of32g", [1, 128], F32)


        sems = {}
        for name in ("Lq", "Aq", "Vq", "Gq", "PE", "ACT", "DVE", "GP",
                     "CC", "ST", "E2", "WA", "WB", "GA", "GB"):
            sems[name] = Sem(nc, ctx, name)
        Lq, Aq, Vq, Gq = sems["Lq"], sems["Aq"], sems["Vq"], sems["Gq"]
        PE, ACTs, DVE, GP = sems["PE"], sems["ACT"], sems["DVE"], sems["GP"]
        CC, STs, E2 = sems["CC"], sems["ST"], sems["E2"]
        WS = [sems["WA"], sems["WB"]]
        GS = [sems["GA"], sems["GB"]]

        bnb = sb("bnb", [128, 216], F32)
        mv = sb("mv", [128, 18], F32)
        sc_sb = sb("sc_sb", [128, 16], F32)
        bq_sb = sb("bq_sb", [128, 16], F32)
        wsm = sb("wsm", [128, 16], F32)
        hstk = ExitStack()
        xh = hstk.enter_context(nc.sbuf_tensor("xh", [128, D], F32))

        block = ctx.enter_context(nc.Block())

        @block.sync
        def _(sync):
            for s in range(NSLOT):
                Lq.inc(sync.dma_start(out=xr[:, s * D:(s + 1) * D],
                                      in_=x0[s * 128:(s + 1) * 128, :]), 16)
            Lq.inc(sync.dma_start(out=xh[:], in_=x0h), 16)
            Lq.inc(sync.dma_start(out=idf[:], in_=ident_f), 16)
            Lq.inc(sync.dma_start(out=idb[:], in_=ident_b), 16)
            Lq.inc(sync.dma_start(out=idb64[:], in_=ident_b64), 16)
            Lq.inc(sync.dma_start(out=idq[:], in_=ident_q), 16)
            Lq.inc(sync.dma_start(out=triuf[:], in_=triu_f), 16)
            Lq.inc(sync.dma_start(out=trili[:], in_=tril_i), 16)
        prologue_loads = Lq.n

        @block.vector
        def _(vector):
            vector.memset(onesb_col[:], 1.0)
            vector.memset(onesb_row[:], 1.0)
            vector.memset(cst[:, 0:1], 1e-5)
            vector.memset(cst[:, 1:2], 1.0)
            vector.memset(cst[:, 2:3], 1e-6)
            vector.memset(of32g[:], 1.0)
            DVE.inc(vector.memset(cst[:, 3:4], 0.0), 1)
        cst_ready = DVE.n


        def emit_ln_stats(src_tile, ntile, absmax):
            @block.vector
            def _(vector):
                last = None
                for j in range(ntile):
                    for q in range(4):
                        last = vector.bn_stats(
                            bnb[:, j * 24 + q * 6: j * 24 + (q + 1) * 6],
                            src_tile(j)[:, q * 512:(q + 1) * 512])
                DVE.inc(last, 1)

            @block.vector
            def _(vector):
                DVE.wait(vector)
                last = None
                for j in range(ntile):
                    last = vector.bn_aggr(mv[:, 2 * j:2 * j + 2],
                                          bnb[:, j * 24:(j + 1) * 24])
                DVE.inc(last, 1)

            @block.vector
            def _(vector):
                DVE.wait(vector)
                DVE.inc(vector.tensor_scalar_mul(
                    stat_nm[:, 0:ntile], mv[:, 0:2 * ntile:2], -1.0), 1)
            d_mv = DVE.n

            @block.scalar
            def _(scalar):
                scalar.wait_ge(DVE.h, d_mv)
                ACTs.inc(scalar.activation(out=stat_rs[:, 0:ntile],
                                           in_=mv[:, 1:2 * ntile:2],
                                           func=AF.Sqrt, bias=cst[:, 0:1],
                                           scale=1.0), 1)

            @block.vector
            def _(vector):
                ACTs.wait(vector)
                DVE.inc(vector.reciprocal(stat_rs[:, 0:ntile],
                                          stat_rs[:, 0:ntile]), 1)

            if absmax:
                @block.vector
                def _(vector):
                    for j in range(ntile):
                        DVE.wait(vector)
                        DVE.inc(vector.tensor_scalar(
                            scr[:], src_tile(j), stat_nm[:, j:j + 1], None,
                            ALU.add), 1)
                        DVE.wait(vector)
                        DVE.inc(vector.reduce_max(
                            stat_ab[:, j:j + 1], scr[:], AX.X,
                            apply_absolute_value=True), 1)

                @block.vector
                def _(vector):
                    DVE.wait(vector)
                    DVE.inc(vector.tensor_mul(stat_ab[:, 0:ntile],
                                              stat_ab[:, 0:ntile],
                                              stat_rs[:, 0:ntile]), 1)
                    DVE.wait(vector)
                    DVE.inc(vector.reduce_max(colb[:, 1:2],
                                              stat_ab[:, 0:ntile], AX.X), 1)

        def emit_ffn0(idx, with_halo):
            ntile = 9 if with_halo else 8

            def src_tile(j):
                return xr[:, j * D:(j + 1) * D] if j < 8 else xh[:]

            @block.vector
            def _(vector):
                if idx == 0:
                    vector.wait_ge(Lq.h, prologue_loads)
                else:
                    vector.wait_ge(E2.h, E2.n)

            @block.scalar
            def _(scalar):
                if idx == 0:
                    scalar.wait_ge(Lq.h, prologue_loads)
                else:
                    scalar.wait_ge(E2.h, E2.n)

            emit_ln_stats(src_tile, ntile, absmax=True)

            @block.vector
            def _(vector):
                DVE.wait(vector)
                vector.memset(tpx[:, 1:128], 0.0)
                vector.memset(small[:, 1:16], 0.0)
                DVE.inc(vector.tensor_copy(tpx[:, 0:1], colb[:, 1:2]), 1)

            p_tx = PE.n
            with nc.psum_tensor(f"f0tpx{idx}", [128, 128], F32) as ptx:
                @block.tensor
                def _(tensor):
                    tensor.wait_ge(DVE.h, DVE.n)
                    PE.inc(tensor.transpose(ptx[:], tpx[:], idf[:]), 1)

                @block.vector
                def _(vector):
                    vector.wait_ge(PE.h, p_tx + 1)
                    DVE.inc(vector.reduce_max(small[:, 0:1], ptx[0:1, :],
                                              AX.X), 1)

            l_xs = Lq.n

            @block.sync
            def _(sync):
                sync.wait_ge(DVE.h, DVE.n)
                Lq.inc(sync.dma_start(out=xs_src[idx], in_=small[:, 0:16]), 16)

            c_xs = CC.n

            @block.gpsimd
            def _(gpsimd):
                gpsimd.wait_ge(Lq.h, l_xs + 16)
                CC.inc(gpsimd.collective_compute(
                    "AllGather", ALU.bypass, replica_groups=[list(range(8))],
                    ins=[xs_src[idx]], outs=[xs_dst[idx]]), 1)

            @block.sync
            def _(sync):
                sync.wait_ge(CC.h, c_xs + 1)
                Lq.inc(sync.dma_start(out=small[:, 16:144],
                                      in_=xs_dst[idx].flatten()[None, :]), 16)

            stk = ExitStack()
            def sbl(name, shape, dt):
                return stk.enter_context(nc.sbuf_tensor(name, shape, dt))
            wrot = [sbl(f"f0w{idx}a", [128, D], BF16),
                    sbl(f"f0w{idx}b", [128, D], BF16)]
            wq_sb = sbl(f"f0wq{idx}", [128, 16 * D], FP8)
            bias_bc = sbl(f"f0bias{idx}", [128, D], F32)
            bias_bb = sbl(f"f0biasb{idx}", [128, D], BF16)

            @block.scalar
            def _(scalar):
                Aq.inc(scalar.dma_start(out=bias_bb[:],
                                        in_=b0row.broadcast_to([128, D])), 16)
                Aq.wait(scalar)
                ACTs.inc(scalar.activation(out=bias_bc[:], in_=bias_bb[:],
                                           func=AF.Copy), 1)
            d_wr = DVE.n
            w_base = [WS[0].n, WS[1].n]

            @block.sync
            def _(sync):
                for k in range(16):
                    sync.wait_ge(DVE.h, d_wr + max(0, k - 1))
                    WS[k % 2].inc(sync.dma_start(
                        out=wrot[k % 2][:],
                        in_=w0T[k * 128:(k + 1) * 128, :]), 16)

            @block.vector
            def _(vector):
                last = None
                for k in range(16):
                    vector.wait_ge(WS[k % 2].h, w_base[k % 2] + 16 * (k // 2 + 1))
                    DVE.inc(vector.tensor_reduce(
                        wsm[:, k:k + 1], wrot[k % 2][:], AX.X, ALU.max,
                        apply_absolute_value=True), 1)

            @block.vector
            def _(vector):
                DVE.wait(vector)
                DVE.inc(vector.reduce_max(colb[:, 2:3], wsm[:, 0:16], AX.X), 1)

            @block.vector
            def _(vector):
                DVE.wait(vector)
                vector.wait_ge(PE.h, p_tx + 1)
                vector.wait_ge(Lq.h, Lq.n)
                DVE.inc(vector.tensor_copy(tpx[:, 0:1], colb[:, 2:3]), 1)

            p_tw = PE.n
            with nc.psum_tensor(f"f0tpw{idx}", [128, 128], F32) as ptw:
                @block.tensor
                def _(tensor):
                    tensor.wait_ge(DVE.h, DVE.n)
                    PE.inc(tensor.transpose(ptw[:], tpx[:], idf[:]), 1)

                @block.vector
                def _(vector):
                    vector.wait_ge(PE.h, p_tw + 1)
                    DVE.inc(vector.reduce_max(small[:, 1:2], ptw[0:1, :],
                                              AX.X), 1)

            @block.vector
            def _(vector):
                vector.wait_ge(Lq.h, Lq.n)
                DVE.wait(vector)
                DVE.inc(vector.reduce_max(small[:, 0:1], small[:, 16:144],
                                          AX.X), 1)
                DVE.wait(vector)
                vector.tensor_scalar_max(small[:, 0:1], small[:, 0:1], 1e-6)
                DVE.inc(vector.tensor_scalar_max(small[:, 1:2], small[:, 1:2],
                                                 1e-6), 1)
                DVE.wait(vector)
                vector.reciprocal(small[:, 2:3], small[:, 0:1])
                vector.reciprocal(small[:, 4:5], small[:, 1:2])
                DVE.inc(vector.tensor_mul(small[:, 3:4], small[:, 0:1],
                                          small[:, 1:2]), 1)

            @block.sync
            def _(sync):
                sync.wait_ge(DVE.h, DVE.n)
                Lq.inc(sync.dma_start(out=scal_d, in_=small[:, 2:6]), 16)
                sync.wait_ge(Lq.h, Lq.n)
                Lq.inc(sync.dma_start(
                    out=colb[:, 3:6],
                    in_=scal_d[0:1, 0:3].broadcast_to([128, 3])), 16)

            @block.vector
            def _(vector):
                vector.wait_ge(Lq.h, Lq.n)
                DVE.inc(vector.engine_nop(), 1)

            q_ready = DVE.n

            w_base = [WS[0].n, WS[1].n]

            @block.sync
            def _(sync):
                for k in range(16):
                    sync.wait_ge(DVE.h, q_ready + max(0, k - 1))
                    WS[k % 2].inc(sync.dma_start(
                        out=wrot[k % 2][:],
                        in_=w0T[k * 128:(k + 1) * 128, :]), 16)

            @block.vector
            def _(vector):
                for k in range(16):
                    vector.wait_ge(WS[k % 2].h, w_base[k % 2] + 16 * (k // 2 + 1))
                    DVE.inc(vector.tensor_scalar(
                        wq_sb[:, k * D:(k + 1) * D], wrot[k % 2][:],
                        colb[:, 5:6], None, ALU.mult), 1)

            wq_ready = DVE.n

            xq = [sbl(f"f0xq{idx}a", [128, D], FP8),
                  sbl(f"f0xq{idx}b", [128, D], FP8)]
            xqb = [sbl(f"f0xqb{idx}a", [128, D], BF16),
                   sbl(f"f0xqb{idx}b", [128, D], BF16)]
            xqT = sbl(f"f0xqT{idx}", [128, ntile * D], FP8)

            @block.vector
            def _(vector):
                vector.wait_ge(DVE.h, q_ready)
                DVE.inc(vector.tensor_scalar(sc_sb[:, 0:ntile],
                                             stat_rs[:, 0:ntile], colb[:, 3:4],
                                             None, ALU.mult), 1)
                DVE.wait(vector)
                DVE.inc(vector.tensor_mul(bq_sb[:, 0:ntile],
                                          stat_nm[:, 0:ntile],
                                          sc_sb[:, 0:ntile]), 1)

            xq_dve = DVE.n
            a_xq = ACTs.n
            d_xb = DVE.n
            p_tr = PE.n

            @block.scalar
            def _(scalar):
                scalar.wait_ge(DVE.h, xq_dve)
                for j in range(ntile):
                    if j >= 2:
                        scalar.wait_ge(DVE.h, d_xb + j - 1)
                    ACTs.inc(scalar.activation(out=xq[j % 2][:],
                                               in_=src_tile(j),
                                               func=AF.Identity,
                                               bias=bq_sb[:, j:j + 1],
                                               scale=sc_sb[:, j:j + 1]), 1)

            ntr = ntile * 16
            e_tr = E2.n
            with nc.psum_tensor(f"f0pta{idx}", [128, 128], BF16) as pta, \
                 nc.psum_tensor(f"f0ptb{idx}", [128, 128], BF16) as ptb:
                pst = [pta, ptb]

                @block.tensor
                def _(tensor):
                    for i in range(ntr):
                        j, k = divmod(i, 16)
                        if k == 0:
                            tensor.wait_ge(DVE.h, d_xb + j + 1)
                        if i >= 2:
                            tensor.wait_ge(E2.h, e_tr + i - 1)
                        PE.inc(tensor.transpose(
                            pst[i % 2][:],
                            xqb[j % 2][:, k * 128:(k + 1) * 128],
                            idb[:]), 1)

                @block.vector
                def _(vector):
                    for j in range(ntile):
                        vector.wait_ge(ACTs.h, a_xq + j + 1)
                        if j >= 2:
                            vector.wait_ge(PE.h, p_tr + 16 * (j - 1))
                        DVE.inc(vector.tensor_copy(xqb[j % 2][:],
                                                   xq[j % 2][:]), 1)
                        for k in range(16):
                            i = j * 16 + k
                            vector.wait_ge(PE.h, p_tr + i + 1)
                            E2.inc(vector.tensor_copy(
                                xqT[:, i * 128:(i + 1) * 128], pst[i % 2][:]), 1)

            ybf = sbl(f"f0y{idx}", [128, 2 * D], BF16)
            p_mm = PE.n
            e_ep = E2.n
            with ExitStack() as s2:
                po = [s2.enter_context(nc.psum_tensor(f"f0po{idx}{d}",
                                                      [128, 512], F32))
                      for d in range(8)]

                @block.tensor
                def _(tensor):
                    tensor.wait_ge(DVE.h, wq_ready)
                    tensor.wait_ge(E2.h, e_ep)
                    for j in range(ntile):
                        if j >= 2:
                            tensor.wait_ge(E2.h, e_ep + 2 * (j - 1))
                        bank = (j % 2) * 4
                        last = None
                        for k in range(16):
                            for d in range(4):
                                last = tensor.matmul(
                                    po[bank + d][:],
                                    xqT[:, (j * 16 + k) * 128:(j * 16 + k + 1) * 128],
                                    wq_sb[:, k * D + d * 512: k * D + (d + 1) * 512],
                                    start=(k == 0), stop=(k == 15))
                        PE.inc(last, 1)

                @block.vector
                def _(vector):
                    for j in range(ntile):
                        vector.wait_ge(PE.h, p_mm + j + 1)
                        bank = (j % 2) * 4
                        yb = ybf[:, (j % 2) * D:(j % 2 + 1) * D]
                        dst = src_tile(j)
                        last = None
                        for d in range(4):
                            last = vector.scalar_tensor_tensor(
                                yb[:, d * 512:(d + 1) * 512], po[bank + d][:],
                                colb[:, 4:5], bias_bc[:, d * 512:(d + 1) * 512],
                                ALU.mult, ALU.add)
                        E2.inc(last, 1)
                        DVE.wait(vector, DVE.n)
                        vector.wait_ge(E2.h, E2.n)
                        for d in range(4):
                            last = vector.scalar_tensor_tensor(
                                dst[:, d * 512:(d + 1) * 512],
                                yb[:, d * 512:(d + 1) * 512], 0.5,
                                dst[:, d * 512:(d + 1) * 512],
                                ALU.mult, ALU.add)
                        E2.inc(last, 1)

            stk.close()

        def emit_mamba():
            def src_tile(j):
                return xr[:, j * D:(j + 1) * D] if j < 8 else xh[:]

            jE, jD, jA, jP = E2.n, DVE.n, ACTs.n, PE.n

            @block.sync
            def _(sync):
                sync.wait_ge(E2.h, jE)

            @block.scalar
            def _(scalar):
                scalar.wait_ge(E2.h, jE)
                scalar.wait_ge(PE.h, jP)

            @block.vector
            def _(vector):
                vector.wait_ge(E2.h, jE)
                vector.wait_ge(ACTs.h, jA)

            @block.tensor
            def _(tensor):
                tensor.wait_ge(E2.h, jE)
                tensor.wait_ge(DVE.h, jD)

            @block.gpsimd
            def _(gpsimd):
                gpsimd.wait_ge(E2.h, jE)

            emit_ln_stats(src_tile, 9, absmax=False)

            @block.vector
            def _(vector):
                DVE.wait(vector)
                DVE.inc(vector.tensor_mul(sc_sb[:, 0:9], stat_nm[:, 0:9],
                                          stat_rs[:, 0:9]), 1)
            d_nmr = DVE.n

            mstk = ExitStack()

            def sbm(name, shape, dt):
                return mstk.enter_context(nc.sbuf_tensor(name, shape, dt))

            cw_sb = sbm("cw_sb", [128, 192], F32)
            cb_sb = sbm("cb_sb", [128, 48], F32)
            zb_sb = sbm("zb_sb", [128, 16], F32)
            dcol = sbm("dcol", [128, NH_M], F32)
            a_sb = sbm("a_sb", [32, 1048], F32)
            cum = sbm("cum", [32, 1024], F32)
            cumT = sbm("cumT", [128, 256], F32)
            sdt = sbm("sdt", [128, 256], F32)
            ecum = sbm("ecum", [128, 256], F32)
            ones32 = sbm("ones32", [32, 128], F32)
            plrow = sbm("plrow", [1, 256], F32)
            cdloc = sbm("cdloc", [1, 256], F32)
            sdpre = sbm("sdpre", [128, 32], F32)
            cf = sbm("cf", [128, 16 * 1024], BF16)

            @block.sync
            def _(sync):
                sync.wait_ge(E2.h, jE)
                Vq.inc(sync.dma_start(out=cw_sb[:], in_=convw), 16)
                Vq.inc(sync.dma_start(out=cb_sb[:], in_=convb), 16)
                Vq.inc(sync.dma_start(out=zb_sb[:], in_=zbF), 16)
                Vq.inc(sync.dma_start(out=dcol[:],
                                      in_=Drow.broadcast_to([128, NH_M])), 16)
            l_mcst = Vq.n

            @block.vector
            def _(vector):
                DVE.wait(vector)
                DVE.inc(vector.memset(ones32[:], 1.0), 1)
            d_ones = DVE.n

            lnstk = ExitStack()

            def sbn(name, shape, dt):
                return lnstk.enter_context(nc.sbuf_tensor(name, shape, dt))

            lnT = sbn("lnT", [128, 16 * 1048], BF16)
            lnb = [sbn("lnb0", [128, D], BF16), sbn("lnb1", [128, D], BF16)]
            hsb = sbn("hsb", [128, 24], F32)
            wl = [sbn("wl0", [128, 2048], BF16), sbn("wl1", [128, 2048], BF16)]
            wlA = sbn("wlA", [128, 512], BF16)
            convin = [sbn("cvin0", [128, 1048], BF16),
                      sbn("cvin1", [128, 1048], BF16)]
            cumrow = [sbn("cumrow0", [1, 128], F32),
                      sbn("cumrow1", [1, 128], F32)]
            zstage = [sbn("zstage0", [128, 1048], BF16)]
            xf = [sbn("xf0", [128, 1024], BF16), sbn("xf1", [128, 1024], BF16)]
            bf2 = [sbn("bf0", [128, 1024], BF16), sbn("bf1", [128, 1024], BF16)]
            stp = [sbn("stp0", [DST, 1024], BF16),
                   sbn("stp1", [DST, 1024], BF16)]
            yp = [sbn("yp0", [128, 1024], BF16), sbn("yp1", [128, 1024], BF16)]
            cvf = sbn("cvf", [128, 128], F32)
            x_tok = [sbn("x_tok0", [128, 64], BF16), sbn("x_tok1", [128, 64], BF16)]
            b_tok = [sbn("b_tok0", [128, 64], BF16), sbn("b_tok1", [128, 64], BF16)]
            bt2 = [sbn("bt20", [128, 64], BF16), sbn("bt21", [128, 64], BF16)]
            dk = [sbn("dk0", [128, 128], F32)]
            expb = [sbn("expb0", [128, 128], BF16), sbn("expb1", [128, 128], BF16)]
            ltb = [sbn("ltb0", [128, 128], BF16), sbn("ltb1", [128, 128], BF16)]

            @block.scalar
            def _(scalar):
                Aq.inc(scalar.dma_start(out=hsb[:],
                                        in_=halo_scale.broadcast_to([128, 24])),
                       16)
                GP.inc(scalar.wait_ge(Aq.h, Aq.n), 1)
            a_hsb = GP.n



            a_ln = ACTs.n
            p_lnt = PE.n
            e_lnt = E2.n

            @block.scalar
            def _(scalar):
                scalar.wait_ge(DVE.h, d_nmr)
                for j in range(9):
                    if j >= 2:
                        scalar.wait_ge(PE.h, p_lnt + 16 * (j - 1))
                    ACTs.inc(scalar.activation(out=lnb[j % 2][:],
                                               in_=src_tile(j),
                                               func=AF.Identity,
                                               bias=sc_sb[:, j:j + 1],
                                               scale=stat_rs[:, j:j + 1]), 1)

            with nc.psum_tensor("m_pta", [128, 128], BF16) as pta, \
                 nc.psum_tensor("m_ptb", [128, 128], BF16) as ptb:
                pst = [pta, ptb]

                @block.tensor
                def _(tensor):
                    for i in range(9 * 16):
                        j, k = divmod(i, 16)
                        if k == 0:
                            tensor.wait_ge(ACTs.h, a_ln + j + 1)
                        if i >= 2:
                            tensor.wait_ge(E2.h, e_lnt + i - 1)
                        PE.inc(tensor.transpose(
                            pst[i % 2][:],
                            lnb[j % 2][:, k * 128:(k + 1) * 128], idb[:]), 1)

                @block.vector
                def _(vector):
                    vector.wait_ge(GP.h, a_hsb)
                    for i in range(9 * 16):
                        j, k = divmod(i, 16)
                        vector.wait_ge(PE.h, p_lnt + i + 1)
                        if j < 8:
                            E2.inc(vector.tensor_copy(
                                lnT[:, k * 1048 + 131 * j + 3:
                                    k * 1048 + 131 * j + 131],
                                pst[i % 2][:]), 1)
                        else:
                            last = None
                            for ss in range(8):
                                last = vector.tensor_mul(
                                    lnT[:, k * 1048 + 131 * ss:
                                        k * 1048 + 131 * ss + 3],
                                    pst[i % 2][:, 3 * ss:3 * ss + 3],
                                    hsb[:, 3 * ss:3 * ss + 3])
                            E2.inc(last, 1)

            lnT_ready = E2.n

            with ExitStack() as pctx:
                pz = [pctx.enter_context(nc.psum_tensor("m_pz0", [128, 512], F32)),
                      pctx.enter_context(nc.psum_tensor("m_pz1", [128, 512], F32)),
                      pctx.enter_context(nc.psum_tensor("m_pz2", [128, 24], F32))]
                ptr = pctx.enter_context(nc.psum_tensor("m_ptr", [128, 128], BF16))
                pd = pctx.enter_context(nc.psum_tensor("m_pd", [128, 128], F32))
                pcb = pctx.enter_context(nc.psum_tensor("m_pcb", [128, 128], F32))
                pstm = pctx.enter_context(nc.psum_tensor("m_pstm", [64, 64], F32))
                pym = pctx.enter_context(nc.psum_tensor("m_pym", [128, 64], F32))

                def win_mms(tensor, lhs_ap, lhs_w):
                    last = None
                    for k in range(16):
                        lt = lhs_ap(k)
                        tensor.matmul(pz[0][0:lhs_w, :], lt,
                                      lnT[:, k * 1048:k * 1048 + 512],
                                      start=(k == 0), stop=(k == 15))
                        tensor.matmul(pz[1][0:lhs_w, :], lt,
                                      lnT[:, k * 1048 + 512:k * 1048 + 1024],
                                      start=(k == 0), stop=(k == 15))
                        last = tensor.matmul(
                            pz[2][0:lhs_w, :], lt,
                            lnT[:, k * 1048 + 1024:k * 1048 + 1048],
                            start=(k == 0), stop=(k == 15))
                    return last

                g_wa = Gq.n

                @block.gpsimd
                def _(gpsimd):
                    Gq.inc(gpsimd.dma_start(out=wlA[:], in_=WinA[:, :]), 16)

                p_al = PE.n

                @block.tensor
                def _(tensor):
                    tensor.wait_ge(E2.h, lnT_ready)
                    tensor.wait_ge(Gq.h, g_wa + 16)
                    PE.inc(win_mms(tensor,
                                   lambda k: wlA[:, k * 32:(k + 1) * 32], 32), 1)

                @block.vector
                def _(vector):
                    vector.wait_ge(PE.h, p_al + 1)
                    vector.tensor_copy(a_sb[:, 0:512], pz[0][0:32, :])
                    vector.tensor_copy(a_sb[:, 512:1024], pz[1][0:32, :])
                    DVE.inc(vector.tensor_copy(a_sb[:, 1024:1048],
                                               pz[2][0:32, :]), 1)
                d_asb = DVE.n

                @block.scalar
                def _(scalar):
                    scalar.wait_ge(DVE.h, d_asb)
                    ACTs.inc(scalar.activation(out=a_sb[:], in_=a_sb[:],
                                               func=AF.Exp), 1)
                    ACTs.wait(scalar)
                    ACTs.inc(scalar.activation(out=a_sb[:], in_=a_sb[:],
                                               func=AF.Ln,
                                               bias=cst[0:32, 1:2],
                                               scale=1.0), 1)
                a_dt = ACTs.n

                @block.vector
                def _(vector):
                    vector.wait_ge(ACTs.h, a_dt)
                    vector.wait_ge(DVE.h, d_ones)
                    last = None
                    for j in range(8):
                        last = vector.tensor_tensor_scan(
                            cum[:, 128 * j:128 * (j + 1)], ones32[:],
                            a_sb[:, 131 * j + 3:131 * j + 131], 0.0,
                            ALU.mult, ALU.add)
                    DVE.inc(last, 1)
                d_cum = DVE.n

                @block.scalar
                def _(scalar):
                    scalar.wait_ge(DVE.h, d_cum)
                    for j in range(8):
                        Aq.inc(scalar.dma_start(
                            out=plrow[0:1, 32 * j:32 * (j + 1)],
                            in_=cum[:, 128 * j + 127:128 * j + 128]), 16)
                    GP.inc(scalar.wait_ge(Aq.h, Aq.n), 1)
                a_pl = GP.n

                p_pre = PE.n
                d_pre = DVE.n
                a_pre = ACTs.n

                @block.tensor
                def _(tensor):
                    tensor.wait_ge(GP.h, a_pl)
                    for j in range(8):
                        if j > 0:
                            tensor.wait_ge(DVE.h, d_pre + 2 * j)
                            tensor.wait_ge(ACTs.h, a_pre + 3 * j)
                        PE.inc(tensor.transpose(pd[0:128, 0:32],
                                                cum[:, 128 * j:128 * (j + 1)],
                                                idf[0:32, 0:32]), 1)
                        PE.inc(tensor.matmul(pcb[:, 0:32], of32g[:],
                                             plrow[0:1, 32 * j:32 * (j + 1)],
                                             start=True, stop=True), 1)

                @block.vector
                def _(vector):
                    for j in range(8):
                        vector.wait_ge(PE.h, p_pre + 2 * j + 1)
                        if j > 0:
                            vector.wait_ge(ACTs.h, a_pre + 3 * (j - 1) + 1)
                        DVE.inc(vector.tensor_copy(cumT[:, 32 * j:32 * (j + 1)],
                                                   pd[:, 0:32]), 1)
                        vector.wait_ge(PE.h, p_pre + 2 * j + 2)
                        vector.wait_ge(DVE.h, d_pre + 2 * j + 1)
                        DVE.inc(vector.tensor_sub(sdpre[:],
                                                  cumT[:, 32 * j:32 * (j + 1)],
                                                  pcb[:, 0:32]), 1)

                @block.scalar
                def _(scalar):
                    for j in range(8):
                        scalar.wait_ge(DVE.h, d_pre + 2 * j + 2)
                        ACTs.inc(scalar.activation(
                            out=sdt[:, 32 * j:32 * (j + 1)],
                            in_=sdpre[:], func=AF.Exp), 1)
                        ACTs.inc(scalar.activation(
                            out=ecum[:, 32 * j:32 * (j + 1)],
                            in_=cumT[:, 32 * j:32 * (j + 1)],
                            func=AF.Exp, scale=-1.0), 1)
                        ACTs.inc(scalar.activation(
                            out=cdloc[0:1, 32 * j:32 * (j + 1)],
                            in_=plrow[0:1, 32 * j:32 * (j + 1)],
                            func=AF.Exp, scale=-1.0), 1)
                a_cd = ACTs.n

                l_cd = Lq.n

                @block.sync
                def _(sync):
                    sync.wait_ge(ACTs.h, a_cd)
                    Lq.inc(sync.dma_start(out=cd_src.flatten()[None, :],
                                          in_=cdloc[:]), 16)

                c_cd = CC.n

                @block.gpsimd
                def _(gpsimd):
                    gpsimd.wait_ge(Lq.h, l_cd + 16)
                    CC.inc(gpsimd.collective_compute(
                        "AllGather", ALU.bypass,
                        replica_groups=[list(range(8))],
                        ins=[cd_src], outs=[cd_dst]), 1)

                aq_z = {}
                pe_zt_end = {}
                gz0 = [GS[0].n, GS[1].n]
                for zt in range(16):
                    @block.gpsimd
                    def _(gpsimd):
                        if zt >= 2:
                            gpsimd.wait_ge(PE.h, pe_zt_end[zt - 2])
                        GS[zt % 2].inc(gpsimd.dma_start(
                            out=wl[zt % 2][:],
                            in_=WinT[128 * (48 + zt):
                                     128 * (48 + zt + 1), :]), 16)

                    p_z = PE.n

                    @block.tensor
                    def _(tensor):
                        if zt == 0:
                            tensor.wait_ge(E2.h, lnT_ready)
                            tensor.wait_ge(DVE.h, DVE.n)
                        tensor.wait_ge(GS[zt % 2].h,
                                       gz0[zt % 2] + 16 * (zt // 2 + 1))
                        if zt >= 1:
                            tensor.wait_ge(ACTs.h, ACTs.n)
                        PE.inc(win_mms(
                            tensor,
                            lambda k: wl[zt % 2][:, k * 128:(k + 1) * 128],
                            128), 1)

                    pe_zt_end[zt] = PE.n

                    @block.vector
                    def _(vector):
                        vector.wait_ge(PE.h, p_z + 1)
                        vector.wait_ge(Vq.h, l_mcst)
                        vector.tensor_scalar(scr[:, 0:512], pz[0][:],
                                             zb_sb[:, zt:zt + 1], None, ALU.add)
                        vector.tensor_scalar(scr[:, 512:1024], pz[1][:],
                                             zb_sb[:, zt:zt + 1], None, ALU.add)
                        DVE.inc(vector.tensor_scalar(
                            scr[:, 1024:1048], pz[2][:],
                            zb_sb[:, zt:zt + 1], None, ALU.add), 1)

                    @block.scalar
                    def _(scalar):
                        scalar.wait_ge(DVE.h, DVE.n)
                        if zt >= 1:
                            scalar.wait_ge(Aq.h, aq_z[zt - 1])
                        ACTs.inc(scalar.activation(
                            out=zstage[0][:],
                            in_=scr[:, 0:1048], func=AF.Silu), 1)
                        ACTs.wait(scalar)
                        Aq.inc(scalar.dma_start(out=svz_d[zt],
                                                in_=zstage[0][:]), 16)
                    aq_z[zt] = Aq.n

                @block.vector
                def _(vector):
                    DVE.wait(vector)
                    vector.wait_ge(ACTs.h, ACTs.n)
                    DVE.inc(vector.memset(scr[:, 0:128], -100.0), 1)

                pe_pair_end = {}
                aq_yp = {}
                pe_decay_hist = {}
                gt0 = [GS[0].n, GS[1].n]
                gt_cnt = {0: 0, 1: 0}
                wl_free = {(0, 0): pe_zt_end[14], (1, 0): pe_zt_end[15]}
                for pair in range(16):
                    for t3 in range(3):
                        ot = pair + 16 * t3
                        gt_cnt[ot % 2] += 1
                        gt_need = gt0[ot % 2] + 16 * gt_cnt[ot % 2]

                        @block.gpsimd
                        def _(gpsimd):
                            gpsimd.wait_ge(PE.h, wl_free.get((pair, t3), 0))
                            GS[ot % 2].inc(gpsimd.dma_start(
                                out=wl[ot % 2][:],
                                in_=WinT[128 * ot:128 * (ot + 1), :]), 16)

                        p_t = PE.n

                        @block.tensor
                        def _(tensor):
                            tensor.wait_ge(GS[ot % 2].h, gt_need)
                            tensor.wait_ge(DVE.h, DVE.n)
                            PE.inc(win_mms(
                                tensor,
                                lambda k: wl[ot % 2][:, k * 128:(k + 1) * 128],
                                128), 1)
                        if t3 < 2:
                            wl_free[(pair, t3 + 1)] = PE.n
                        else:
                            wl_free[(pair + 2, 0)] = PE.n

                        @block.vector
                        def _(vector):
                            vector.wait_ge(PE.h, p_t + 1)
                            if pair >= 2 and t3 < 2:
                                vector.wait_ge(PE.h, pe_pair_end[pair - 2])
                            vector.tensor_copy(convin[t3 % 2][:, 0:512], pz[0][:])
                            vector.tensor_copy(convin[t3 % 2][:, 512:1024],
                                               pz[1][:])
                            DVE.inc(vector.tensor_copy(
                                convin[t3 % 2][:, 1024:1048], pz[2][:]), 1)
                        d_cv = DVE.n

                        dstb = xf[pair % 2] if t3 == 0 else bf2[pair % 2]
                        a_c0 = ACTs.n

                        @block.scalar
                        def _(scalar):
                            scalar.wait_ge(DVE.h, d_cv)
                            scalar.wait_ge(Vq.h, l_mcst)
                            for j in range(8):
                                if j > 0:
                                    scalar.wait_ge(DVE.h, d_cv + 3 * j)
                                ACTs.inc(scalar.activation(
                                    out=cvf[:],
                                    in_=convin[t3 % 2][:, 131 * j:131 * j + 128],
                                    func=AF.Identity,
                                    bias=cb_sb[:, ot:ot + 1],
                                    scale=cw_sb[:, 4 * ot:4 * ot + 1]), 1)

                        @block.vector
                        def _(vector):
                            for j in range(8):
                                vector.wait_ge(ACTs.h, a_c0 + j + 1)
                                for kk in (1, 2):
                                    DVE.wait(vector)
                                    DVE.inc(vector.scalar_tensor_tensor(
                                        cvf[:],
                                        convin[t3 % 2][:, 131 * j + kk:
                                                       131 * j + kk + 128],
                                        cw_sb[:, 4 * ot + kk:4 * ot + kk + 1],
                                        cvf[:], ALU.mult, ALU.add), 1)
                                DVE.wait(vector)
                                dst = (cf[:, pair * 1024 + 128 * j:
                                          pair * 1024 + 128 * (j + 1)]
                                       if t3 == 2 else
                                       dstb[:, 128 * j:128 * (j + 1)])
                                DVE.inc(vector.scalar_tensor_tensor(
                                    dst,
                                    convin[t3 % 2][:, 131 * j + 3:
                                                   131 * j + 131],
                                    cw_sb[:, 4 * ot + 3:4 * ot + 4],
                                    cvf[:], ALU.mult, ALU.add), 1)

                    for it in range(16):
                        hl, j = divmod(it, 8)
                        h = 2 * pair + hl
                        rot = it % 2
                        bp = 64 * hl

                        p_i = PE.n
                        d_i = DVE.n
                        a_i = ACTs.n

                        @block.scalar
                        def _(scalar):
                            if (pair, it) in pe_decay_hist:
                                scalar.wait_ge(PE.h, pe_decay_hist[(pair, it)])
                            WS[it % 2].inc(scalar.dma_start(
                                out=cumrow[it % 2][:],
                                in_=cum[h:h + 1, 128 * j:128 * (j + 1)]), 16)
                        aq_cr = (WS[it % 2], WS[it % 2].n)

                        @block.tensor
                        def _(tensor):
                            tensor.wait_ge(DVE.h, d_i)
                            if it == 0:
                                if pair >= 2:
                                    tensor.wait_ge(GP.h, aq_yp[pair - 2])
                            PE.inc(tensor.transpose(
                                ptr[:, 0:64],
                                xf[pair % 2][bp:bp + 64, 128 * j:128 * (j + 1)],
                                idb64[bp:bp + 64, :]), 1)
                            PE.inc(tensor.transpose(
                                ptr[:, 64:128],
                                bf2[pair % 2][bp:bp + 64, 128 * j:128 * (j + 1)],
                                idb64[bp:bp + 64, :]), 1)

                        @block.vector
                        def _(vector):
                            vector.wait_ge(PE.h, p_i + 2)
                            vector.tensor_copy(x_tok[rot][:], ptr[:, 0:64])
                            DVE.inc(vector.tensor_copy(b_tok[rot][:],
                                                       ptr[:, 64:128]), 1)
                            DVE.wait(vector)
                            DVE.inc(vector.tensor_scalar(
                                bt2[rot][:], b_tok[rot][:],
                                sdt[:, 32 * j + h:32 * j + h + 1],
                                None, ALU.mult), 1)

                        @block.tensor
                        def _(tensor):
                            tensor.wait_ge(DVE.h, d_i + 2)
                            tensor.wait_ge(ACTs.h, a_i)
                            PE.inc(tensor.matmul(pstm[:], bt2[rot][:],
                                                 x_tok[rot][:],
                                                 start=True, stop=True), 1)
                            tensor.wait_ge(aq_cr[0].h, aq_cr[1])
                            PE.inc(tensor.matmul(
                                pd[:], of32g[:], cumrow[it % 2][:],
                                start=True, stop=True), 1)
                        pe_decay_hist[(pair, it + 2)] = PE.n if it + 2 < 16 \
                            else PE.n
                        if it + 2 >= 16 and pair + 1 < 16:
                            pe_decay_hist[(pair + 1, it + 2 - 16)] = PE.n

                        @block.vector
                        def _(vector):
                            vector.wait_ge(PE.h, p_i + 3)
                            DVE.inc(vector.tensor_copy(
                                stp[pair % 2][:, j * 128 + bp:
                                              j * 128 + bp + 64],
                                pstm[:]), 1)

                        @block.vector
                        def _(vector):
                            vector.wait_ge(PE.h, p_i + 4)
                            vector.wait_ge(ACTs.h, a_i)
                            DVE.inc(vector.tensor_scalar(
                                dk[0][:], pd[:], -1.0,
                                cumT[:, 32 * j + h:32 * j + h + 1],
                                ALU.mult, ALU.add), 1)
                            DVE.wait(vector)
                            DVE.inc(vector.copy_predicated(
                                dk[0][:], trili[:],
                                scr[:, 0:128]), 1)

                        @block.scalar
                        def _(scalar):
                            scalar.wait_ge(DVE.h, d_i + 5)
                            ACTs.inc(scalar.activation(
                                out=expb[rot][:], in_=dk[0][:],
                                func=AF.Exp), 1)

                        @block.tensor
                        def _(tensor):
                            tensor.wait_ge(DVE.h, d_i + 4)
                            PE.inc(tensor.matmul(
                                pcb[:],
                                bf2[pair % 2][bp:bp + 64, 128 * j:128 * (j + 1)],
                                cf[bp:bp + 64, pair * 1024 + 128 * j:
                                   pair * 1024 + 128 * (j + 1)],
                                start=True, stop=True), 1)

                        @block.vector
                        def _(vector):
                            vector.wait_ge(PE.h, p_i + 5)
                            vector.wait_ge(ACTs.h, a_i + 1)
                            DVE.inc(vector.tensor_mul(ltb[rot][:], pcb[:],
                                                      expb[rot][:]), 1)

                        @block.tensor
                        def _(tensor):
                            tensor.wait_ge(DVE.h, d_i + 6)
                            PE.inc(tensor.matmul(pym[:], ltb[rot][:],
                                                 x_tok[rot][:],
                                                 start=True, stop=True), 1)

                        @block.vector
                        def _(vector):
                            vector.wait_ge(PE.h, p_i + 6)
                            vector.wait_ge(Vq.h, l_mcst)
                            DVE.inc(vector.scalar_tensor_tensor(
                                yp[pair % 2][:, j * 128 + bp:j * 128 + bp + 64],
                                x_tok[rot][:], dcol[:, h:h + 1], pym[:],
                                ALU.mult, ALU.add), 1)

                    pe_pair_end[pair] = PE.n

                    @block.scalar
                    def _(scalar):
                        scalar.wait_ge(DVE.h, DVE.n)
                        Aq.inc(scalar.dma_start(out=yp_d[pair],
                                                in_=yp[pair % 2][:]), 16)
                        Aq.inc(scalar.dma_start(
                            out=st_src.rearrange("n (j c) -> n j c", j=8)
                            [:, :, 128 * pair:128 * (pair + 1)],
                            in_=stp[pair % 2].rearrange("n (j c) -> n j c",
                                                        j=8)), 16)
                        GP.inc(scalar.wait_ge(Aq.h, Aq.n), 1)
                    aq_yp[pair] = GP.n

            lnstk.close()

            c_st = CC.n

            @block.gpsimd
            def _(gpsimd):
                gpsimd.wait_ge(GP.h, GP.n)
                CC.inc(gpsimd.collective_compute(
                    "AllGather", ALU.bypass, replica_groups=[list(range(8))],
                    ins=[st_src], outs=[st_dst]), 1)

            snaps2 = sbm("snaps2", [128, NSLOT * 1024], BF16)
            fstk = ExitStack()

            def sbf(name, shape, dt):
                return fstk.enter_context(nc.sbuf_tensor(name, shape, dt))

            P_run = sbf("P_run", [64, 2048], F32)
            snaps = sbf("snaps", [64, NSLOT * 2048], BF16)
            cdb_all = sbf("cdb_all", [64, 64 * 32], F32)
            selS = sbf("selS", [64, 64], F32)
            stf = [sbf("stf0", [64, 2048], BF16), sbf("stf1", [64, 2048], BF16)]

            @block.sync
            def _(sync):
                sync.wait_ge(PE.h, PE.n)
                sync.wait_ge(Aq.h, Aq.n)
                Lq.inc(sync.dma_start(out=selS[:], in_=selb), 16)
                sync.wait_ge(CC.h, c_cd + 1)
                for g in range(64):
                    sg, rg = g // 8, (g % 8 - g // 8) % 8
                    Lq.inc(sync.dma_start(
                        out=cdb_all[:, 32 * g:32 * (g + 1)],
                        in_=cd_dst[(8 * rg + sg):(8 * rg + sg + 1), :]
                        .broadcast_to([64, 32])), 16)
            l_cdb = Lq.n

            @block.vector
            def _(vector):
                DVE.wait(vector)
                vector.wait_ge(PE.h, PE.n)
                vector.wait_ge(Aq.h, Aq.n)
                vector.memset(P_run[:], 0.0)
                DVE.inc(vector.memset(snaps[:], 0.0), 1)
            d_f0 = DVE.n

            g_stf0 = GP.n

            @block.scalar
            def _(scalar):
                scalar.wait_ge(CC.h, c_st + 1)
                for g in range(2):
                    sg, rg = g // 8, (g % 8 - g // 8) % 8
                    Aq.inc(scalar.dma_start(
                        out=stf[g % 2][:],
                        in_=st_dst[64 * rg:64 * rg + 64,
                                   2048 * sg:2048 * (sg + 1)]), 16)
                    GP.inc(scalar.wait_ge(Aq.h, Aq.n), 1)

            @block.vector
            def _(vector):
                vector.wait_ge(Lq.h, l_cdb)
                vector.wait_ge(DVE.h, d_f0)
                for g in range(64):
                    vector.wait_ge(GP.h, g_stf0 + g + 1)
                    last = None
                    for h in range(32):
                        last = vector.scalar_tensor_tensor(
                            snaps[:, (g // 8) * 2048 + 64 * h:
                                  (g // 8) * 2048 + 64 * (h + 1)],
                            P_run[:, 64 * h:64 * (h + 1)],
                            selS[:, g:g + 1],
                            snaps[:, (g // 8) * 2048 + 64 * h:
                                  (g // 8) * 2048 + 64 * (h + 1)],
                            ALU.mult, ALU.add)
                    DVE.inc(last, 1)
                    DVE.wait(vector)
                    last = None
                    for h in range(32):
                        last = vector.scalar_tensor_tensor(
                            P_run[:, 64 * h:64 * (h + 1)],
                            P_run[:, 64 * h:64 * (h + 1)],
                            cdb_all[:, 32 * g + h:32 * g + h + 1],
                            stf[g % 2][:, 64 * h:64 * (h + 1)],
                            ALU.mult, ALU.add)
                    DVE.inc(last, 1)
                    DVE.wait(vector)

            d_fold = DVE.n

            @block.scalar
            def _(scalar):
                for g in range(2, 64):
                    scalar.wait_ge(DVE.h, d_f0 + 2 * (g - 1))
                    sg, rg = g // 8, (g % 8 - g // 8) % 8
                    Aq.inc(scalar.dma_start(
                        out=stf[g % 2][:],
                        in_=st_dst[64 * rg:64 * rg + 64,
                                   2048 * sg:2048 * (sg + 1)]), 16)
                    GP.inc(scalar.wait_ge(Aq.h, Aq.n), 1)

            l_sn2 = Lq.n

            @block.sync
            def _(sync):
                sync.wait_ge(DVE.h, d_fold)
                for par in range(2):
                    Lq.inc(sync.dma_start(
                        out=snaps2[64 * par:64 * (par + 1), :]
                        .rearrange("n (s m c) -> n s m c", s=8, m=16),
                        in_=snaps.rearrange("n (s m c2) -> n s m c2",
                                            s=8, m=16)
                        [:, :, :, 64 * par:64 * (par + 1)]), 16)
                STs.inc(sync.wait_ge(Lq.h, Lq.n), 1)
            l_sn2e = STs.n

            fstk.close()

            yps = [sbm("yps0", [128, 2048], BF16), sbm("yps1", [128, 2048], BF16)]
            svs = [sbm("svs0", [128, 2048], BF16), sbm("svs1", [128, 2048], BF16)]
            worot = [sbm("worot0", [64, 2048], BF16),
                     sbm("worot1", [64, 2048], BF16)]
            tq = [sbm("tq0", [128, 64], BF16), sbm("tq1", [128, 64], BF16)]
            ytmp = [sbm("ytmp0", [64, 128], BF16), sbm("ytmp1", [64, 128], BF16)]

            with ExitStack() as pctx2:
                pyi = pctx2.enter_context(nc.psum_tensor("m_pyi", [128, 64], F32))
                ptb2 = pctx2.enter_context(nc.psum_tensor("m_ptb2", [64, 128], BF16))
                pout = [pctx2.enter_context(
                    nc.psum_tensor(f"m_po{d}", [128, 512], F32))
                    for d in range(4)]

                yps_mark = {}

                @block.scalar
                def _(scalar):
                    scalar.wait_ge(DVE.h, d_fold)
                    scalar.wait_ge(STs.h, l_sn2e)
                    for s in range(2):
                        for h in range(32):
                            Aq.inc(scalar.dma_start(
                                out=yps[s % 2][:, 64 * h:64 * (h + 1)],
                                in_=yp_d[h // 2][:, 128 * s + 64 * (h % 2):
                                                 128 * s + 64 * (h % 2) + 64]),
                                16)
                    GP.inc(scalar.wait_ge(Aq.h, Aq.n), 1)
                yps_mark[0] = GP.n
                yps_mark[1] = GP.n

                svs_mark = {}

                @block.sync
                def _(sync):
                    sync.wait_ge(DVE.h, d_fold)
                    sync.wait_ge(STs.h, l_sn2e)
                    for s in range(2):
                        for h in range(32):
                            Lq.inc(sync.dma_start(
                                out=svs[s % 2][64 * (h % 2):64 * (h % 2) + 64,
                                               (h // 2) * 128:(h // 2 + 1) * 128],
                                in_=svz_d[h // 2][64 * (h % 2):64 * (h % 2) + 64,
                                                  131 * s + 3:131 * s + 131]),
                                16)
                    STs.inc(sync.wait_ge(Lq.h, Lq.n), 1)
                svs_mark[0] = STs.n
                svs_mark[1] = STs.n

                gw0 = [GS[0].n, GS[1].n]
                gw_cnt = {0: 0, 1: 0}
                gw_need = {}

                @block.gpsimd
                def _(gpsimd):
                    gpsimd.wait_ge(DVE.h, d_fold)
                    gpsimd.wait_ge(STs.h, l_sn2e)
                    for h in range(2):
                        GS[h % 2].inc(gpsimd.dma_start(
                            out=worot[h % 2][:],
                            in_=WoutT[64 * h:64 * (h + 1), :]), 16)
                for h in range(2):
                    gw_cnt[h % 2] += 1
                    gw_need[h] = gw0[h % 2] + 16 * gw_cnt[h % 2]

                for s in range(NSLOT):
                    for h in range(32):
                        it = s * 32 + h
                        rot = h % 2
                        bp = 64 * (h % 2)

                        p_i = PE.n
                        d_i = DVE.n

                        @block.tensor
                        def _(tensor):
                            if it == 0:
                                tensor.wait_ge(DVE.h, d_fold)
                            tensor.wait_ge(DVE.h, d_i)
                            if it == 0:
                                tensor.wait_ge(STs.h, l_sn2e)
                            PE.inc(tensor.matmul(
                                pyi[:],
                                cf[bp:bp + 64, (h // 2) * 1024 + 128 * s:
                                   (h // 2) * 1024 + 128 * (s + 1)],
                                snaps2[bp:bp + 64, s * 1024 + (h // 2) * 64:
                                       s * 1024 + (h // 2) * 64 + 64],
                                start=True, stop=True), 1)

                        @block.vector
                        def _(vector):
                            vector.wait_ge(PE.h, p_i + 1)
                            vector.wait_ge(GP.h, yps_mark[s])
                            DVE.inc(vector.scalar_tensor_tensor(
                                tq[rot][:], pyi[:],
                                ecum[:, 32 * s + h:32 * s + h + 1],
                                yps[s % 2][:, 64 * h:64 * (h + 1)],
                                ALU.mult, ALU.add), 1)

                        @block.tensor
                        def _(tensor):
                            tensor.wait_ge(DVE.h, d_i + 1)
                            PE.inc(tensor.transpose(ptb2[:], tq[rot][:],
                                                    idb[:]), 1)

                        @block.vector
                        def _(vector):
                            vector.wait_ge(PE.h, p_i + 2)
                            vector.wait_ge(STs.h, svs_mark[s])
                            DVE.inc(vector.tensor_mul(
                                ytmp[rot][:], ptb2[:],
                                svs[s % 2][bp:bp + 64,
                                           (h // 2) * 128:(h // 2 + 1) * 128]), 1)

                        @block.tensor
                        def _(tensor):
                            tensor.wait_ge(DVE.h, d_i + 2)
                            tensor.wait_ge(GS[rot].h, gw_need[it])
                            last = None
                            for dd in range(4):
                                last = tensor.matmul(
                                    pout[dd][:], ytmp[rot][:],
                                    worot[rot][:, 512 * dd:512 * (dd + 1)],
                                    start=(h == 0), stop=(h == 31))
                            PE.inc(last, 1)

                        @block.gpsimd
                        def _(gpsimd):
                            if it + 2 < 256:
                                gpsimd.wait_ge(PE.h, p_i + 3)
                                hh = (it + 2) % 32
                                GS[rot].inc(gpsimd.dma_start(
                                    out=worot[rot][:],
                                    in_=WoutT[64 * hh:64 * (hh + 1), :]), 16)
                        if it + 2 < 256:
                            gw_cnt[rot] += 1
                            gw_need[it + 2] = gw0[rot] + 16 * gw_cnt[rot]

                    p_s_end = PE.n

                    @block.vector
                    def _(vector):
                        vector.wait_ge(PE.h, p_s_end)
                        last = None
                        for dd in range(4):
                            last = vector.tensor_add(
                                xr[:, s * 2048 + 512 * dd:
                                   s * 2048 + 512 * (dd + 1)],
                                pout[dd][:],
                                xr[:, s * 2048 + 512 * dd:
                                   s * 2048 + 512 * (dd + 1)])
                        E2.inc(last, 1)

                    @block.scalar
                    def _(scalar):
                        if s + 2 < NSLOT:
                            scalar.wait_ge(DVE.h, DVE.n)
                            for h in range(32):
                                Aq.inc(scalar.dma_start(
                                    out=yps[s % 2][:, 64 * h:64 * (h + 1)],
                                    in_=yp_d[h // 2][:,
                                                     128 * (s + 2) + 64 * (h % 2):
                                                     128 * (s + 2) + 64 * (h % 2) + 64]),
                                    16)
                            GP.inc(scalar.wait_ge(Aq.h, Aq.n), 1)
                    if s + 2 < NSLOT:
                        yps_mark[s + 2] = GP.n

                    @block.sync
                    def _(sync):
                        if s + 2 < NSLOT:
                            sync.wait_ge(DVE.h, DVE.n)
                            for h in range(32):
                                Lq.inc(sync.dma_start(
                                    out=svs[s % 2][64 * (h % 2):64 * (h % 2) + 64,
                                                   (h // 2) * 128:(h // 2 + 1) * 128],
                                    in_=svz_d[h // 2][64 * (h % 2):
                                                      64 * (h % 2) + 64,
                                                      131 * (s + 2) + 3:
                                                      131 * (s + 2) + 131]),
                                    16)
                            STs.inc(sync.wait_ge(Lq.h, Lq.n), 1)
                    if s + 2 < NSLOT:
                        svs_mark[s + 2] = STs.n

            mstk.close()

        def emit_ffn1(idx):
            def src_tile(j):
                return xr[:, j * D:(j + 1) * D]

            jE, jD, jA, jP = E2.n, DVE.n, ACTs.n, PE.n

            @block.sync
            def _(sync):
                sync.wait_ge(E2.h, jE)

            @block.scalar
            def _(scalar):
                scalar.wait_ge(E2.h, jE)
                scalar.wait_ge(PE.h, jP)

            @block.vector
            def _(vector):
                vector.wait_ge(E2.h, jE)
                vector.wait_ge(ACTs.h, jA)

            @block.tensor
            def _(tensor):
                tensor.wait_ge(E2.h, jE)
                tensor.wait_ge(DVE.h, jD)

            @block.gpsimd
            def _(gpsimd):
                gpsimd.wait_ge(E2.h, jE)
                gpsimd.wait_ge(DVE.h, jD)

            emit_ln_stats(src_tile, 8, absmax=False)

            fstk1 = ExitStack()

            def sbq(name, shape, dt):
                return fstk1.enter_context(nc.sbuf_tensor(name, shape, dt))

            xT = sbq(f"q{idx}xT", [128, 16 * 1024], BF16)
            dacc = sbq(f"q{idx}dacc", [128, 16 * 1024], F32)
            gsT = sbq(f"q{idx}gsT", [128, 64], F32)
            usT = sbq(f"q{idx}usT", [128, 64], F32)
            dsT = sbq(f"q{idx}dsT", [128, 16], F32)
            fu = scr[:, 512:1024]
            fsb = [scr[:, 1024:1536], scr[:, 1536:2048]]
            lstk1 = ExitStack()
            lnb1 = [lstk1.enter_context(nc.sbuf_tensor(f"q{idx}lnb0",
                                                       [128, D], BF16)),
                    lstk1.enter_context(nc.sbuf_tensor(f"q{idx}lnb1",
                                                       [128, D], BF16))]

            @block.vector
            def _(vector):
                DVE.wait(vector)
                DVE.inc(vector.tensor_mul(sc_sb[:, 0:8], stat_nm[:, 0:8],
                                          stat_rs[:, 0:8]), 1)
            d_nmr1 = DVE.n

            l_sc = Lq.n

            @block.sync
            def _(sync):
                sync.wait_ge(E2.h, jE)
                Lq.inc(sync.dma_start(out=scr[0:64, 0:128], in_=gs_in), 16)
                Lq.inc(sync.dma_start(out=scr[0:64, 128:256], in_=us_in), 16)
                Lq.inc(sync.dma_start(out=scr[0:16, 256:384], in_=ds_in), 16)
                STs.inc(sync.wait_ge(Lq.h, Lq.n), 1)
            m_sc = STs.n

            p_sc = PE.n
            with nc.psum_tensor(f"q{idx}psc", [128, 128], F32) as psc:
                @block.tensor
                def _(tensor):
                    tensor.wait_ge(STs.h, m_sc)
                    PE.inc(tensor.transpose(psc[:, 0:64], scr[0:64, 0:128],
                                            idf[0:64, 0:64]), 1)

                @block.vector
                def _(vector):
                    vector.wait_ge(PE.h, p_sc + 1)
                    DVE.inc(vector.tensor_copy(gsT[:], psc[:, 0:64]), 1)

                @block.tensor
                def _(tensor):
                    tensor.wait_ge(DVE.h, DVE.n)
                    PE.inc(tensor.transpose(psc[:, 0:64], scr[0:64, 128:256],
                                            idf[0:64, 0:64]), 1)

                @block.vector
                def _(vector):
                    vector.wait_ge(PE.h, p_sc + 2)
                    DVE.inc(vector.tensor_copy(usT[:], psc[:, 0:64]), 1)

                @block.tensor
                def _(tensor):
                    tensor.wait_ge(DVE.h, DVE.n)
                    PE.inc(tensor.transpose(psc[:, 0:16], scr[0:16, 256:384],
                                            idf[0:16, 0:16]), 1)

                @block.vector
                def _(vector):
                    vector.wait_ge(PE.h, p_sc + 3)
                    DVE.inc(vector.tensor_copy(dsT[:], psc[:, 0:16]), 1)
                    DVE.wait(vector)
                    DVE.inc(vector.tensor_scalar_mul(dsT[:], dsT[:], 0.5), 1)

            a_l1 = ACTs.n
            p_l1 = PE.n
            e_l1 = E2.n

            @block.scalar
            def _(scalar):
                scalar.wait_ge(DVE.h, d_nmr1)
                for j in range(8):
                    if j >= 2:
                        scalar.wait_ge(PE.h, p_l1 + 16 * (j - 1))
                    ACTs.inc(scalar.activation(out=lnb1[j % 2][:],
                                               in_=src_tile(j),
                                               func=AF.Identity,
                                               bias=sc_sb[:, j:j + 1],
                                               scale=stat_rs[:, j:j + 1]), 1)

            with nc.psum_tensor(f"q{idx}pt0", [128, 128], BF16) as pq0, \
                 nc.psum_tensor(f"q{idx}pt1", [128, 128], BF16) as pq1:
                pstq = [pq0, pq1]

                @block.tensor
                def _(tensor):
                    for i in range(8 * 16):
                        j, k = divmod(i, 16)
                        if k == 0:
                            tensor.wait_ge(ACTs.h, a_l1 + j + 1)
                        if i >= 2:
                            tensor.wait_ge(E2.h, e_l1 + i - 1)
                        PE.inc(tensor.transpose(
                            pstq[i % 2][:],
                            lnb1[j % 2][:, k * 128:(k + 1) * 128], idb[:]), 1)

                @block.vector
                def _(vector):
                    for j in range(8):
                        for k in range(16):
                            i = 16 * j + k
                            vector.wait_ge(PE.h, p_l1 + i + 1)
                            E2.inc(vector.tensor_copy(
                                xT[:, k * 1024 + j * 128:
                                   k * 1024 + (j + 1) * 128],
                                pstq[i % 2][:]), 1)
            xT_ready = E2.n
            lstk1.close()
            wg_sb = [sbq(f"q{idx}wg0", [128, 2048], BF16),
                     sbq(f"q{idx}wg1", [128, 2048], BF16)]
            wu_sb = [sbq(f"q{idx}wu0", [128, 2048], BF16),
                     sbq(f"q{idx}wu1", [128, 2048], BF16)]
            dw_sb = [sbq(f"q{idx}dw{i}", [128, 2048], BF16) for i in range(2)]
            h_sb = [sbq(f"q{idx}h{i}", [128, 1024], BF16) for i in range(2)]

            with ExitStack() as qctx:
                pdn = [qctx.enter_context(nc.psum_tensor(f"q{idx}pdn{i}",
                                                         [128, 512], F32))
                       for i in range(4)]
                qctx1 = ExitStack()
                pg = [qctx1.enter_context(nc.psum_tensor(f"q{idx}pg{i}",
                                                         [128, 512], F32))
                      for i in range(2)]
                pu = [qctx1.enter_context(nc.psum_tensor(f"q{idx}pu{i}",
                                                         [128, 512], F32))
                      for i in range(2)]

                gq0 = [GS[0].n, GS[1].n]
                gq_cnt = {0: 0, 1: 0}
                dw_need = {}
                fold_mark = {}
                wb_mark = {}
                fs_free = {}
                pe_i_end = {}
                pe_gu_end = {}
                for isup in range(32):
                    for i2 in range(2):
                        i = 2 * isup + i2
                        par = i % 2
                        gq_cnt[par] += 1
                        nper = 3 if idx == 0 else 1
                        dw_need[i] = gq0[par] + 16 * nper * gq_cnt[par]

                        if idx == 0:
                            @block.gpsimd
                            def _(gpsimd):
                                if i >= 2:
                                    gpsimd.wait_ge(PE.h, pe_gu_end[i - 2])
                                    gpsimd.wait_ge(STs.h, wb_mark[i - 2])
                                else:
                                    gpsimd.wait_ge(E2.h, xT_ready)
                                GS[par].inc(gpsimd.dma_start(
                                    out=wg_sb[par][:],
                                    in_=WgT[128 * i:128 * (i + 1), :]), 16)
                                GS[par].inc(gpsimd.dma_start(
                                    out=wu_sb[par][:],
                                    in_=WuT[128 * i:128 * (i + 1), :]), 16)
                                if i >= 2:
                                    gpsimd.wait_ge(PE.h, pe_i_end[i - 2])
                                GS[par].inc(gpsimd.dma_start(
                                    out=dw_sb[i2][:],
                                    in_=dwT[128 * i:128 * (i + 1), :]), 16)

                            @block.vector
                            def _(vector):
                                vector.wait_ge(GS[par].h, dw_need[i])
                                vector.tensor_scalar(
                                    wg_sb[par][:], wg_sb[par][:],
                                    -128.0, None, ALU.add)
                                vector.tensor_scalar(
                                    wu_sb[par][:], wu_sb[par][:],
                                    -128.0, None, ALU.add)
                                DVE.inc(vector.tensor_scalar(
                                    dw_sb[i2][:], dw_sb[i2][:],
                                    -128.0, None, ALU.add), 1)
                            fold_mark[i] = DVE.n

                            @block.sync
                            def _(sync):
                                sync.wait_ge(DVE.h, fold_mark[i])
                                Lq.inc(sync.dma_start(
                                    out=Wg_c[128 * i:128 * (i + 1), :],
                                    in_=wg_sb[par][:]), 16)
                                Lq.inc(sync.dma_start(
                                    out=Wu_c[128 * i:128 * (i + 1), :],
                                    in_=wu_sb[par][:]), 16)
                                Lq.inc(sync.dma_start(
                                    out=dw_c[128 * i:128 * (i + 1), :],
                                    in_=dw_sb[i2][:]), 16)
                                STs.inc(sync.wait_ge(Lq.h, Lq.n), 1)
                            wb_mark[i] = STs.n
                        else:
                            @block.sync
                            def _(sync):
                                if i >= 2:
                                    sync.wait_ge(PE.h, pe_gu_end[i - 2])
                                else:
                                    sync.wait_ge(E2.h, xT_ready)
                                Lq.inc(sync.dma_start(
                                    out=wg_sb[par][:],
                                    in_=Wg_c[128 * i:128 * (i + 1), :]), 16)
                                Lq.inc(sync.dma_start(
                                    out=wu_sb[par][:],
                                    in_=Wu_c[128 * i:128 * (i + 1), :]), 16)
                                STs.inc(sync.wait_ge(Lq.h, Lq.n), 1)
                            wb_mark[i] = STs.n

                            @block.gpsimd
                            def _(gpsimd):
                                if i >= 2:
                                    gpsimd.wait_ge(PE.h, pe_i_end[i - 2])
                                GS[par].inc(gpsimd.dma_start(
                                    out=dw_sb[i2][:],
                                    in_=dw_c[128 * i:128 * (i + 1), :]), 16)

                        p_gu = PE.n

                        @block.tensor
                        def _(tensor):
                            if i == 0:
                                tensor.wait_ge(E2.h, xT_ready)
                            if idx == 0:
                                tensor.wait_ge(DVE.h, fold_mark[i])
                            else:
                                tensor.wait_ge(STs.h, wb_mark[i])
                            tensor.wait_ge(DVE.h, DVE.n)
                            tensor.wait_ge(ACTs.h, ACTs.n)
                            last = None
                            for k in range(16):
                                for qh in range(2):
                                    tensor.matmul(
                                        pg[qh][:],
                                        wg_sb[par][:, 128 * k:128 * (k + 1)],
                                        xT[:, 1024 * k + 512 * qh:
                                           1024 * k + 512 * (qh + 1)],
                                        start=(k == 0), stop=(k == 15))
                                    last = tensor.matmul(
                                        pu[qh][:],
                                        wu_sb[par][:, 128 * k:128 * (k + 1)],
                                        xT[:, 1024 * k + 512 * qh:
                                           1024 * k + 512 * (qh + 1)],
                                        start=(k == 0), stop=(k == 15))
                            PE.inc(last, 1)
                        pe_gu_end[i] = PE.n

                        d_h = DVE.n
                        a_h = ACTs.n

                        @block.scalar
                        def _(scalar):
                            scalar.wait_ge(DVE.h, fs_free.get(i - 1, 0))
                            scalar.wait_ge(PE.h, p_gu + 1)
                            for qh in range(2):
                                ACTs.inc(scalar.activation(
                                    out=fsb[qh], in_=pg[qh][:], func=AF.Silu,
                                    scale=gsT[:, i:i + 1]), 1)

                        @block.vector
                        def _(vector):
                            vector.wait_ge(PE.h, p_gu + 1)
                            if i >= 2:
                                vector.wait_ge(PE.h, pe_i_end[i - 2])
                            for qh in range(2):
                                DVE.inc(vector.tensor_scalar(
                                    fu[:], pu[qh][:], usT[:, i:i + 1],
                                    None, ALU.mult), 1)
                                vector.wait_ge(ACTs.h, a_h + qh + 1)
                                DVE.wait(vector)
                                DVE.inc(vector.tensor_mul(
                                    h_sb[i2][:, 512 * qh:512 * (qh + 1)],
                                    fsb[qh], fu[:]), 1)
                        fs_free[i] = DVE.n

                        pe_i_end[i] = None

                    d_dn = DVE.n

                    dacc_mark = getattr(emit_ffn1, "_dm", None)
                    if dacc_mark is None or isup == 0:
                        dacc_mark = {}
                        emit_ffn1._dm = dacc_mark
                    for dd in range(16):
                        for qh in range(2):
                            p_dd = PE.n
                            bank = (dd * 2 + qh) % 4

                            @block.tensor
                            def _(tensor):
                                if (dd, qh) == (0, 0):
                                    tensor.wait_ge(DVE.h, d_dn)
                                    tensor.wait_ge(GS[0].h, dw_need[2 * isup])
                                    tensor.wait_ge(GS[1].h,
                                                   dw_need[2 * isup + 1])
                                tensor.wait_ge(DVE.h,
                                               dacc_mark.get(bank, 0))
                                last = None
                                for i2 in range(2):
                                    last = tensor.matmul(
                                        pdn[bank][:],
                                        dw_sb[i2][:, 128 * dd:128 * (dd + 1)],
                                        h_sb[i2][:, 512 * qh:512 * (qh + 1)],
                                        start=(i2 == 0), stop=(i2 == 1))
                                PE.inc(last, 1)

                            @block.vector
                            def _(vector):
                                vector.wait_ge(PE.h, p_dd + 1)
                                dst = dacc[:, 1024 * dd + 512 * qh:
                                           1024 * dd + 512 * (qh + 1)]
                                if isup == 0:
                                    DVE.inc(vector.tensor_copy(
                                        dst, pdn[bank][:]), 1)
                                else:
                                    DVE.inc(vector.tensor_add(
                                        dst, pdn[bank][:], dst), 1)
                            dacc_mark[bank] = DVE.n

                    for i2 in range(2):
                        pe_i_end[2 * isup + i2] = PE.n

                qctx1.close()

                with nc.psum_tensor(f"q{idx}pe0", [128, 128], F32) as pe0, \
                     nc.psum_tensor(f"q{idx}pe1", [128, 128], F32) as pe1:
                    peb = [pe0, pe1]
                    for dd in range(16):
                        d_t = DVE.n

                        @block.vector
                        def _(vector):
                            if dd >= 1:
                                vector.wait_ge(PE.h, PE.n)
                            DVE.inc(vector.tensor_scalar(
                                dacc[:, 1024 * dd:1024 * (dd + 1)],
                                dacc[:, 1024 * dd:1024 * (dd + 1)],
                                dsT[:, dd:dd + 1], None, ALU.mult), 1)

                        p_tb = PE.n
                        e_tb = E2.n

                        @block.tensor
                        def _(tensor):
                            for j in range(8):
                                tensor.wait_ge(DVE.h, d_t + 1)
                                if j >= 2:
                                    tensor.wait_ge(E2.h, e_tb + j - 1)
                                PE.inc(tensor.transpose(
                                    peb[j % 2][:],
                                    dacc[:, 1024 * dd + 128 * j:
                                         1024 * dd + 128 * (j + 1)],
                                    idf[:]), 1)

                        @block.vector
                        def _(vector):
                            for j in range(8):
                                vector.wait_ge(PE.h, p_tb + j + 1)
                                E2.inc(vector.tensor_add(
                                    xr[:, 2048 * j + 128 * dd:
                                       2048 * j + 128 * (dd + 1)],
                                    peb[j % 2][:],
                                    xr[:, 2048 * j + 128 * dd:
                                       2048 * j + 128 * (dd + 1)]), 1)

            fstk1.close()

        def emit_attn():
            def src_tile(j):
                return xr[:, j * D:(j + 1) * D]

            jE, jD, jA, jP = E2.n, DVE.n, ACTs.n, PE.n

            @block.sync
            def _(sync):
                sync.wait_ge(E2.h, jE)

            @block.scalar
            def _(scalar):
                scalar.wait_ge(E2.h, jE)
                scalar.wait_ge(PE.h, jP)

            @block.vector
            def _(vector):
                vector.wait_ge(E2.h, jE)
                vector.wait_ge(ACTs.h, jA)

            @block.tensor
            def _(tensor):
                tensor.wait_ge(E2.h, jE)
                tensor.wait_ge(DVE.h, jD)

            @block.gpsimd
            def _(gpsimd):
                gpsimd.wait_ge(E2.h, jE)
                gpsimd.wait_ge(DVE.h, jD)

            emit_ln_stats(src_tile, 8, absmax=False)

            astk = ExitStack()

            def sba(name, shape, dt):
                return astk.enter_context(nc.sbuf_tensor(name, shape, dt))

            qT = sba("a_qT", [128, 16 * 1024], BF16)
            gain_bc = sba("a_gain", [128, NH_A], F32)
            ngain = sba("a_ngain", [128, NH_A], F32)
            rsq = sba("a_rsq", [128, 2], F32)
            tstage = [sba(f"a_tst{i}", [128, 128], BF16) for i in range(4)]
            vstg = [sba(f"a_vst{i}", [128, 128], BF16) for i in range(4)]

            pstk = ExitStack()

            def sbp(name, shape, dt):
                return pstk.enter_context(nc.sbuf_tensor(name, shape, dt))

            xT = sbp("a_xT", [128, 16 * 1024], BF16)
            ropeC = sbp("a_ropeC", [128, 512], F32)
            ropeS = sbp("a_ropeS", [128, 512], F32)
            qsc = sbp("a_qsc", [128, 128], F32)
            rp1 = sbp("a_rp1", [128, 128], F32)
            rp2 = sbp("a_rp2", [128, 128], F32)
            nstage = [sbp("a_nst0", [128, 128], BF16),
                      sbp("a_nst1", [128, 128], BF16)]
            lnb2 = [sbp("a_lnb0", [128, D], BF16), sbp("a_lnb1", [128, D], BF16)]
            wq_og = [sbp("a_wq0", [128, 16 * 512], BF16),
                     sbp("a_wq1", [128, 16 * 512], BF16)]

            @block.sync
            def _(sync):
                sync.wait_ge(E2.h, jE)
                for j in range(8):
                    Lq.inc(sync.dma_start(
                        out=ropeC[:, 64 * j:64 * (j + 1)],
                        in_=rope_cos[128 * j:128 * (j + 1), :]), 16)
                    Lq.inc(sync.dma_start(
                        out=ropeS[:, 64 * j:64 * (j + 1)],
                        in_=rope_sin[128 * j:128 * (j + 1), :]), 16)
                Lq.inc(sync.dma_start(out=gain_bc[:],
                                      in_=grow.broadcast_to([128, NH_A])), 16)
                STs.inc(sync.wait_ge(Lq.h, Lq.n), 1)
            m_ac = STs.n

            @block.vector
            def _(vector):
                vector.wait_ge(STs.h, m_ac)
                DVE.wait(vector)
                DVE.inc(vector.tensor_scalar_mul(ngain[:], gain_bc[:], -1.0), 1)
                DVE.wait(vector)
                DVE.inc(vector.tensor_mul(sc_sb[:, 0:8], stat_nm[:, 0:8],
                                          stat_rs[:, 0:8]), 1)
            d_ng = DVE.n

            a_l2 = ACTs.n
            p_l2 = PE.n
            e_l2 = E2.n

            @block.scalar
            def _(scalar):
                scalar.wait_ge(DVE.h, d_ng)
                for j in range(8):
                    if j >= 2:
                        scalar.wait_ge(PE.h, p_l2 + 16 * (j - 1))
                    ACTs.inc(scalar.activation(out=lnb2[j % 2][:],
                                               in_=src_tile(j),
                                               func=AF.Identity,
                                               bias=sc_sb[:, j:j + 1],
                                               scale=stat_rs[:, j:j + 1]), 1)

            with nc.psum_tensor("a_pt0", [128, 128], BF16) as pa0, \
                 nc.psum_tensor("a_pt1", [128, 128], BF16) as pa1:
                psta = [pa0, pa1]

                @block.tensor
                def _(tensor):
                    for i in range(8 * 16):
                        j, k = divmod(i, 16)
                        if k == 0:
                            tensor.wait_ge(ACTs.h, a_l2 + j + 1)
                        if i >= 2:
                            tensor.wait_ge(E2.h, e_l2 + i - 1)
                        PE.inc(tensor.transpose(
                            psta[i % 2][:],
                            lnb2[j % 2][:, k * 128:(k + 1) * 128], idb[:]), 1)

                @block.vector
                def _(vector):
                    for i in range(8 * 16):
                        j, k = divmod(i, 16)
                        vector.wait_ge(PE.h, p_l2 + i + 1)
                        E2.inc(vector.tensor_copy(
                            xT[:, k * 1024 + j * 128:k * 1024 + (j + 1) * 128],
                            psta[i % 2][:]), 1)
            xT_ready = E2.n

            ga0 = [GS[0].n, GS[1].n]
            ga_cnt = {0: 0, 1: 0}
            pe_og_end = {}
            kv_mark = {}
            c_att = CC.n
            k_cnt = [0]
            v_cnt = [0]
            k_marks = {}
            v_marks = {}

            with ExitStack() as actx:
                pa = [actx.enter_context(nc.psum_tensor(f"a_pa{i}",
                                                        [128, 512], F32))
                      for i in range(4)]
                ptt = [actx.enter_context(nc.psum_tensor(f"a_ptt{i}",
                                                         [128, 128], BF16))
                       for i in range(2)]

                for og in range(12):
                    par = og % 2
                    ga_cnt[par] += 1
                    ga_need = ga0[par] + 16 * ga_cnt[par]

                    @block.gpsimd
                    def _(gpsimd):
                        if og >= 2:
                            gpsimd.wait_ge(PE.h, pe_og_end[og - 2])
                        else:
                            gpsimd.wait_ge(E2.h, xT_ready)
                        GS[par].inc(gpsimd.dma_start(
                            out=wq_og[par][:],
                            in_=wqkv[128 * og:128 * (og + 1), :]), 16)

                    for jg in range(2):
                        p_qk = PE.n

                        @block.tensor
                        def _(tensor):
                            tensor.wait_ge(GS[par].h, ga_need)
                            tensor.wait_ge(DVE.h, DVE.n)
                            tensor.wait_ge(ACTs.h, ACTs.n)
                            last = None
                            for j4 in range(4):
                                j = 4 * jg + j4
                                for k in range(16):
                                    last = tensor.matmul(
                                        pa[j4][:],
                                        xT[:, k * 1024 + 128 * j:
                                           k * 1024 + 128 * (j + 1)],
                                        wq_og[par][:, 512 * k:512 * (k + 1)],
                                        start=(k == 0), stop=(k == 15))
                            PE.inc(last, 1)

                        for j4 in range(4):
                            j = 4 * jg + j4
                            for hh in range(4):
                                h = (og % 4) * 4 + hh
                                kind = og // 4
                                chunk = pa[j4][:, 128 * hh:128 * (hh + 1)]
                                it = ((og * 2 + jg) * 4 + j4) * 4 + hh
                                rot = it % 2

                                if kind == 2:
                                    nv = v_cnt[0]
                                    v_cnt[0] += 1
                                    rv = nv % 4
                                    d_v = DVE.n

                                    @block.vector
                                    def _(vector):
                                        vector.wait_ge(PE.h, p_qk + 1)
                                        if nv >= 4:
                                            vector.wait_ge(STs.h,
                                                           v_marks[nv - 4])
                                        DVE.wait(vector)
                                        DVE.inc(vector.tensor_copy(
                                            vstg[rv][:], chunk), 1)

                                    @block.sync
                                    def _(sync):
                                        sync.wait_ge(DVE.h, d_v + 1)
                                        Lq.inc(sync.dma_start(
                                            out=v_src[h // 8]
                                            [:, ((h % 8) * 8 + j) * 128:
                                             ((h % 8) * 8 + j) * 128 + 128],
                                            in_=vstg[rv][:]), 16)
                                        STs.inc(sync.wait_ge(Lq.h, Lq.n), 1)
                                    v_marks[nv] = STs.n
                                    continue

                                d_n = DVE.n
                                a_n = ACTs.n

                                @block.scalar
                                def _(scalar):
                                    scalar.wait_ge(PE.h, p_qk + 1)
                                    scalar.wait_ge(DVE.h, d_n)
                                    ACTs.inc(scalar.activation(
                                        out=qsc[:], in_=chunk, func=AF.Square,
                                        accum_out=rsq[:, 0:1]), 1)
                                    ACTs.wait(scalar)
                                    ACTs.inc(scalar.activation(
                                        out=rsq[:, 1:2], in_=rsq[:, 0:1],
                                        func=AF.Sqrt, bias=cst[:, 2:3],
                                        scale=1.0), 1)

                                @block.vector
                                def _(vector):
                                    vector.wait_ge(ACTs.h, a_n + 2)
                                    DVE.inc(vector.reciprocal(rsq[:, 0:1],
                                                              rsq[:, 1:2]), 1)
                                    DVE.wait(vector)
                                    if h % 2 == 0:
                                        vector.tensor_scalar(
                                            rp1[:, 0:64], chunk[:, 0:64],
                                            rsq[:, 0:1], None, ALU.mult)
                                        DVE.inc(vector.tensor_scalar(
                                            rp1[:, 64:128], chunk[:, 64:128],
                                            rsq[:, 0:1], None, ALU.mult), 1)
                                        DVE.wait(vector)
                                        cj = ropeC[:, 64 * j:64 * (j + 1)]
                                        sj = ropeS[:, 64 * j:64 * (j + 1)]
                                        vector.tensor_mul(rp2[:, 0:64],
                                                          rp1[:, 0:64], cj)
                                        vector.tensor_mul(rp2[:, 64:128],
                                                          rp1[:, 64:128], sj)
                                        vector.tensor_mul(qsc[:, 0:64],
                                                          rp1[:, 0:64], sj)
                                        DVE.inc(vector.tensor_mul(
                                            qsc[:, 64:128],
                                            rp1[:, 64:128], cj), 1)
                                        DVE.wait(vector)
                                        vector.tensor_sub(
                                            nstage[rot][:, 0:64],
                                            rp2[:, 0:64], rp2[:, 64:128])
                                        DVE.inc(vector.tensor_add(
                                            nstage[rot][:, 64:128],
                                            qsc[:, 0:64], qsc[:, 64:128]), 1)
                                    else:
                                        DVE.inc(vector.tensor_scalar(
                                            nstage[rot][:], chunk,
                                            rsq[:, 0:1], None, ALU.mult), 1)

                                p_n = PE.n

                                @block.tensor
                                def _(tensor):
                                    tensor.wait_ge(DVE.h, DVE.n)
                                    PE.inc(tensor.transpose(
                                        ptt[rot][:], nstage[rot][:],
                                        idb[:]), 1)

                                if kind == 0:
                                    @block.vector
                                    def _(vector):
                                        vector.wait_ge(PE.h, p_n + 1)
                                        E2.inc(vector.tensor_copy(
                                            qT[:, h * 1024 + 128 * j:
                                               h * 1024 + 128 * (j + 1)],
                                            ptt[rot][:]), 1)
                                else:
                                    nk = k_cnt[0]
                                    k_cnt[0] += 1
                                    rk = nk % 4
                                    d_k = DVE.n

                                    @block.vector
                                    def _(vector):
                                        vector.wait_ge(PE.h, p_n + 1)
                                        if nk >= 4:
                                            vector.wait_ge(STs.h,
                                                           k_marks[nk - 4])
                                        DVE.inc(vector.tensor_copy(
                                            tstage[rk][:], ptt[rot][:]), 1)

                                    @block.sync
                                    def _(sync):
                                        sync.wait_ge(DVE.h, d_k + 1)
                                        Lq.inc(sync.dma_start(
                                            out=kt_src[h // 8]
                                            [128 * (h % 8):
                                             128 * (h % 8) + 128,
                                             128 * j:128 * (j + 1)],
                                            in_=tstage[rk][:]), 16)
                                        STs.inc(sync.wait_ge(Lq.h, Lq.n), 1)
                                    k_marks[nk] = STs.n

                    pe_og_end[og] = PE.n
                    kv_mark[og] = STs.n

                    if og in (5, 7, 9, 11):
                        @block.gpsimd
                        def _(gpsimd):
                            gpsimd.wait_ge(STs.h, kv_mark[og])
                            if og == 5:
                                CC.inc(gpsimd.collective_compute(
                                    "AllGather", ALU.bypass,
                                    replica_groups=[list(range(8))],
                                    ins=[kt_src[0]], outs=[kt_dst[0]]), 1)
                            elif og == 7:
                                CC.inc(gpsimd.collective_compute(
                                    "AllGather", ALU.bypass,
                                    replica_groups=[list(range(8))],
                                    ins=[kt_src[1]], outs=[kt_dst[1]]), 1)
                            elif og == 9:
                                CC.inc(gpsimd.collective_compute(
                                    "AllGather", ALU.bypass,
                                    replica_groups=[list(range(8))],
                                    ins=[v_src[0]], outs=[v_dst[0]]), 1)
                            else:
                                CC.inc(gpsimd.collective_compute(
                                    "AllGather", ALU.bypass,
                                    replica_groups=[list(range(8))],
                                    ins=[v_src[1]], outs=[v_dst[1]]), 1)

            pe_all = PE.n
            e_qT = E2.n

            pstk.close()
            cstk = ExitStack()

            def sbc(name, shape, dt):
                return cstk.enter_context(nc.sbuf_tensor(name, shape, dt))

            ktb = [sbc(f"a_ktb{i}", [128, 1024], BF16) for i in range(2)]
            vtb = [sbc(f"a_vtb{i}", [128, 1024], BF16) for i in range(2)]
            mask_rb = [sbc(f"a_mrb{i}", [128, 1024], BF16) for i in range(2)]
            pbuf = [sbc(f"a_pbuf{i}", [128, 1024], BF16) for i in range(2)]
            dinv_bc = sbc("a_dbc", [128, 1024], F32)
            wo_sb = sbc("a_wo", [128, 16 * 2048], BF16)
            dinv = scr[0:1, 0:1024]

            e_head = {}
            g_wo0 = Gq.n

            with ExitStack() as cctx:
                pSd = [cctx.enter_context(nc.psum_tensor(f"a_pS{i}",
                                                         [128, 1024], F32))
                       for i in range(2)]
                pO = [cctx.enter_context(nc.psum_tensor(f"a_pO{i}",
                                                        [128, 512], F32))
                      for i in range(2)]
                pden = [cctx.enter_context(nc.psum_tensor(f"a_pd{i}",
                                                          [1, 512], F32))
                        for i in range(2)]

                pe_r_done = {}
                a_exp = {}
                d_msk = {}
                p_S = {}
                m_kt = {}
                d_bc = [0]
                for h in range(NH_A):
                    need_cc = c_att + (3 if h < 8 else 4)

                    def emit_loads(r, h=h, need_cc=need_cc):
                        t_i = 8 * h + r
                        rot = r % 2
                        pe_pref = pe_r_done.get(t_i - 2, pe_all)
                        m_kt[t_i] = STs.n + 1

                        @block.sync
                        def _(sync):
                            sync.wait_ge(CC.h, need_cc)
                            sync.wait_ge(PE.h, pe_pref)
                            Lq.inc(sync.dma_start(
                                out=ktb[rot][:],
                                in_=kt_dst[h // 8]
                                [1024 * r + 128 * (h % 8):
                                 1024 * r + 128 * (h % 8) + 128, :]), 16)
                            Lq.inc(sync.dma_start(
                                out=vtb[rot][:],
                                in_=v_dst[h // 8]
                                [128 * r:128 * (r + 1),
                                 1024 * (h % 8):1024 * (h % 8) + 1024]), 16)
                            Lq.inc(sync.dma_start(
                                out=mask_rb[rot][:], in_=masks[r]), 16)
                            STs.inc(sync.wait_ge(Lq.h, Lq.n), 1)

                    def emit_pv(u, h=h):
                        r, ss = divmod(u, 8)
                        rot = r % 2
                        sr = u % 2
                        lo = 128 * ss
                        gu = 64 * h + u

                        @block.tensor
                        def _(tensor):
                            tensor.wait_ge(DVE.h, d_msk[gu])
                            first = (u == 0)
                            if lo < 512:
                                tensor.matmul(
                                    pden[0][:, lo:512], onesb_col[:],
                                    pbuf[sr][:, lo:512],
                                    start=first, stop=(u == 59))
                            tensor.matmul(
                                pden[1][:, max(0, lo - 512):512],
                                onesb_col[:],
                                pbuf[sr][:, max(512, lo):1024],
                                start=first, stop=(u == 63))
                            if lo < 512:
                                tensor.matmul(
                                    pO[0][:, lo:512],
                                    vtb[rot][:, 128 * ss:128 * (ss + 1)],
                                    pbuf[sr][:, lo:512],
                                    start=first, stop=(u == 59))
                            PE.inc(tensor.matmul(
                                pO[1][:, max(0, lo - 512):512],
                                vtb[rot][:, 128 * ss:128 * (ss + 1)],
                                pbuf[sr][:, max(512, lo):1024],
                                start=first, stop=(u == 63)), 1)
                        if ss == 7:
                            pe_r_done[8 * h + r] = PE.n

                    for r in range(8):
                        t_i = 8 * h + r
                        emit_loads(r)
                        for ss in range(8):
                            u = 8 * r + ss
                            gu = 64 * h + u
                            sr = u % 2
                            rot = r % 2
                            lo = 128 * ss

                            @block.tensor
                            def _(tensor):
                                if ss == 0:
                                    tensor.wait_ge(STs.h, m_kt[t_i])
                                if u == 0:
                                    if t_i == 0:
                                        tensor.wait_ge(E2.h, e_qT)
                                    else:
                                        tensor.wait_ge(DVE.h, d_bc[0])
                                if gu >= 2 and (gu - 2) in a_exp:
                                    tensor.wait_ge(ACTs.h, a_exp[gu - 2])
                                if lo < 512:
                                    tensor.matmul(
                                        pSd[sr][:, lo:512],
                                        ktb[rot][:, 128 * ss:128 * (ss + 1)],
                                        qT[:, h * 1024 + lo:h * 1024 + 512],
                                        start=True, stop=True)
                                PE.inc(tensor.matmul(
                                    pSd[sr][:, max(512, lo):1024],
                                    ktb[rot][:, 128 * ss:128 * (ss + 1)],
                                    qT[:, h * 1024 + max(512, lo):
                                       h * 1024 + 1024],
                                    start=True, stop=True), 1)
                            p_S[gu] = PE.n

                            @block.scalar
                            def _(scalar):
                                scalar.wait_ge(PE.h, p_S[gu])
                                ACTs.inc(scalar.activation(
                                    out=pbuf[sr][:, lo:1024],
                                    in_=pSd[sr][:, lo:1024], func=AF.Exp,
                                    bias=ngain[:, h:h + 1],
                                    scale=gain_bc[:, h:h + 1]), 1)
                            a_exp[gu] = ACTs.n

                            @block.vector
                            def _(vector):
                                vector.wait_ge(ACTs.h, a_exp[gu])
                                DVE.inc(vector.tensor_mul(
                                    pbuf[sr][:, lo:lo + 128],
                                    pbuf[sr][:, lo:lo + 128],
                                    mask_rb[rot][:, 128 * ss:
                                                  128 * (ss + 1)]), 1)
                            d_msk[gu] = DVE.n

                            if u >= 1:
                                emit_pv(u - 1)
                    emit_pv(63)

                    d_dn = DVE.n

                    @block.vector
                    def _(vector):
                        vector.wait_ge(PE.h, PE.n)
                        vector.tensor_copy(dinv[0:1, 0:512], pden[0][:])
                        DVE.inc(vector.tensor_copy(dinv[0:1, 512:1024],
                                                   pden[1][:]), 1)
                        DVE.wait(vector)
                        DVE.inc(vector.reciprocal(dinv[:], dinv[:]), 1)

                    p_b = PE.n

                    @block.tensor
                    def _(tensor):
                        tensor.wait_ge(DVE.h, d_dn + 2)
                        tensor.matmul(pSd[0][:, 0:512], of32g[:],
                                      dinv[0:1, 0:512],
                                      start=True, stop=True)
                        PE.inc(tensor.matmul(pSd[0][:, 512:1024], of32g[:],
                                             dinv[0:1, 512:1024],
                                             start=True, stop=True), 1)

                    @block.vector
                    def _(vector):
                        vector.wait_ge(PE.h, p_b + 1)
                        vector.tensor_copy(dinv_bc[:, 0:512],
                                           pSd[0][:, 0:512])
                        DVE.inc(vector.tensor_copy(dinv_bc[:, 512:1024],
                                                   pSd[0][:, 512:1024]), 1)
                        DVE.wait(vector)
                        vector.tensor_mul(qT[:, h * 1024:h * 1024 + 512],
                                          pO[0][:], dinv_bc[:, 0:512])
                        E2.inc(vector.tensor_mul(
                            qT[:, h * 1024 + 512:h * 1024 + 1024],
                            pO[1][:], dinv_bc[:, 512:1024]), 1)
                    d_bc[0] = DVE.n
                    e_head[h] = E2.n

                    if h % 4 == 3:
                        k4 = h // 4

                        @block.gpsimd
                        def _(gpsimd, k4=k4, eh=e_head[h]):
                            gpsimd.wait_ge(E2.h, eh)
                            Gq.inc(gpsimd.dma_start(
                                out=wo_sb[:, 4 * 2048 * k4:
                                          4 * 2048 * (k4 + 1)],
                                in_=wo_a[:, 4 * 2048 * k4:
                                         4 * 2048 * (k4 + 1)]), 16)

            oT_ready = E2.n
            g_wo = g_wo0

            with ExitStack() as wctx:
                pw = [wctx.enter_context(nc.psum_tensor(f"a_pw{i}",
                                                        [128, 512], F32))
                      for i in range(4)]

                for j in range(8):
                    p_w = PE.n

                    @block.tensor
                    def _(tensor):
                        if j == 0:
                            tensor.wait_ge(Gq.h, g_wo + 64)
                            tensor.wait_ge(E2.h, oT_ready)
                        else:
                            tensor.wait_ge(E2.h, E2.n)
                        last = None
                        for h in range(16):
                            for dd in range(4):
                                last = tensor.matmul(
                                    pw[dd][:],
                                    qT[:, h * 1024 + 128 * j:
                                       h * 1024 + 128 * (j + 1)],
                                    wo_sb[:, h * 2048 + 512 * dd:
                                          h * 2048 + 512 * (dd + 1)],
                                    start=(h == 0), stop=(h == 15))
                        PE.inc(last, 1)

                    @block.vector
                    def _(vector):
                        vector.wait_ge(PE.h, p_w + 1)
                        last = None
                        for dd in range(4):
                            last = vector.tensor_add(
                                xr[:, 2048 * j + 512 * dd:
                                   2048 * j + 512 * (dd + 1)],
                                pw[dd][:],
                                xr[:, 2048 * j + 512 * dd:
                                   2048 * j + 512 * (dd + 1)])
                        E2.inc(last, 1)

            cstk.close()
            astk.close()

        emit_ffn0(0, with_halo=(n_sub >= 2))
        if n_sub >= 2:
            emit_mamba()
        if n_sub >= 3:
            emit_ffn0(1, with_halo=False)
        hstk.close()
        if n_sub >= 4:
            emit_ffn1(0)
        if n_sub >= 5:
            emit_attn()
        if n_sub >= 6:
            emit_ffn1(1)

        @block.sync
        def _(sync):
            sync.wait_ge(E2.h, E2.n)
            for s in range(NSLOT):
                STs.inc(sync.dma_start(out=out[s * 128:(s + 1) * 128, :],
                                       in_=xr[:, s * D:(s + 1) * D]), 16)
            STs.wait(sync)

    return nc



def make_in_maps(inputs):
    f32 = np.float32
    bf = ml_dtypes.bfloat16
    x = np.asarray(inputs["L0_attn_u"]).reshape(8192, D)

    WinT_full = np.asarray(inputs["L0_attn_W_in"]).T
    WinT = np.ascontiguousarray(
        WinT_full[:, :8192].reshape(16, 128, 64, 128)
        .transpose(2, 1, 0, 3).reshape(64 * 128, 16 * 128))
    WinA = np.ascontiguousarray(
        WinT_full[:, 8192:8224].reshape(16, 128, 32)
        .transpose(1, 0, 2).reshape(128, 512))
    WoutT = np.ascontiguousarray(np.asarray(inputs["L0_attn_W_out"]).T)
    w0T = np.ascontiguousarray(np.asarray(inputs["L0_ffn_weight"]).T)
    WgT = np.ascontiguousarray(
        np.asarray(inputs["L1_ffn_gate_w"]).T.reshape(16, 128, 64, 128)
        .transpose(2, 1, 0, 3).reshape(64 * 128, 16 * 128))
    WuT = np.ascontiguousarray(
        np.asarray(inputs["L1_ffn_up_w"]).T.reshape(16, 128, 64, 128)
        .transpose(2, 1, 0, 3).reshape(64 * 128, 16 * 128))
    dwT = np.ascontiguousarray(np.asarray(inputs["L1_ffn_down_w"]).T)
    wqkv_t = np.ascontiguousarray(
        np.asarray(inputs["L1_attn_w_qkv"]).reshape(16, 128, 12, 512)
        .transpose(2, 1, 0, 3).reshape(12 * 128, 16 * 512))
    wo_t = np.ascontiguousarray(
        np.asarray(inputs["L1_attn_w_out"]).reshape(16, 128, 2048)
        .transpose(1, 0, 2).reshape(128, 16 * 2048))

    convw = np.ascontiguousarray(
        np.asarray(inputs["L0_attn_conv_w"]).reshape(48, 128, 4)
        .transpose(1, 0, 2).reshape(128, 192))
    convb = np.ascontiguousarray(
        np.asarray(inputs["L0_attn_conv_b"]).reshape(48, 128).T)

    ident_f = np.eye(128, dtype=f32)
    ident_b = np.eye(128, dtype=bf)
    ident_b64 = np.ascontiguousarray(np.tile(np.eye(64, dtype=bf), (2, 1)))
    ident_q = np.eye(128, dtype=ml_dtypes.float8_e4m3)
    triu_f = np.triu(np.ones((128, 128), dtype=f32))
    tril_i = np.ascontiguousarray(
        np.tril(np.ones((128, 128), dtype=np.int8), -1))

    half = 64
    inv_freq = (10000.0 ** (-np.arange(half, dtype=f32) / half)).astype(f32)
    theta = np.arange(8192, dtype=f32)[:, None] * inv_freq[None, :]
    cos_t = np.cos(theta).astype(f32)
    sin_t = np.sin(theta).astype(f32)

    tri_b = np.triu(np.ones((128, 128), dtype=f32)).astype(bf)
    one_t = np.ones((128, 128), dtype=bf)
    zero_t = np.zeros((128, 128), dtype=bf)

    in_maps = []
    for c in range(8):
        gs = [gtile(c, s) for s in range(NSLOT)]
        rows = np.concatenate([np.arange(g * 128, (g + 1) * 128) for g in gs])
        x0c = np.ascontiguousarray(x[rows])
        x0h = np.zeros((128, D), dtype=f32)
        halo_scale = np.ones((1, 24), dtype=f32)
        for s, g in enumerate(gs):
            if g == 0:
                halo_scale[0, 3 * s:3 * s + 3] = 0.0
            else:
                x0h[3 * s:3 * s + 3] = x[g * 128 - 3:g * 128]
        selb = np.zeros((64, 64), dtype=f32)
        for g in range(64):
            if g % 8 == (c + g // 8) % 8:
                selb[:, g] = 1.0
        mk = np.empty((8, 8, 128, 128), dtype=bf)
        for r in range(8):
            for s in range(NSLOT):
                kp = (r + s) % 8
                op = (c + s) % 8
                mk[r, s] = one_t if kp < op else (tri_b if kp == op else zero_t)
        mk = np.ascontiguousarray(mk.transpose(0, 2, 1, 3).reshape(8, 128, 1024))

        in_maps.append({
            "x0": x0c, "x0h": x0h, "halo_scale": halo_scale, "selb": selb,
            "masks": mk,
            "rope_cos": np.ascontiguousarray(cos_t[rows]),
            "rope_sin": np.ascontiguousarray(sin_t[rows]),
            "ident_f": ident_f, "ident_b": ident_b, "ident_b64": ident_b64,
            "ident_q": ident_q,
            "triu_f": triu_f, "tril_i": tril_i,
            "w0T": w0T, "b0row": np.asarray(inputs["L0_ffn_bias"]).reshape(1, D),
            "WinT": WinT, "WinA": WinA, "WoutT": WoutT,
            "convw": convw, "convb": convb,
            "zbF": np.ascontiguousarray(
                np.asarray(inputs["L0_attn_z_bias"]).reshape(16, 128).T),
            "Drow": np.asarray(inputs["L0_attn_D"]).reshape(1, NH_M),
            "wqkv": wqkv_t,
            "wo_a": wo_t,
            "grow": np.asarray(inputs["L1_attn_q_gain"]).reshape(1, NH_A),
            "WgT": WgT, "WuT": WuT, "dwT": dwT,
            "gs_in": np.asarray(inputs["L1_ffn_gate_scale"]).reshape(64, 128),
            "us_in": np.asarray(inputs["L1_ffn_up_scale"]).reshape(64, 128),
            "ds_in": np.asarray(inputs["L1_ffn_down_scale"]).reshape(16, 128),
        })
    return in_maps


def assemble(results):
    full = np.empty((1, 8192, D), dtype=np.float32)
    for c in range(8):
        o = results[c]["out"]
        for s in range(NSLOT):
            g = gtile(c, s)
            full[0, g * 128:(g + 1) * 128] = o[s * 128:(s + 1) * 128]
    return full


def kernel(**inputs):
    in_maps = make_in_maps(inputs)
    nc = build(N_SUB_DEFAULT)
    res = run_bass_kernel_spmd(nc, in_maps, core_ids=list(range(8)))
    return assemble(res.results)


# version 11
# speedup vs baseline: 1.1788x; 1.1788x over previous
from contextlib import ExitStack

import numpy as np
import ml_dtypes

import concourse.bass as bass
import concourse.mybir as mybir
from concourse.bass_utils import run_bass_kernel_spmd
from concourse.bass_isa import ReduceOp
from concourse import library_config

F32 = mybir.dt.float32
BF16 = mybir.dt.bfloat16
FP8 = mybir.dt.float8e4
I32 = mybir.dt.int32
AF = mybir.ActivationFunctionType
ALU = mybir.AluOpType
AX = mybir.AxisListType

D = 2048
TPC = 1024
NSLOT = 8
CONV_DIM = 6144
IN_OUT = 8224
NH_M = 32
DST = 64
HP = 64
NH_A = 16
I_FFN = 8192

N_SUB_DEFAULT = 6


def gtile(c, s):
    return 8 * s + (c + s) % 8


class Sem:
    def __init__(self, nc, ctx, name):
        self.h = ctx.enter_context(nc.semaphore(name))
        self.n = 0

    def inc(self, inst, k=1):
        inst.then_inc(self.h, k)
        self.n += k
        return self.n

    def wait(self, eng, n=None):
        eng.wait_ge(self.h, self.n if n is None else n)


def build(n_sub=N_SUB_DEFAULT):
    nc = bass.Bass("TRN2", target_bir_lowering=False)

    def inp(name, shape, dt):
        return nc.dram_tensor(name, shape, dt, kind="ExternalInput").ap()

    x0 = inp("x0", [TPC, D], F32)
    x0h = inp("x0h", [128, D], F32)
    halo_scale = inp("halo_scale", [1, 24], F32)
    selb = inp("selb", [64, 64], F32)
    masks = inp("masks", [8, 128, 8 * 128], BF16)
    rope_cos = inp("rope_cos", [TPC, 64], F32)
    rope_sin = inp("rope_sin", [TPC, 64], F32)
    ident_f = inp("ident_f", [128, 128], F32)
    ident_b = inp("ident_b", [128, 128], BF16)
    ident_b64 = inp("ident_b64", [128, 64], BF16)
    ident_q = inp("ident_q", [128, 128], FP8)
    triu_f = inp("triu_f", [128, 128], F32)
    tril_i = inp("tril_i", [128, 128], mybir.dt.int8)

    w0T = inp("w0T", [D, D], BF16)
    b0row = inp("b0row", [1, D], BF16)
    WinT = inp("WinT", [64 * 128, D], F32)
    WinA = inp("WinA", [128, 512], F32)
    WoutT = inp("WoutT", [D, D], F32)
    convw = inp("convw", [128, 192], F32)
    convb = inp("convb", [128, 48], F32)
    zbF = inp("zbF", [128, 16], F32)
    Drow = inp("Drow", [1, NH_M], F32)
    wqkv = inp("wqkv", [12 * 128, 16 * 512], F32)
    wo_a = inp("wo_a", [128, 16 * D], F32)
    grow = inp("grow", [1, NH_A], F32)
    WgT = inp("WgT", [I_FFN, D], I32)
    WuT = inp("WuT", [I_FFN, D], I32)
    dwT = inp("dwT", [I_FFN, D], I32)
    gs_in = inp("gs_in", [64, 128], F32)
    us_in = inp("us_in", [64, 128], F32)
    ds_in = inp("ds_in", [16, 128], F32)

    out = nc.dram_tensor("out", [TPC, D], F32, kind="ExternalOutput").ap()

    def idram(name, shape, dt):
        return nc.dram_tensor(name, shape, dt, kind="Internal").ap()

    def idram_sh(name, shape, dt):
        return nc.dram_tensor(name, shape, dt, kind="Internal",
                              addr_space="Shared").ap()

    x_spill = idram("x_spill", [TPC, D], F32)
    scal_d = idram("scal_d", [1, 4], F32)
    Wg_c = idram("Wg_c", [I_FFN, D], BF16)
    Wu_c = idram("Wu_c", [I_FFN, D], BF16)
    dw_c = idram("dw_c", [I_FFN, D], BF16)
    xs_src = [idram(f"xs_src{i}", [1, 16], F32) for i in range(2)]
    xs_dst = [idram_sh(f"xs_dst{i}", [8, 1, 16], F32) for i in range(2)]
    cd_src = idram("cd_src", [8, NH_M], F32)
    cd_dst = idram_sh("cd_dst", [64, NH_M], F32)
    st_src = idram("st_src", [DST, NSLOT * 2048], BF16)
    st_dst = idram_sh("st_dst", [8 * DST, NSLOT * 2048], BF16)
    yp_d = [idram(f"yp_d{i}", [128, 1024], BF16) for i in range(16)]
    svz_d = [idram(f"svz_d{i}", [128, 1048], BF16) for i in range(16)]
    kt_src = [idram(f"kt_src{i}", [1024, 1024], BF16) for i in range(2)]
    kt_dst = [idram_sh(f"kt_dst{i}", [8 * 1024, 1024], BF16) for i in range(2)]
    v_src = [idram(f"v_src{i}", [128, 8 * 1024], BF16) for i in range(2)]
    v_dst = [idram_sh(f"v_dst{i}", [8 * 128, 8 * 1024], BF16)
             for i in range(2)]

    with ExitStack() as ctx:
        def sb(name, shape, dt):
            return ctx.enter_context(nc.sbuf_tensor(name, shape, dt))

        xr = sb("xr", [128, NSLOT * D], F32)
        stat_nm = sb("stat_nm", [128, 16], F32)
        stat_rs = sb("stat_rs", [128, 16], F32)
        stat_ab = sb("stat_ab", [128, 16], F32)
        colb = sb("colb", [128, 8], F32)
        onesb_col = sb("onesb_col", [128, 1], BF16)
        onesb_row = sb("onesb_row", [1, 128], BF16)
        idf = sb("idf", [128, 128], F32)
        idb = sb("idb", [128, 128], BF16)
        idb64 = sb("idb64", [128, 64], BF16)
        idq = sb("idq", [128, 128], FP8)
        triuf = sb("triuf", [128, 128], F32)
        trili = sb("trili", [128, 128], mybir.dt.int8)
        scr = sb("scr", [128, D], F32)
        small = sb("small", [1, 144], F32)
        cst = sb("cst", [128, 4], F32)
        tpx = sb("tpx", [128, 128], F32)
        of32g = sb("of32g", [1, 128], F32)


        sems = {}
        for name in ("Lq", "Aq", "Vq", "Gq", "PE", "ACT", "DVE", "GP",
                     "CC", "ST", "E2", "WA", "WB", "GA", "GB"):
            sems[name] = Sem(nc, ctx, name)
        Lq, Aq, Vq, Gq = sems["Lq"], sems["Aq"], sems["Vq"], sems["Gq"]
        PE, ACTs, DVE, GP = sems["PE"], sems["ACT"], sems["DVE"], sems["GP"]
        CC, STs, E2 = sems["CC"], sems["ST"], sems["E2"]
        WS = [sems["WA"], sems["WB"]]
        GS = [sems["GA"], sems["GB"]]

        bnb = sb("bnb", [128, 216], F32)
        mv = sb("mv", [128, 18], F32)
        sc_sb = sb("sc_sb", [128, 16], F32)
        bq_sb = sb("bq_sb", [128, 16], F32)
        wsm = sb("wsm", [128, 16], F32)
        hstk = ExitStack()
        xh = hstk.enter_context(nc.sbuf_tensor("xh", [128, D], F32))

        block = ctx.enter_context(nc.Block())

        @block.sync
        def _(sync):
            for s in range(NSLOT):
                Lq.inc(sync.dma_start(out=xr[:, s * D:(s + 1) * D],
                                      in_=x0[s * 128:(s + 1) * 128, :]), 16)
            Lq.inc(sync.dma_start(out=xh[:], in_=x0h), 16)
            Lq.inc(sync.dma_start(out=idf[:], in_=ident_f), 16)
            Lq.inc(sync.dma_start(out=idb[:], in_=ident_b), 16)
            Lq.inc(sync.dma_start(out=idb64[:], in_=ident_b64), 16)
            Lq.inc(sync.dma_start(out=idq[:], in_=ident_q), 16)
            Lq.inc(sync.dma_start(out=triuf[:], in_=triu_f), 16)
            Lq.inc(sync.dma_start(out=trili[:], in_=tril_i), 16)
        prologue_loads = Lq.n

        @block.vector
        def _(vector):
            vector.memset(onesb_col[:], 1.0)
            vector.memset(onesb_row[:], 1.0)
            vector.memset(cst[:, 0:1], 1e-5)
            vector.memset(cst[:, 1:2], 1.0)
            vector.memset(cst[:, 2:3], 1e-6)
            vector.memset(of32g[:], 1.0)
            DVE.inc(vector.memset(cst[:, 3:4], 0.0), 1)
        cst_ready = DVE.n


        def emit_ln_stats(src_tile, ntile, absmax):
            @block.vector
            def _(vector):
                last = None
                for j in range(ntile):
                    for q in range(4):
                        last = vector.bn_stats(
                            bnb[:, j * 24 + q * 6: j * 24 + (q + 1) * 6],
                            src_tile(j)[:, q * 512:(q + 1) * 512])
                DVE.inc(last, 1)

            @block.vector
            def _(vector):
                DVE.wait(vector)
                last = None
                for j in range(ntile):
                    last = vector.bn_aggr(mv[:, 2 * j:2 * j + 2],
                                          bnb[:, j * 24:(j + 1) * 24])
                DVE.inc(last, 1)

            @block.vector
            def _(vector):
                DVE.wait(vector)
                DVE.inc(vector.tensor_scalar_mul(
                    stat_nm[:, 0:ntile], mv[:, 0:2 * ntile:2], -1.0), 1)
            d_mv = DVE.n

            @block.scalar
            def _(scalar):
                scalar.wait_ge(DVE.h, d_mv)
                ACTs.inc(scalar.activation(out=stat_rs[:, 0:ntile],
                                           in_=mv[:, 1:2 * ntile:2],
                                           func=AF.Sqrt, bias=cst[:, 0:1],
                                           scale=1.0), 1)

            @block.vector
            def _(vector):
                ACTs.wait(vector)
                DVE.inc(vector.reciprocal(stat_rs[:, 0:ntile],
                                          stat_rs[:, 0:ntile]), 1)

            if absmax:
                @block.vector
                def _(vector):
                    for j in range(ntile):
                        DVE.wait(vector)
                        DVE.inc(vector.tensor_scalar(
                            scr[:], src_tile(j), stat_nm[:, j:j + 1], None,
                            ALU.add), 1)
                        DVE.wait(vector)
                        DVE.inc(vector.reduce_max(
                            stat_ab[:, j:j + 1], scr[:], AX.X,
                            apply_absolute_value=True), 1)

                @block.vector
                def _(vector):
                    DVE.wait(vector)
                    DVE.inc(vector.tensor_mul(stat_ab[:, 0:ntile],
                                              stat_ab[:, 0:ntile],
                                              stat_rs[:, 0:ntile]), 1)
                    DVE.wait(vector)
                    DVE.inc(vector.reduce_max(colb[:, 1:2],
                                              stat_ab[:, 0:ntile], AX.X), 1)

        def emit_ffn0(idx, with_halo):
            ntile = 9 if with_halo else 8

            def src_tile(j):
                return xr[:, j * D:(j + 1) * D] if j < 8 else xh[:]

            @block.vector
            def _(vector):
                if idx == 0:
                    vector.wait_ge(Lq.h, prologue_loads)
                else:
                    vector.wait_ge(E2.h, E2.n)

            @block.scalar
            def _(scalar):
                if idx == 0:
                    scalar.wait_ge(Lq.h, prologue_loads)
                else:
                    scalar.wait_ge(E2.h, E2.n)

            emit_ln_stats(src_tile, ntile, absmax=True)

            @block.vector
            def _(vector):
                DVE.wait(vector)
                vector.memset(tpx[:, 1:128], 0.0)
                vector.memset(small[:, 1:16], 0.0)
                DVE.inc(vector.tensor_copy(tpx[:, 0:1], colb[:, 1:2]), 1)

            p_tx = PE.n
            with nc.psum_tensor(f"f0tpx{idx}", [128, 128], F32) as ptx:
                @block.tensor
                def _(tensor):
                    tensor.wait_ge(DVE.h, DVE.n)
                    PE.inc(tensor.transpose(ptx[:], tpx[:], idf[:]), 1)

                @block.vector
                def _(vector):
                    vector.wait_ge(PE.h, p_tx + 1)
                    DVE.inc(vector.reduce_max(small[:, 0:1], ptx[0:1, :],
                                              AX.X), 1)

            l_xs = Lq.n

            @block.sync
            def _(sync):
                sync.wait_ge(DVE.h, DVE.n)
                Lq.inc(sync.dma_start(out=xs_src[idx], in_=small[:, 0:16]), 16)

            c_xs = CC.n

            @block.gpsimd
            def _(gpsimd):
                gpsimd.wait_ge(Lq.h, l_xs + 16)
                CC.inc(gpsimd.collective_compute(
                    "AllGather", ALU.bypass, replica_groups=[list(range(8))],
                    ins=[xs_src[idx]], outs=[xs_dst[idx]]), 1)

            @block.sync
            def _(sync):
                sync.wait_ge(CC.h, c_xs + 1)
                Lq.inc(sync.dma_start(out=small[:, 16:144],
                                      in_=xs_dst[idx].flatten()[None, :]), 16)

            stk = ExitStack()
            def sbl(name, shape, dt):
                return stk.enter_context(nc.sbuf_tensor(name, shape, dt))
            wrot = [sbl(f"f0w{idx}a", [128, D], BF16),
                    sbl(f"f0w{idx}b", [128, D], BF16)]
            wq_sb = sbl(f"f0wq{idx}", [128, 16 * D], FP8)
            bias_bc = sbl(f"f0bias{idx}", [128, D], F32)
            bias_bb = sbl(f"f0biasb{idx}", [128, D], BF16)

            @block.scalar
            def _(scalar):
                Aq.inc(scalar.dma_start(out=bias_bb[:],
                                        in_=b0row.broadcast_to([128, D])), 16)
                Aq.wait(scalar)
                ACTs.inc(scalar.activation(out=bias_bc[:], in_=bias_bb[:],
                                           func=AF.Copy), 1)
            d_wr = DVE.n
            w_base = [WS[0].n, WS[1].n]

            @block.sync
            def _(sync):
                for k in range(16):
                    sync.wait_ge(DVE.h, d_wr + max(0, k - 1))
                    WS[k % 2].inc(sync.dma_start(
                        out=wrot[k % 2][:],
                        in_=w0T[k * 128:(k + 1) * 128, :]), 16)

            @block.vector
            def _(vector):
                last = None
                for k in range(16):
                    vector.wait_ge(WS[k % 2].h, w_base[k % 2] + 16 * (k // 2 + 1))
                    DVE.inc(vector.tensor_reduce(
                        wsm[:, k:k + 1], wrot[k % 2][:], AX.X, ALU.max,
                        apply_absolute_value=True), 1)

            @block.vector
            def _(vector):
                DVE.wait(vector)
                DVE.inc(vector.reduce_max(colb[:, 2:3], wsm[:, 0:16], AX.X), 1)

            @block.vector
            def _(vector):
                DVE.wait(vector)
                vector.wait_ge(PE.h, p_tx + 1)
                vector.wait_ge(Lq.h, Lq.n)
                DVE.inc(vector.tensor_copy(tpx[:, 0:1], colb[:, 2:3]), 1)

            p_tw = PE.n
            with nc.psum_tensor(f"f0tpw{idx}", [128, 128], F32) as ptw:
                @block.tensor
                def _(tensor):
                    tensor.wait_ge(DVE.h, DVE.n)
                    PE.inc(tensor.transpose(ptw[:], tpx[:], idf[:]), 1)

                @block.vector
                def _(vector):
                    vector.wait_ge(PE.h, p_tw + 1)
                    DVE.inc(vector.reduce_max(small[:, 1:2], ptw[0:1, :],
                                              AX.X), 1)

            @block.vector
            def _(vector):
                vector.wait_ge(Lq.h, Lq.n)
                DVE.wait(vector)
                DVE.inc(vector.reduce_max(small[:, 0:1], small[:, 16:144],
                                          AX.X), 1)
                DVE.wait(vector)
                vector.tensor_scalar_max(small[:, 0:1], small[:, 0:1], 1e-6)
                DVE.inc(vector.tensor_scalar_max(small[:, 1:2], small[:, 1:2],
                                                 1e-6), 1)
                DVE.wait(vector)
                vector.reciprocal(small[:, 2:3], small[:, 0:1])
                vector.reciprocal(small[:, 4:5], small[:, 1:2])
                DVE.inc(vector.tensor_mul(small[:, 3:4], small[:, 0:1],
                                          small[:, 1:2]), 1)

            @block.sync
            def _(sync):
                sync.wait_ge(DVE.h, DVE.n)
                Lq.inc(sync.dma_start(out=scal_d, in_=small[:, 2:6]), 16)
                sync.wait_ge(Lq.h, Lq.n)
                Lq.inc(sync.dma_start(
                    out=colb[:, 3:6],
                    in_=scal_d[0:1, 0:3].broadcast_to([128, 3])), 16)

            @block.vector
            def _(vector):
                vector.wait_ge(Lq.h, Lq.n)
                DVE.inc(vector.engine_nop(), 1)

            q_ready = DVE.n

            w_base = [WS[0].n, WS[1].n]

            @block.sync
            def _(sync):
                for k in range(16):
                    sync.wait_ge(DVE.h, q_ready + max(0, k - 1))
                    WS[k % 2].inc(sync.dma_start(
                        out=wrot[k % 2][:],
                        in_=w0T[k * 128:(k + 1) * 128, :]), 16)

            @block.vector
            def _(vector):
                for k in range(16):
                    vector.wait_ge(WS[k % 2].h, w_base[k % 2] + 16 * (k // 2 + 1))
                    DVE.inc(vector.tensor_scalar(
                        wq_sb[:, k * D:(k + 1) * D], wrot[k % 2][:],
                        colb[:, 5:6], None, ALU.mult), 1)

            wq_ready = DVE.n

            xq = [sbl(f"f0xq{idx}a", [128, D], FP8),
                  sbl(f"f0xq{idx}b", [128, D], FP8)]
            xqb = [sbl(f"f0xqb{idx}a", [128, D], BF16),
                   sbl(f"f0xqb{idx}b", [128, D], BF16)]
            xqT = sbl(f"f0xqT{idx}", [128, ntile * D], FP8)

            @block.vector
            def _(vector):
                vector.wait_ge(DVE.h, q_ready)
                DVE.inc(vector.tensor_scalar(sc_sb[:, 0:ntile],
                                             stat_rs[:, 0:ntile], colb[:, 3:4],
                                             None, ALU.mult), 1)
                DVE.wait(vector)
                DVE.inc(vector.tensor_mul(bq_sb[:, 0:ntile],
                                          stat_nm[:, 0:ntile],
                                          sc_sb[:, 0:ntile]), 1)

            xq_dve = DVE.n
            a_xq = ACTs.n
            d_xb = DVE.n
            p_tr = PE.n

            @block.scalar
            def _(scalar):
                scalar.wait_ge(DVE.h, xq_dve)
                for j in range(ntile):
                    if j >= 2:
                        scalar.wait_ge(DVE.h, d_xb + j - 1)
                    ACTs.inc(scalar.activation(out=xq[j % 2][:],
                                               in_=src_tile(j),
                                               func=AF.Identity,
                                               bias=bq_sb[:, j:j + 1],
                                               scale=sc_sb[:, j:j + 1]), 1)

            ntr = ntile * 16
            e_tr = E2.n
            with nc.psum_tensor(f"f0pta{idx}", [128, 128], BF16) as pta, \
                 nc.psum_tensor(f"f0ptb{idx}", [128, 128], BF16) as ptb:
                pst = [pta, ptb]

                @block.tensor
                def _(tensor):
                    for i in range(ntr):
                        j, k = divmod(i, 16)
                        if k == 0:
                            tensor.wait_ge(DVE.h, d_xb + j + 1)
                        if i >= 2:
                            tensor.wait_ge(E2.h, e_tr + i - 1)
                        PE.inc(tensor.transpose(
                            pst[i % 2][:],
                            xqb[j % 2][:, k * 128:(k + 1) * 128],
                            idb[:]), 1)

                @block.vector
                def _(vector):
                    for j in range(ntile):
                        vector.wait_ge(ACTs.h, a_xq + j + 1)
                        if j >= 2:
                            vector.wait_ge(PE.h, p_tr + 16 * (j - 1))
                        DVE.inc(vector.tensor_copy(xqb[j % 2][:],
                                                   xq[j % 2][:]), 1)
                        for k in range(16):
                            i = j * 16 + k
                            vector.wait_ge(PE.h, p_tr + i + 1)
                            E2.inc(vector.tensor_copy(
                                xqT[:, i * 128:(i + 1) * 128], pst[i % 2][:]), 1)

            ybf = sbl(f"f0y{idx}", [128, 2 * D], BF16)
            p_mm = PE.n
            e_ep = E2.n
            with ExitStack() as s2:
                po = [s2.enter_context(nc.psum_tensor(f"f0po{idx}{d}",
                                                      [128, 512], F32))
                      for d in range(8)]

                @block.tensor
                def _(tensor):
                    tensor.wait_ge(DVE.h, wq_ready)
                    tensor.wait_ge(E2.h, e_ep)
                    for j in range(ntile):
                        if j >= 2:
                            tensor.wait_ge(E2.h, e_ep + 2 * (j - 1))
                        bank = (j % 2) * 4
                        last = None
                        for k in range(16):
                            for d in range(4):
                                last = tensor.matmul(
                                    po[bank + d][:],
                                    xqT[:, (j * 16 + k) * 128:(j * 16 + k + 1) * 128],
                                    wq_sb[:, k * D + d * 512: k * D + (d + 1) * 512],
                                    start=(k == 0), stop=(k == 15))
                        PE.inc(last, 1)

                @block.vector
                def _(vector):
                    for j in range(ntile):
                        vector.wait_ge(PE.h, p_mm + j + 1)
                        bank = (j % 2) * 4
                        yb = ybf[:, (j % 2) * D:(j % 2 + 1) * D]
                        dst = src_tile(j)
                        last = None
                        for d in range(4):
                            last = vector.scalar_tensor_tensor(
                                yb[:, d * 512:(d + 1) * 512], po[bank + d][:],
                                colb[:, 4:5], bias_bc[:, d * 512:(d + 1) * 512],
                                ALU.mult, ALU.add)
                        E2.inc(last, 1)
                        DVE.wait(vector, DVE.n)
                        vector.wait_ge(E2.h, E2.n)
                        for d in range(4):
                            last = vector.scalar_tensor_tensor(
                                dst[:, d * 512:(d + 1) * 512],
                                yb[:, d * 512:(d + 1) * 512], 0.5,
                                dst[:, d * 512:(d + 1) * 512],
                                ALU.mult, ALU.add)
                        E2.inc(last, 1)

            stk.close()

        def emit_mamba():
            def src_tile(j):
                return xr[:, j * D:(j + 1) * D] if j < 8 else xh[:]

            jE, jD, jA, jP = E2.n, DVE.n, ACTs.n, PE.n

            @block.sync
            def _(sync):
                sync.wait_ge(E2.h, jE)

            @block.scalar
            def _(scalar):
                scalar.wait_ge(E2.h, jE)
                scalar.wait_ge(PE.h, jP)

            @block.vector
            def _(vector):
                vector.wait_ge(E2.h, jE)
                vector.wait_ge(ACTs.h, jA)

            @block.tensor
            def _(tensor):
                tensor.wait_ge(E2.h, jE)
                tensor.wait_ge(DVE.h, jD)

            @block.gpsimd
            def _(gpsimd):
                gpsimd.wait_ge(E2.h, jE)

            emit_ln_stats(src_tile, 9, absmax=False)

            @block.vector
            def _(vector):
                DVE.wait(vector)
                DVE.inc(vector.tensor_mul(sc_sb[:, 0:9], stat_nm[:, 0:9],
                                          stat_rs[:, 0:9]), 1)
            d_nmr = DVE.n

            mstk = ExitStack()

            def sbm(name, shape, dt):
                return mstk.enter_context(nc.sbuf_tensor(name, shape, dt))

            cw_sb = sbm("cw_sb", [128, 192], F32)
            cb_sb = sbm("cb_sb", [128, 48], F32)
            zb_sb = sbm("zb_sb", [128, 16], F32)
            dcol = sbm("dcol", [128, NH_M], F32)
            a_sb = sbm("a_sb", [32, 1048], F32)
            cum = sbm("cum", [32, 1024], F32)
            cumT = sbm("cumT", [128, 256], F32)
            sdt = sbm("sdt", [128, 256], F32)
            ecum = sbm("ecum", [128, 256], F32)
            ones32 = sbm("ones32", [32, 128], F32)
            plrow = sbm("plrow", [1, 256], F32)
            cdloc = sbm("cdloc", [1, 256], F32)
            sdpre = sbm("sdpre", [128, 32], F32)
            cf = sbm("cf", [128, 16 * 1024], BF16)

            @block.sync
            def _(sync):
                sync.wait_ge(E2.h, jE)
                Vq.inc(sync.dma_start(out=cw_sb[:], in_=convw), 16)
                Vq.inc(sync.dma_start(out=cb_sb[:], in_=convb), 16)
                Vq.inc(sync.dma_start(out=zb_sb[:], in_=zbF), 16)
                Vq.inc(sync.dma_start(out=dcol[:],
                                      in_=Drow.broadcast_to([128, NH_M])), 16)
            l_mcst = Vq.n

            @block.vector
            def _(vector):
                DVE.wait(vector)
                DVE.inc(vector.memset(ones32[:], 1.0), 1)
            d_ones = DVE.n

            lnstk = ExitStack()

            def sbn(name, shape, dt):
                return lnstk.enter_context(nc.sbuf_tensor(name, shape, dt))

            lnT = sbn("lnT", [128, 16 * 1048], BF16)
            lnb = [sbn("lnb0", [128, D], BF16), sbn("lnb1", [128, D], BF16)]
            hsb = sbn("hsb", [128, 24], F32)
            wl = [sbn("wl0", [128, 2048], BF16), sbn("wl1", [128, 2048], BF16)]
            wlA = sbn("wlA", [128, 512], BF16)
            convin = [sbn("cvin0", [128, 1048], BF16),
                      sbn("cvin1", [128, 1048], BF16)]
            cumrow = [sbn("cumrow0", [1, 128], F32),
                      sbn("cumrow1", [1, 128], F32)]
            zstage = [sbn("zstage0", [128, 1048], BF16)]
            xf = [sbn("xf0", [128, 1024], BF16), sbn("xf1", [128, 1024], BF16)]
            bf2 = [sbn("bf0", [128, 1024], BF16), sbn("bf1", [128, 1024], BF16)]
            stp = [sbn("stp0", [DST, 1024], BF16),
                   sbn("stp1", [DST, 1024], BF16)]
            yp = [sbn("yp0", [128, 1024], BF16), sbn("yp1", [128, 1024], BF16)]
            cvf = sbn("cvf", [128, 128], F32)
            x_tok = [sbn("x_tok0", [128, 64], BF16), sbn("x_tok1", [128, 64], BF16)]
            b_tok = [sbn("b_tok0", [128, 64], BF16), sbn("b_tok1", [128, 64], BF16)]
            bt2 = [sbn("bt20", [128, 64], BF16), sbn("bt21", [128, 64], BF16)]
            dk = [sbn("dk0", [128, 128], F32)]
            expb = [sbn("expb0", [128, 128], BF16), sbn("expb1", [128, 128], BF16)]
            ltb = [sbn("ltb0", [128, 128], BF16), sbn("ltb1", [128, 128], BF16)]

            @block.scalar
            def _(scalar):
                Aq.inc(scalar.dma_start(out=hsb[:],
                                        in_=halo_scale.broadcast_to([128, 24])),
                       16)
                GP.inc(scalar.wait_ge(Aq.h, Aq.n), 1)
            a_hsb = GP.n



            a_ln = ACTs.n
            p_lnt = PE.n
            e_lnt = E2.n

            @block.scalar
            def _(scalar):
                scalar.wait_ge(DVE.h, d_nmr)
                for j in range(9):
                    if j >= 2:
                        scalar.wait_ge(PE.h, p_lnt + 16 * (j - 1))
                    ACTs.inc(scalar.activation(out=lnb[j % 2][:],
                                               in_=src_tile(j),
                                               func=AF.Identity,
                                               bias=sc_sb[:, j:j + 1],
                                               scale=stat_rs[:, j:j + 1]), 1)

            with nc.psum_tensor("m_pta", [128, 128], BF16) as pta, \
                 nc.psum_tensor("m_ptb", [128, 128], BF16) as ptb:
                pst = [pta, ptb]

                @block.tensor
                def _(tensor):
                    for i in range(9 * 16):
                        j, k = divmod(i, 16)
                        if k == 0:
                            tensor.wait_ge(ACTs.h, a_ln + j + 1)
                        if i >= 2:
                            tensor.wait_ge(E2.h, e_lnt + i - 1)
                        PE.inc(tensor.transpose(
                            pst[i % 2][:],
                            lnb[j % 2][:, k * 128:(k + 1) * 128], idb[:]), 1)

                @block.vector
                def _(vector):
                    vector.wait_ge(GP.h, a_hsb)
                    for i in range(9 * 16):
                        j, k = divmod(i, 16)
                        vector.wait_ge(PE.h, p_lnt + i + 1)
                        if j < 8:
                            E2.inc(vector.tensor_copy(
                                lnT[:, k * 1048 + 131 * j + 3:
                                    k * 1048 + 131 * j + 131],
                                pst[i % 2][:]), 1)
                        else:
                            last = None
                            for ss in range(8):
                                last = vector.tensor_mul(
                                    lnT[:, k * 1048 + 131 * ss:
                                        k * 1048 + 131 * ss + 3],
                                    pst[i % 2][:, 3 * ss:3 * ss + 3],
                                    hsb[:, 3 * ss:3 * ss + 3])
                            E2.inc(last, 1)

            lnT_ready = E2.n

            with ExitStack() as pctx:
                pz = [pctx.enter_context(nc.psum_tensor("m_pz0", [128, 512], F32)),
                      pctx.enter_context(nc.psum_tensor("m_pz1", [128, 512], F32)),
                      pctx.enter_context(nc.psum_tensor("m_pz2", [128, 24], F32))]
                ptr = pctx.enter_context(nc.psum_tensor("m_ptr", [128, 128], BF16))
                pd = pctx.enter_context(nc.psum_tensor("m_pd", [128, 128], F32))
                pcb = pctx.enter_context(nc.psum_tensor("m_pcb", [128, 128], F32))
                pstm = pctx.enter_context(nc.psum_tensor("m_pstm", [64, 64], F32))
                pym = pctx.enter_context(nc.psum_tensor("m_pym", [128, 64], F32))

                def win_mms(tensor, lhs_ap, lhs_w):
                    last = None
                    for k in range(16):
                        lt = lhs_ap(k)
                        tensor.matmul(pz[0][0:lhs_w, :], lt,
                                      lnT[:, k * 1048:k * 1048 + 512],
                                      start=(k == 0), stop=(k == 15))
                        tensor.matmul(pz[1][0:lhs_w, :], lt,
                                      lnT[:, k * 1048 + 512:k * 1048 + 1024],
                                      start=(k == 0), stop=(k == 15))
                        last = tensor.matmul(
                            pz[2][0:lhs_w, :], lt,
                            lnT[:, k * 1048 + 1024:k * 1048 + 1048],
                            start=(k == 0), stop=(k == 15))
                    return last

                g_wa = Gq.n

                @block.gpsimd
                def _(gpsimd):
                    Gq.inc(gpsimd.dma_start(out=wlA[:], in_=WinA[:, :]), 16)

                p_al = PE.n

                @block.tensor
                def _(tensor):
                    tensor.wait_ge(E2.h, lnT_ready)
                    tensor.wait_ge(Gq.h, g_wa + 16)
                    PE.inc(win_mms(tensor,
                                   lambda k: wlA[:, k * 32:(k + 1) * 32], 32), 1)

                @block.vector
                def _(vector):
                    vector.wait_ge(PE.h, p_al + 1)
                    vector.tensor_copy(a_sb[:, 0:512], pz[0][0:32, :])
                    vector.tensor_copy(a_sb[:, 512:1024], pz[1][0:32, :])
                    DVE.inc(vector.tensor_copy(a_sb[:, 1024:1048],
                                               pz[2][0:32, :]), 1)
                d_asb = DVE.n

                @block.scalar
                def _(scalar):
                    scalar.wait_ge(DVE.h, d_asb)
                    ACTs.inc(scalar.activation(out=a_sb[:], in_=a_sb[:],
                                               func=AF.Exp), 1)
                    ACTs.wait(scalar)
                    ACTs.inc(scalar.activation(out=a_sb[:], in_=a_sb[:],
                                               func=AF.Ln,
                                               bias=cst[0:32, 1:2],
                                               scale=1.0), 1)
                a_dt = ACTs.n

                @block.vector
                def _(vector):
                    vector.wait_ge(ACTs.h, a_dt)
                    vector.wait_ge(DVE.h, d_ones)
                    last = None
                    for j in range(8):
                        last = vector.tensor_tensor_scan(
                            cum[:, 128 * j:128 * (j + 1)], ones32[:],
                            a_sb[:, 131 * j + 3:131 * j + 131], 0.0,
                            ALU.mult, ALU.add)
                    DVE.inc(last, 1)
                d_cum = DVE.n

                @block.scalar
                def _(scalar):
                    scalar.wait_ge(DVE.h, d_cum)
                    for j in range(8):
                        Aq.inc(scalar.dma_start(
                            out=plrow[0:1, 32 * j:32 * (j + 1)],
                            in_=cum[:, 128 * j + 127:128 * j + 128]), 16)
                    GP.inc(scalar.wait_ge(Aq.h, Aq.n), 1)
                a_pl = GP.n

                p_pre = PE.n
                d_pre = DVE.n
                a_pre = ACTs.n

                @block.tensor
                def _(tensor):
                    tensor.wait_ge(GP.h, a_pl)
                    for j in range(8):
                        if j > 0:
                            tensor.wait_ge(DVE.h, d_pre + 2 * j)
                            tensor.wait_ge(ACTs.h, a_pre + 3 * j)
                        PE.inc(tensor.transpose(pd[0:128, 0:32],
                                                cum[:, 128 * j:128 * (j + 1)],
                                                idf[0:32, 0:32]), 1)
                        PE.inc(tensor.matmul(pcb[:, 0:32], of32g[:],
                                             plrow[0:1, 32 * j:32 * (j + 1)],
                                             start=True, stop=True), 1)

                @block.vector
                def _(vector):
                    for j in range(8):
                        vector.wait_ge(PE.h, p_pre + 2 * j + 1)
                        if j > 0:
                            vector.wait_ge(ACTs.h, a_pre + 3 * (j - 1) + 1)
                        DVE.inc(vector.tensor_copy(cumT[:, 32 * j:32 * (j + 1)],
                                                   pd[:, 0:32]), 1)
                        vector.wait_ge(PE.h, p_pre + 2 * j + 2)
                        vector.wait_ge(DVE.h, d_pre + 2 * j + 1)
                        DVE.inc(vector.tensor_sub(sdpre[:],
                                                  cumT[:, 32 * j:32 * (j + 1)],
                                                  pcb[:, 0:32]), 1)

                @block.scalar
                def _(scalar):
                    for j in range(8):
                        scalar.wait_ge(DVE.h, d_pre + 2 * j + 2)
                        ACTs.inc(scalar.activation(
                            out=sdt[:, 32 * j:32 * (j + 1)],
                            in_=sdpre[:], func=AF.Exp), 1)
                        ACTs.inc(scalar.activation(
                            out=ecum[:, 32 * j:32 * (j + 1)],
                            in_=cumT[:, 32 * j:32 * (j + 1)],
                            func=AF.Exp, scale=-1.0), 1)
                        ACTs.inc(scalar.activation(
                            out=cdloc[0:1, 32 * j:32 * (j + 1)],
                            in_=plrow[0:1, 32 * j:32 * (j + 1)],
                            func=AF.Exp, scale=-1.0), 1)
                a_cd = ACTs.n

                l_cd = Lq.n

                @block.sync
                def _(sync):
                    sync.wait_ge(ACTs.h, a_cd)
                    Lq.inc(sync.dma_start(out=cd_src.flatten()[None, :],
                                          in_=cdloc[:]), 16)

                c_cd = CC.n

                @block.gpsimd
                def _(gpsimd):
                    gpsimd.wait_ge(Lq.h, l_cd + 16)
                    CC.inc(gpsimd.collective_compute(
                        "AllGather", ALU.bypass,
                        replica_groups=[list(range(8))],
                        ins=[cd_src], outs=[cd_dst]), 1)

                aq_z = {}
                pe_zt_end = {}
                gz0 = [GS[0].n, GS[1].n]
                for zt in range(16):
                    @block.gpsimd
                    def _(gpsimd):
                        if zt >= 2:
                            gpsimd.wait_ge(PE.h, pe_zt_end[zt - 2])
                        GS[zt % 2].inc(gpsimd.dma_start(
                            out=wl[zt % 2][:],
                            in_=WinT[128 * (48 + zt):
                                     128 * (48 + zt + 1), :]), 16)

                    p_z = PE.n

                    @block.tensor
                    def _(tensor):
                        if zt == 0:
                            tensor.wait_ge(E2.h, lnT_ready)
                            tensor.wait_ge(DVE.h, DVE.n)
                        tensor.wait_ge(GS[zt % 2].h,
                                       gz0[zt % 2] + 16 * (zt // 2 + 1))
                        if zt >= 1:
                            tensor.wait_ge(ACTs.h, ACTs.n)
                        PE.inc(win_mms(
                            tensor,
                            lambda k: wl[zt % 2][:, k * 128:(k + 1) * 128],
                            128), 1)

                    pe_zt_end[zt] = PE.n

                    @block.vector
                    def _(vector):
                        vector.wait_ge(PE.h, p_z + 1)
                        vector.wait_ge(Vq.h, l_mcst)
                        vector.tensor_scalar(scr[:, 0:512], pz[0][:],
                                             zb_sb[:, zt:zt + 1], None, ALU.add)
                        vector.tensor_scalar(scr[:, 512:1024], pz[1][:],
                                             zb_sb[:, zt:zt + 1], None, ALU.add)
                        DVE.inc(vector.tensor_scalar(
                            scr[:, 1024:1048], pz[2][:],
                            zb_sb[:, zt:zt + 1], None, ALU.add), 1)

                    @block.scalar
                    def _(scalar):
                        scalar.wait_ge(DVE.h, DVE.n)
                        if zt >= 1:
                            scalar.wait_ge(Aq.h, aq_z[zt - 1])
                        ACTs.inc(scalar.activation(
                            out=zstage[0][:],
                            in_=scr[:, 0:1048], func=AF.Silu), 1)
                        ACTs.wait(scalar)
                        Aq.inc(scalar.dma_start(out=svz_d[zt],
                                                in_=zstage[0][:]), 16)
                    aq_z[zt] = Aq.n

                @block.vector
                def _(vector):
                    DVE.wait(vector)
                    vector.wait_ge(ACTs.h, ACTs.n)
                    DVE.inc(vector.memset(scr[:, 0:128], -100.0), 1)

                pe_pair_end = {}
                aq_yp = {}
                pe_decay_hist = {}
                gt0 = [GS[0].n, GS[1].n]
                gt_cnt = {0: 0, 1: 0}
                wl_free = {(0, 0): pe_zt_end[14], (1, 0): pe_zt_end[15]}
                last_dcv = [DVE.n]

                def emit_trio(pair, t3):
                    ot = pair + 16 * t3
                    gt_cnt[ot % 2] += 1
                    gt_need = gt0[ot % 2] + 16 * gt_cnt[ot % 2]

                    @block.gpsimd
                    def _(gpsimd):
                        gpsimd.wait_ge(PE.h, wl_free.get((pair, t3), 0))
                        GS[ot % 2].inc(gpsimd.dma_start(
                            out=wl[ot % 2][:],
                            in_=WinT[128 * ot:128 * (ot + 1), :]), 16)

                    p_t = PE.n

                    @block.tensor
                    def _(tensor):
                        tensor.wait_ge(GS[ot % 2].h, gt_need)
                        tensor.wait_ge(DVE.h, last_dcv[0])
                        PE.inc(win_mms(
                            tensor,
                            lambda k: wl[ot % 2][:, k * 128:(k + 1) * 128],
                            128), 1)
                    if t3 < 2:
                        wl_free[(pair, t3 + 1)] = PE.n
                    else:
                        wl_free[(pair + 2, 0)] = PE.n

                    @block.vector
                    def _(vector):
                        vector.wait_ge(PE.h, p_t + 1)
                        if pair >= 2 and t3 < 2:
                            vector.wait_ge(PE.h, pe_pair_end[pair - 2])
                        vector.tensor_copy(convin[t3 % 2][:, 0:512], pz[0][:])
                        vector.tensor_copy(convin[t3 % 2][:, 512:1024],
                                           pz[1][:])
                        DVE.inc(vector.tensor_copy(
                            convin[t3 % 2][:, 1024:1048], pz[2][:]), 1)
                    d_cv = DVE.n
                    last_dcv[0] = d_cv

                    dstb = xf[pair % 2] if t3 == 0 else bf2[pair % 2]
                    a_c0 = ACTs.n

                    @block.scalar
                    def _(scalar):
                        scalar.wait_ge(DVE.h, d_cv)
                        scalar.wait_ge(Vq.h, l_mcst)
                        for j in range(8):
                            if j > 0:
                                scalar.wait_ge(DVE.h, d_cv + 3 * j)
                            ACTs.inc(scalar.activation(
                                out=cvf[:],
                                in_=convin[t3 % 2][:, 131 * j:131 * j + 128],
                                func=AF.Identity,
                                bias=cb_sb[:, ot:ot + 1],
                                scale=cw_sb[:, 4 * ot:4 * ot + 1]), 1)

                    @block.vector
                    def _(vector):
                        for j in range(8):
                            vector.wait_ge(ACTs.h, a_c0 + j + 1)
                            for kk in (1, 2):
                                DVE.wait(vector)
                                DVE.inc(vector.scalar_tensor_tensor(
                                    cvf[:],
                                    convin[t3 % 2][:, 131 * j + kk:
                                                   131 * j + kk + 128],
                                    cw_sb[:, 4 * ot + kk:4 * ot + kk + 1],
                                    cvf[:], ALU.mult, ALU.add), 1)
                            DVE.wait(vector)
                            dst = (cf[:, pair * 1024 + 128 * j:
                                      pair * 1024 + 128 * (j + 1)]
                                   if t3 == 2 else
                                   dstb[:, 128 * j:128 * (j + 1)])
                            DVE.inc(vector.scalar_tensor_tensor(
                                dst,
                                convin[t3 % 2][:, 131 * j + 3:
                                               131 * j + 131],
                                cw_sb[:, 4 * ot + 3:4 * ot + 4],
                                cvf[:], ALU.mult, ALU.add), 1)

                for pair in range(16):
                    if pair == 0:
                        for t3 in range(3):
                            emit_trio(0, t3)
                    for it in range(16):
                        hl, j = divmod(it, 8)
                        h = 2 * pair + hl
                        rot = it % 2
                        bp = 64 * hl

                        p_i = PE.n
                        d_i = DVE.n
                        a_i = ACTs.n

                        @block.scalar
                        def _(scalar):
                            if (pair, it) in pe_decay_hist:
                                scalar.wait_ge(PE.h, pe_decay_hist[(pair, it)])
                            WS[it % 2].inc(scalar.dma_start(
                                out=cumrow[it % 2][:],
                                in_=cum[h:h + 1, 128 * j:128 * (j + 1)]), 16)
                        aq_cr = (WS[it % 2], WS[it % 2].n)

                        @block.tensor
                        def _(tensor):
                            tensor.wait_ge(DVE.h, d_i)
                            if it == 0:
                                if pair >= 2:
                                    tensor.wait_ge(GP.h, aq_yp[pair - 2])
                            PE.inc(tensor.transpose(
                                ptr[:, 0:64],
                                xf[pair % 2][bp:bp + 64, 128 * j:128 * (j + 1)],
                                idb64[bp:bp + 64, :]), 1)
                            PE.inc(tensor.transpose(
                                ptr[:, 64:128],
                                bf2[pair % 2][bp:bp + 64, 128 * j:128 * (j + 1)],
                                idb64[bp:bp + 64, :]), 1)

                        @block.vector
                        def _(vector):
                            vector.wait_ge(PE.h, p_i + 2)
                            vector.tensor_copy(x_tok[rot][:], ptr[:, 0:64])
                            DVE.inc(vector.tensor_copy(b_tok[rot][:],
                                                       ptr[:, 64:128]), 1)
                            DVE.wait(vector)
                            DVE.inc(vector.tensor_scalar(
                                bt2[rot][:], b_tok[rot][:],
                                sdt[:, 32 * j + h:32 * j + h + 1],
                                None, ALU.mult), 1)

                        @block.tensor
                        def _(tensor):
                            tensor.wait_ge(DVE.h, d_i + 2)
                            tensor.wait_ge(ACTs.h, a_i)
                            PE.inc(tensor.matmul(pstm[:], bt2[rot][:],
                                                 x_tok[rot][:],
                                                 start=True, stop=True), 1)
                            tensor.wait_ge(aq_cr[0].h, aq_cr[1])
                            PE.inc(tensor.matmul(
                                pd[:], of32g[:], cumrow[it % 2][:],
                                start=True, stop=True), 1)
                        pe_decay_hist[(pair, it + 2)] = PE.n if it + 2 < 16 \
                            else PE.n
                        if it + 2 >= 16 and pair + 1 < 16:
                            pe_decay_hist[(pair + 1, it + 2 - 16)] = PE.n

                        @block.vector
                        def _(vector):
                            vector.wait_ge(PE.h, p_i + 3)
                            DVE.inc(vector.tensor_copy(
                                stp[pair % 2][:, j * 128 + bp:
                                              j * 128 + bp + 64],
                                pstm[:]), 1)

                        @block.vector
                        def _(vector):
                            vector.wait_ge(PE.h, p_i + 4)
                            vector.wait_ge(ACTs.h, a_i)
                            DVE.inc(vector.tensor_scalar(
                                dk[0][:], pd[:], -1.0,
                                cumT[:, 32 * j + h:32 * j + h + 1],
                                ALU.mult, ALU.add), 1)
                            DVE.wait(vector)
                            DVE.inc(vector.copy_predicated(
                                dk[0][:], trili[:],
                                scr[:, 0:128]), 1)

                        @block.scalar
                        def _(scalar):
                            scalar.wait_ge(DVE.h, d_i + 5)
                            ACTs.inc(scalar.activation(
                                out=expb[rot][:], in_=dk[0][:],
                                func=AF.Exp), 1)

                        @block.tensor
                        def _(tensor):
                            tensor.wait_ge(DVE.h, d_i + 4)
                            PE.inc(tensor.matmul(
                                pcb[:],
                                bf2[pair % 2][bp:bp + 64, 128 * j:128 * (j + 1)],
                                cf[bp:bp + 64, pair * 1024 + 128 * j:
                                   pair * 1024 + 128 * (j + 1)],
                                start=True, stop=True), 1)

                        @block.vector
                        def _(vector):
                            vector.wait_ge(PE.h, p_i + 5)
                            vector.wait_ge(ACTs.h, a_i + 1)
                            DVE.inc(vector.tensor_mul(ltb[rot][:], pcb[:],
                                                      expb[rot][:]), 1)

                        @block.tensor
                        def _(tensor):
                            tensor.wait_ge(DVE.h, d_i + 6)
                            PE.inc(tensor.matmul(pym[:], ltb[rot][:],
                                                 x_tok[rot][:],
                                                 start=True, stop=True), 1)

                        @block.vector
                        def _(vector):
                            vector.wait_ge(PE.h, p_i + 6)
                            vector.wait_ge(Vq.h, l_mcst)
                            DVE.inc(vector.scalar_tensor_tensor(
                                yp[pair % 2][:, j * 128 + bp:j * 128 + bp + 64],
                                x_tok[rot][:], dcol[:, h:h + 1], pym[:],
                                ALU.mult, ALU.add), 1)

                        if pair + 1 < 16 and it in (3, 8, 13):
                            emit_trio(pair + 1, (3, 8, 13).index(it))

                    pe_pair_end[pair] = PE.n

                    @block.scalar
                    def _(scalar):
                        scalar.wait_ge(DVE.h, DVE.n)
                        Aq.inc(scalar.dma_start(out=yp_d[pair],
                                                in_=yp[pair % 2][:]), 16)
                        Aq.inc(scalar.dma_start(
                            out=st_src.rearrange("n (j c) -> n j c", j=8)
                            [:, :, 128 * pair:128 * (pair + 1)],
                            in_=stp[pair % 2].rearrange("n (j c) -> n j c",
                                                        j=8)), 16)
                        GP.inc(scalar.wait_ge(Aq.h, Aq.n), 1)
                    aq_yp[pair] = GP.n

            lnstk.close()

            c_st = CC.n

            @block.gpsimd
            def _(gpsimd):
                gpsimd.wait_ge(GP.h, GP.n)
                CC.inc(gpsimd.collective_compute(
                    "AllGather", ALU.bypass, replica_groups=[list(range(8))],
                    ins=[st_src], outs=[st_dst]), 1)

            snaps2 = sbm("snaps2", [128, NSLOT * 1024], BF16)
            fstk = ExitStack()

            def sbf(name, shape, dt):
                return fstk.enter_context(nc.sbuf_tensor(name, shape, dt))

            P_run = sbf("P_run", [64, 2048], F32)
            snaps = sbf("snaps", [64, NSLOT * 2048], BF16)
            cdb_all = sbf("cdb_all", [64, 64 * 32], F32)
            selS = sbf("selS", [64, 64], F32)
            stf = [sbf("stf0", [64, 2048], BF16), sbf("stf1", [64, 2048], BF16)]

            @block.sync
            def _(sync):
                sync.wait_ge(PE.h, PE.n)
                sync.wait_ge(Aq.h, Aq.n)
                Lq.inc(sync.dma_start(out=selS[:], in_=selb), 16)
                sync.wait_ge(CC.h, c_cd + 1)
                for g in range(64):
                    sg, rg = g // 8, (g % 8 - g // 8) % 8
                    Lq.inc(sync.dma_start(
                        out=cdb_all[:, 32 * g:32 * (g + 1)],
                        in_=cd_dst[(8 * rg + sg):(8 * rg + sg + 1), :]
                        .broadcast_to([64, 32])), 16)
            l_cdb = Lq.n

            @block.vector
            def _(vector):
                DVE.wait(vector)
                vector.wait_ge(PE.h, PE.n)
                vector.wait_ge(Aq.h, Aq.n)
                vector.memset(P_run[:], 0.0)
                DVE.inc(vector.memset(snaps[:], 0.0), 1)
            d_f0 = DVE.n

            g_stf0 = GP.n

            @block.scalar
            def _(scalar):
                scalar.wait_ge(CC.h, c_st + 1)
                for g in range(2):
                    sg, rg = g // 8, (g % 8 - g // 8) % 8
                    Aq.inc(scalar.dma_start(
                        out=stf[g % 2][:],
                        in_=st_dst[64 * rg:64 * rg + 64,
                                   2048 * sg:2048 * (sg + 1)]), 16)
                    GP.inc(scalar.wait_ge(Aq.h, Aq.n), 1)

            @block.vector
            def _(vector):
                vector.wait_ge(Lq.h, l_cdb)
                vector.wait_ge(DVE.h, d_f0)
                for g in range(64):
                    vector.wait_ge(GP.h, g_stf0 + g + 1)
                    last = None
                    for h in range(32):
                        last = vector.scalar_tensor_tensor(
                            snaps[:, (g // 8) * 2048 + 64 * h:
                                  (g // 8) * 2048 + 64 * (h + 1)],
                            P_run[:, 64 * h:64 * (h + 1)],
                            selS[:, g:g + 1],
                            snaps[:, (g // 8) * 2048 + 64 * h:
                                  (g // 8) * 2048 + 64 * (h + 1)],
                            ALU.mult, ALU.add)
                    DVE.inc(last, 1)
                    DVE.wait(vector)
                    last = None
                    for h in range(32):
                        last = vector.scalar_tensor_tensor(
                            P_run[:, 64 * h:64 * (h + 1)],
                            P_run[:, 64 * h:64 * (h + 1)],
                            cdb_all[:, 32 * g + h:32 * g + h + 1],
                            stf[g % 2][:, 64 * h:64 * (h + 1)],
                            ALU.mult, ALU.add)
                    DVE.inc(last, 1)
                    DVE.wait(vector)

            d_fold = DVE.n

            @block.scalar
            def _(scalar):
                for g in range(2, 64):
                    scalar.wait_ge(DVE.h, d_f0 + 2 * (g - 1))
                    sg, rg = g // 8, (g % 8 - g // 8) % 8
                    Aq.inc(scalar.dma_start(
                        out=stf[g % 2][:],
                        in_=st_dst[64 * rg:64 * rg + 64,
                                   2048 * sg:2048 * (sg + 1)]), 16)
                    GP.inc(scalar.wait_ge(Aq.h, Aq.n), 1)

            l_sn2 = Lq.n

            @block.sync
            def _(sync):
                sync.wait_ge(DVE.h, d_fold)
                for par in range(2):
                    Lq.inc(sync.dma_start(
                        out=snaps2[64 * par:64 * (par + 1), :]
                        .rearrange("n (s m c) -> n s m c", s=8, m=16),
                        in_=snaps.rearrange("n (s m c2) -> n s m c2",
                                            s=8, m=16)
                        [:, :, :, 64 * par:64 * (par + 1)]), 16)
                STs.inc(sync.wait_ge(Lq.h, Lq.n), 1)
            l_sn2e = STs.n

            fstk.close()

            yps = [sbm("yps0", [128, 2048], BF16), sbm("yps1", [128, 2048], BF16)]
            svs = [sbm("svs0", [128, 2048], BF16), sbm("svs1", [128, 2048], BF16)]
            worot = [sbm("worot0", [64, 2048], BF16),
                     sbm("worot1", [64, 2048], BF16)]
            tq = [sbm("tq0", [128, 64], BF16), sbm("tq1", [128, 64], BF16)]
            ytmp = [sbm("ytmp0", [64, 128], BF16), sbm("ytmp1", [64, 128], BF16)]

            with ExitStack() as pctx2:
                pyi = pctx2.enter_context(nc.psum_tensor("m_pyi", [128, 64], F32))
                ptb2 = pctx2.enter_context(nc.psum_tensor("m_ptb2", [64, 128], BF16))
                pout = [pctx2.enter_context(
                    nc.psum_tensor(f"m_po{d}", [128, 512], F32))
                    for d in range(4)]

                yps_mark = {}

                @block.scalar
                def _(scalar):
                    scalar.wait_ge(DVE.h, d_fold)
                    scalar.wait_ge(STs.h, l_sn2e)
                    for s in range(2):
                        for h in range(32):
                            Aq.inc(scalar.dma_start(
                                out=yps[s % 2][:, 64 * h:64 * (h + 1)],
                                in_=yp_d[h // 2][:, 128 * s + 64 * (h % 2):
                                                 128 * s + 64 * (h % 2) + 64]),
                                16)
                    GP.inc(scalar.wait_ge(Aq.h, Aq.n), 1)
                yps_mark[0] = GP.n
                yps_mark[1] = GP.n

                svs_mark = {}

                @block.sync
                def _(sync):
                    sync.wait_ge(DVE.h, d_fold)
                    sync.wait_ge(STs.h, l_sn2e)
                    for s in range(2):
                        for h in range(32):
                            Lq.inc(sync.dma_start(
                                out=svs[s % 2][64 * (h % 2):64 * (h % 2) + 64,
                                               (h // 2) * 128:(h // 2 + 1) * 128],
                                in_=svz_d[h // 2][64 * (h % 2):64 * (h % 2) + 64,
                                                  131 * s + 3:131 * s + 131]),
                                16)
                    STs.inc(sync.wait_ge(Lq.h, Lq.n), 1)
                svs_mark[0] = STs.n
                svs_mark[1] = STs.n

                gw0 = [GS[0].n, GS[1].n]
                gw_cnt = {0: 0, 1: 0}
                gw_need = {}

                @block.gpsimd
                def _(gpsimd):
                    gpsimd.wait_ge(DVE.h, d_fold)
                    gpsimd.wait_ge(STs.h, l_sn2e)
                    for h in range(2):
                        GS[h % 2].inc(gpsimd.dma_start(
                            out=worot[h % 2][:],
                            in_=WoutT[64 * h:64 * (h + 1), :]), 16)
                for h in range(2):
                    gw_cnt[h % 2] += 1
                    gw_need[h] = gw0[h % 2] + 16 * gw_cnt[h % 2]

                for s in range(NSLOT):
                    for h in range(32):
                        it = s * 32 + h
                        rot = h % 2
                        bp = 64 * (h % 2)

                        p_i = PE.n
                        d_i = DVE.n

                        @block.tensor
                        def _(tensor):
                            if it == 0:
                                tensor.wait_ge(DVE.h, d_fold)
                            tensor.wait_ge(DVE.h, d_i)
                            if it == 0:
                                tensor.wait_ge(STs.h, l_sn2e)
                            PE.inc(tensor.matmul(
                                pyi[:],
                                cf[bp:bp + 64, (h // 2) * 1024 + 128 * s:
                                   (h // 2) * 1024 + 128 * (s + 1)],
                                snaps2[bp:bp + 64, s * 1024 + (h // 2) * 64:
                                       s * 1024 + (h // 2) * 64 + 64],
                                start=True, stop=True), 1)

                        @block.vector
                        def _(vector):
                            vector.wait_ge(PE.h, p_i + 1)
                            vector.wait_ge(GP.h, yps_mark[s])
                            DVE.inc(vector.scalar_tensor_tensor(
                                tq[rot][:], pyi[:],
                                ecum[:, 32 * s + h:32 * s + h + 1],
                                yps[s % 2][:, 64 * h:64 * (h + 1)],
                                ALU.mult, ALU.add), 1)

                        @block.tensor
                        def _(tensor):
                            tensor.wait_ge(DVE.h, d_i + 1)
                            PE.inc(tensor.transpose(ptb2[:], tq[rot][:],
                                                    idb[:]), 1)

                        @block.vector
                        def _(vector):
                            vector.wait_ge(PE.h, p_i + 2)
                            vector.wait_ge(STs.h, svs_mark[s])
                            DVE.inc(vector.tensor_mul(
                                ytmp[rot][:], ptb2[:],
                                svs[s % 2][bp:bp + 64,
                                           (h // 2) * 128:(h // 2 + 1) * 128]), 1)

                        @block.tensor
                        def _(tensor):
                            tensor.wait_ge(DVE.h, d_i + 2)
                            tensor.wait_ge(GS[rot].h, gw_need[it])
                            last = None
                            for dd in range(4):
                                last = tensor.matmul(
                                    pout[dd][:], ytmp[rot][:],
                                    worot[rot][:, 512 * dd:512 * (dd + 1)],
                                    start=(h == 0), stop=(h == 31))
                            PE.inc(last, 1)

                        @block.gpsimd
                        def _(gpsimd):
                            if it + 2 < 256:
                                gpsimd.wait_ge(PE.h, p_i + 3)
                                hh = (it + 2) % 32
                                GS[rot].inc(gpsimd.dma_start(
                                    out=worot[rot][:],
                                    in_=WoutT[64 * hh:64 * (hh + 1), :]), 16)
                        if it + 2 < 256:
                            gw_cnt[rot] += 1
                            gw_need[it + 2] = gw0[rot] + 16 * gw_cnt[rot]

                    p_s_end = PE.n

                    @block.vector
                    def _(vector):
                        vector.wait_ge(PE.h, p_s_end)
                        last = None
                        for dd in range(4):
                            last = vector.tensor_add(
                                xr[:, s * 2048 + 512 * dd:
                                   s * 2048 + 512 * (dd + 1)],
                                pout[dd][:],
                                xr[:, s * 2048 + 512 * dd:
                                   s * 2048 + 512 * (dd + 1)])
                        E2.inc(last, 1)

                    @block.scalar
                    def _(scalar):
                        if s + 2 < NSLOT:
                            scalar.wait_ge(DVE.h, DVE.n)
                            for h in range(32):
                                Aq.inc(scalar.dma_start(
                                    out=yps[s % 2][:, 64 * h:64 * (h + 1)],
                                    in_=yp_d[h // 2][:,
                                                     128 * (s + 2) + 64 * (h % 2):
                                                     128 * (s + 2) + 64 * (h % 2) + 64]),
                                    16)
                            GP.inc(scalar.wait_ge(Aq.h, Aq.n), 1)
                    if s + 2 < NSLOT:
                        yps_mark[s + 2] = GP.n

                    @block.sync
                    def _(sync):
                        if s + 2 < NSLOT:
                            sync.wait_ge(DVE.h, DVE.n)
                            for h in range(32):
                                Lq.inc(sync.dma_start(
                                    out=svs[s % 2][64 * (h % 2):64 * (h % 2) + 64,
                                                   (h // 2) * 128:(h // 2 + 1) * 128],
                                    in_=svz_d[h // 2][64 * (h % 2):
                                                      64 * (h % 2) + 64,
                                                      131 * (s + 2) + 3:
                                                      131 * (s + 2) + 131]),
                                    16)
                            STs.inc(sync.wait_ge(Lq.h, Lq.n), 1)
                    if s + 2 < NSLOT:
                        svs_mark[s + 2] = STs.n

            mstk.close()

        def emit_ffn1(idx):
            def src_tile(j):
                return xr[:, j * D:(j + 1) * D]

            jE, jD, jA, jP = E2.n, DVE.n, ACTs.n, PE.n

            @block.sync
            def _(sync):
                sync.wait_ge(E2.h, jE)

            @block.scalar
            def _(scalar):
                scalar.wait_ge(E2.h, jE)
                scalar.wait_ge(PE.h, jP)

            @block.vector
            def _(vector):
                vector.wait_ge(E2.h, jE)
                vector.wait_ge(ACTs.h, jA)

            @block.tensor
            def _(tensor):
                tensor.wait_ge(E2.h, jE)
                tensor.wait_ge(DVE.h, jD)

            @block.gpsimd
            def _(gpsimd):
                gpsimd.wait_ge(E2.h, jE)
                gpsimd.wait_ge(DVE.h, jD)

            emit_ln_stats(src_tile, 8, absmax=False)

            fstk1 = ExitStack()

            def sbq(name, shape, dt):
                return fstk1.enter_context(nc.sbuf_tensor(name, shape, dt))

            xT = sbq(f"q{idx}xT", [128, 16 * 1024], BF16)
            dacc = sbq(f"q{idx}dacc", [128, 16 * 1024], F32)
            gsT = sbq(f"q{idx}gsT", [128, 64], F32)
            usT = sbq(f"q{idx}usT", [128, 64], F32)
            dsT = sbq(f"q{idx}dsT", [128, 16], F32)
            fu = scr[:, 512:1024]
            fsb = [scr[:, 1024:1536], scr[:, 1536:2048]]
            lstk1 = ExitStack()
            lnb1 = [lstk1.enter_context(nc.sbuf_tensor(f"q{idx}lnb0",
                                                       [128, D], BF16)),
                    lstk1.enter_context(nc.sbuf_tensor(f"q{idx}lnb1",
                                                       [128, D], BF16))]

            @block.vector
            def _(vector):
                DVE.wait(vector)
                DVE.inc(vector.tensor_mul(sc_sb[:, 0:8], stat_nm[:, 0:8],
                                          stat_rs[:, 0:8]), 1)
            d_nmr1 = DVE.n

            l_sc = Lq.n

            @block.sync
            def _(sync):
                sync.wait_ge(E2.h, jE)
                Lq.inc(sync.dma_start(out=scr[0:64, 0:128], in_=gs_in), 16)
                Lq.inc(sync.dma_start(out=scr[0:64, 128:256], in_=us_in), 16)
                Lq.inc(sync.dma_start(out=scr[0:16, 256:384], in_=ds_in), 16)
                STs.inc(sync.wait_ge(Lq.h, Lq.n), 1)
            m_sc = STs.n

            p_sc = PE.n
            with nc.psum_tensor(f"q{idx}psc", [128, 128], F32) as psc:
                @block.tensor
                def _(tensor):
                    tensor.wait_ge(STs.h, m_sc)
                    PE.inc(tensor.transpose(psc[:, 0:64], scr[0:64, 0:128],
                                            idf[0:64, 0:64]), 1)

                @block.vector
                def _(vector):
                    vector.wait_ge(PE.h, p_sc + 1)
                    DVE.inc(vector.tensor_copy(gsT[:], psc[:, 0:64]), 1)

                @block.tensor
                def _(tensor):
                    tensor.wait_ge(DVE.h, DVE.n)
                    PE.inc(tensor.transpose(psc[:, 0:64], scr[0:64, 128:256],
                                            idf[0:64, 0:64]), 1)

                @block.vector
                def _(vector):
                    vector.wait_ge(PE.h, p_sc + 2)
                    DVE.inc(vector.tensor_copy(usT[:], psc[:, 0:64]), 1)

                @block.tensor
                def _(tensor):
                    tensor.wait_ge(DVE.h, DVE.n)
                    PE.inc(tensor.transpose(psc[:, 0:16], scr[0:16, 256:384],
                                            idf[0:16, 0:16]), 1)

                @block.vector
                def _(vector):
                    vector.wait_ge(PE.h, p_sc + 3)
                    DVE.inc(vector.tensor_copy(dsT[:], psc[:, 0:16]), 1)
                    DVE.wait(vector)
                    DVE.inc(vector.tensor_scalar_mul(dsT[:], dsT[:], 0.5), 1)

            a_l1 = ACTs.n
            p_l1 = PE.n
            e_l1 = E2.n

            @block.scalar
            def _(scalar):
                scalar.wait_ge(DVE.h, d_nmr1)
                for j in range(8):
                    if j >= 2:
                        scalar.wait_ge(PE.h, p_l1 + 16 * (j - 1))
                    ACTs.inc(scalar.activation(out=lnb1[j % 2][:],
                                               in_=src_tile(j),
                                               func=AF.Identity,
                                               bias=sc_sb[:, j:j + 1],
                                               scale=stat_rs[:, j:j + 1]), 1)

            with nc.psum_tensor(f"q{idx}pt0", [128, 128], BF16) as pq0, \
                 nc.psum_tensor(f"q{idx}pt1", [128, 128], BF16) as pq1:
                pstq = [pq0, pq1]

                @block.tensor
                def _(tensor):
                    for i in range(8 * 16):
                        j, k = divmod(i, 16)
                        if k == 0:
                            tensor.wait_ge(ACTs.h, a_l1 + j + 1)
                        if i >= 2:
                            tensor.wait_ge(E2.h, e_l1 + i - 1)
                        PE.inc(tensor.transpose(
                            pstq[i % 2][:],
                            lnb1[j % 2][:, k * 128:(k + 1) * 128], idb[:]), 1)

                @block.vector
                def _(vector):
                    for j in range(8):
                        for k in range(16):
                            i = 16 * j + k
                            vector.wait_ge(PE.h, p_l1 + i + 1)
                            E2.inc(vector.tensor_copy(
                                xT[:, k * 1024 + j * 128:
                                   k * 1024 + (j + 1) * 128],
                                pstq[i % 2][:]), 1)
            xT_ready = E2.n
            lstk1.close()
            wg_sb = [sbq(f"q{idx}wg0", [128, 2048], BF16),
                     sbq(f"q{idx}wg1", [128, 2048], BF16)]
            wu_sb = [sbq(f"q{idx}wu0", [128, 2048], BF16),
                     sbq(f"q{idx}wu1", [128, 2048], BF16)]
            dw_sb = [sbq(f"q{idx}dw{i}", [128, 2048], BF16) for i in range(2)]
            h_sb = [sbq(f"q{idx}h{i}", [128, 1024], BF16) for i in range(2)]

            with ExitStack() as qctx:
                pdn = [qctx.enter_context(nc.psum_tensor(f"q{idx}pdn{i}",
                                                         [128, 512], F32))
                       for i in range(4)]
                qctx1 = ExitStack()
                pg = [qctx1.enter_context(nc.psum_tensor(f"q{idx}pg{i}",
                                                         [128, 512], F32))
                      for i in range(2)]
                pu = [qctx1.enter_context(nc.psum_tensor(f"q{idx}pu{i}",
                                                         [128, 512], F32))
                      for i in range(2)]

                gq0 = [GS[0].n, GS[1].n]
                gq_cnt = {0: 0, 1: 0}
                dw_need = {}
                fold_mark = {}
                wb_mark = {}
                fs_free = {}
                pe_i_end = {}
                pe_gu_end = {}
                for isup in range(32):
                    for i2 in range(2):
                        i = 2 * isup + i2
                        par = i % 2
                        gq_cnt[par] += 1
                        nper = 3 if idx == 0 else 1
                        dw_need[i] = gq0[par] + 16 * nper * gq_cnt[par]

                        if idx == 0:
                            @block.gpsimd
                            def _(gpsimd):
                                if i >= 2:
                                    gpsimd.wait_ge(PE.h, pe_gu_end[i - 2])
                                    gpsimd.wait_ge(STs.h, wb_mark[i - 2])
                                else:
                                    gpsimd.wait_ge(E2.h, xT_ready)
                                GS[par].inc(gpsimd.dma_start(
                                    out=wg_sb[par][:],
                                    in_=WgT[128 * i:128 * (i + 1), :]), 16)
                                GS[par].inc(gpsimd.dma_start(
                                    out=wu_sb[par][:],
                                    in_=WuT[128 * i:128 * (i + 1), :]), 16)
                                if i >= 2:
                                    gpsimd.wait_ge(PE.h, pe_i_end[i - 2])
                                GS[par].inc(gpsimd.dma_start(
                                    out=dw_sb[i2][:],
                                    in_=dwT[128 * i:128 * (i + 1), :]), 16)

                            @block.vector
                            def _(vector):
                                vector.wait_ge(GS[par].h, dw_need[i])
                                vector.tensor_scalar(
                                    wg_sb[par][:], wg_sb[par][:],
                                    -128.0, None, ALU.add)
                                vector.tensor_scalar(
                                    wu_sb[par][:], wu_sb[par][:],
                                    -128.0, None, ALU.add)
                                DVE.inc(vector.tensor_scalar(
                                    dw_sb[i2][:], dw_sb[i2][:],
                                    -128.0, None, ALU.add), 1)
                            fold_mark[i] = DVE.n

                            @block.sync
                            def _(sync):
                                sync.wait_ge(DVE.h, fold_mark[i])
                                Lq.inc(sync.dma_start(
                                    out=Wg_c[128 * i:128 * (i + 1), :],
                                    in_=wg_sb[par][:]), 16)
                                Lq.inc(sync.dma_start(
                                    out=Wu_c[128 * i:128 * (i + 1), :],
                                    in_=wu_sb[par][:]), 16)
                                Lq.inc(sync.dma_start(
                                    out=dw_c[128 * i:128 * (i + 1), :],
                                    in_=dw_sb[i2][:]), 16)
                                STs.inc(sync.wait_ge(Lq.h, Lq.n), 1)
                            wb_mark[i] = STs.n
                        else:
                            @block.sync
                            def _(sync):
                                if i >= 2:
                                    sync.wait_ge(PE.h, pe_gu_end[i - 2])
                                else:
                                    sync.wait_ge(E2.h, xT_ready)
                                Lq.inc(sync.dma_start(
                                    out=wg_sb[par][:],
                                    in_=Wg_c[128 * i:128 * (i + 1), :]), 16)
                                Lq.inc(sync.dma_start(
                                    out=wu_sb[par][:],
                                    in_=Wu_c[128 * i:128 * (i + 1), :]), 16)
                                STs.inc(sync.wait_ge(Lq.h, Lq.n), 1)
                            wb_mark[i] = STs.n

                            @block.gpsimd
                            def _(gpsimd):
                                if i >= 2:
                                    gpsimd.wait_ge(PE.h, pe_i_end[i - 2])
                                GS[par].inc(gpsimd.dma_start(
                                    out=dw_sb[i2][:],
                                    in_=dw_c[128 * i:128 * (i + 1), :]), 16)

                        p_gu = PE.n

                        @block.tensor
                        def _(tensor):
                            if i == 0:
                                tensor.wait_ge(E2.h, xT_ready)
                            if idx == 0:
                                tensor.wait_ge(DVE.h, fold_mark[i])
                            else:
                                tensor.wait_ge(STs.h, wb_mark[i])
                            tensor.wait_ge(DVE.h, DVE.n)
                            tensor.wait_ge(ACTs.h, ACTs.n)
                            last = None
                            for k in range(16):
                                for qh in range(2):
                                    tensor.matmul(
                                        pg[qh][:],
                                        wg_sb[par][:, 128 * k:128 * (k + 1)],
                                        xT[:, 1024 * k + 512 * qh:
                                           1024 * k + 512 * (qh + 1)],
                                        start=(k == 0), stop=(k == 15))
                                    last = tensor.matmul(
                                        pu[qh][:],
                                        wu_sb[par][:, 128 * k:128 * (k + 1)],
                                        xT[:, 1024 * k + 512 * qh:
                                           1024 * k + 512 * (qh + 1)],
                                        start=(k == 0), stop=(k == 15))
                            PE.inc(last, 1)
                        pe_gu_end[i] = PE.n

                        d_h = DVE.n
                        a_h = ACTs.n

                        @block.scalar
                        def _(scalar):
                            scalar.wait_ge(DVE.h, fs_free.get(i - 1, 0))
                            scalar.wait_ge(PE.h, p_gu + 1)
                            for qh in range(2):
                                ACTs.inc(scalar.activation(
                                    out=fsb[qh], in_=pg[qh][:], func=AF.Silu,
                                    scale=gsT[:, i:i + 1]), 1)

                        @block.vector
                        def _(vector):
                            vector.wait_ge(PE.h, p_gu + 1)
                            if i >= 2:
                                vector.wait_ge(PE.h, pe_i_end[i - 2])
                            for qh in range(2):
                                DVE.inc(vector.tensor_scalar(
                                    fu[:], pu[qh][:], usT[:, i:i + 1],
                                    None, ALU.mult), 1)
                                vector.wait_ge(ACTs.h, a_h + qh + 1)
                                DVE.wait(vector)
                                DVE.inc(vector.tensor_mul(
                                    h_sb[i2][:, 512 * qh:512 * (qh + 1)],
                                    fsb[qh], fu[:]), 1)
                        fs_free[i] = DVE.n

                        pe_i_end[i] = None

                    d_dn = DVE.n

                    dacc_mark = getattr(emit_ffn1, "_dm", None)
                    if dacc_mark is None or isup == 0:
                        dacc_mark = {}
                        emit_ffn1._dm = dacc_mark
                    for dd in range(16):
                        for qh in range(2):
                            p_dd = PE.n
                            bank = (dd * 2 + qh) % 4

                            @block.tensor
                            def _(tensor):
                                if (dd, qh) == (0, 0):
                                    tensor.wait_ge(DVE.h, d_dn)
                                    tensor.wait_ge(GS[0].h, dw_need[2 * isup])
                                    tensor.wait_ge(GS[1].h,
                                                   dw_need[2 * isup + 1])
                                tensor.wait_ge(DVE.h,
                                               dacc_mark.get(bank, 0))
                                last = None
                                for i2 in range(2):
                                    last = tensor.matmul(
                                        pdn[bank][:],
                                        dw_sb[i2][:, 128 * dd:128 * (dd + 1)],
                                        h_sb[i2][:, 512 * qh:512 * (qh + 1)],
                                        start=(i2 == 0), stop=(i2 == 1))
                                PE.inc(last, 1)

                            @block.vector
                            def _(vector):
                                vector.wait_ge(PE.h, p_dd + 1)
                                dst = dacc[:, 1024 * dd + 512 * qh:
                                           1024 * dd + 512 * (qh + 1)]
                                if isup == 0:
                                    DVE.inc(vector.tensor_copy(
                                        dst, pdn[bank][:]), 1)
                                else:
                                    DVE.inc(vector.tensor_add(
                                        dst, pdn[bank][:], dst), 1)
                            dacc_mark[bank] = DVE.n

                    for i2 in range(2):
                        pe_i_end[2 * isup + i2] = PE.n

                qctx1.close()

                with nc.psum_tensor(f"q{idx}pe0", [128, 128], F32) as pe0, \
                     nc.psum_tensor(f"q{idx}pe1", [128, 128], F32) as pe1:
                    peb = [pe0, pe1]
                    for dd in range(16):
                        d_t = DVE.n

                        @block.vector
                        def _(vector):
                            if dd >= 1:
                                vector.wait_ge(PE.h, PE.n)
                            DVE.inc(vector.tensor_scalar(
                                dacc[:, 1024 * dd:1024 * (dd + 1)],
                                dacc[:, 1024 * dd:1024 * (dd + 1)],
                                dsT[:, dd:dd + 1], None, ALU.mult), 1)

                        p_tb = PE.n
                        e_tb = E2.n

                        @block.tensor
                        def _(tensor):
                            for j in range(8):
                                tensor.wait_ge(DVE.h, d_t + 1)
                                if j >= 2:
                                    tensor.wait_ge(E2.h, e_tb + j - 1)
                                PE.inc(tensor.transpose(
                                    peb[j % 2][:],
                                    dacc[:, 1024 * dd + 128 * j:
                                         1024 * dd + 128 * (j + 1)],
                                    idf[:]), 1)

                        @block.vector
                        def _(vector):
                            for j in range(8):
                                vector.wait_ge(PE.h, p_tb + j + 1)
                                E2.inc(vector.tensor_add(
                                    xr[:, 2048 * j + 128 * dd:
                                       2048 * j + 128 * (dd + 1)],
                                    peb[j % 2][:],
                                    xr[:, 2048 * j + 128 * dd:
                                       2048 * j + 128 * (dd + 1)]), 1)

            fstk1.close()

        def emit_attn():
            def src_tile(j):
                return xr[:, j * D:(j + 1) * D]

            jE, jD, jA, jP = E2.n, DVE.n, ACTs.n, PE.n

            @block.sync
            def _(sync):
                sync.wait_ge(E2.h, jE)

            @block.scalar
            def _(scalar):
                scalar.wait_ge(E2.h, jE)
                scalar.wait_ge(PE.h, jP)

            @block.vector
            def _(vector):
                vector.wait_ge(E2.h, jE)
                vector.wait_ge(ACTs.h, jA)

            @block.tensor
            def _(tensor):
                tensor.wait_ge(E2.h, jE)
                tensor.wait_ge(DVE.h, jD)

            @block.gpsimd
            def _(gpsimd):
                gpsimd.wait_ge(E2.h, jE)
                gpsimd.wait_ge(DVE.h, jD)

            emit_ln_stats(src_tile, 8, absmax=False)

            astk = ExitStack()

            def sba(name, shape, dt):
                return astk.enter_context(nc.sbuf_tensor(name, shape, dt))

            qT = sba("a_qT", [128, 16 * 1024], BF16)
            gain_bc = sba("a_gain", [128, NH_A], F32)
            ngain = sba("a_ngain", [128, NH_A], F32)
            rsq = sba("a_rsq", [128, 2], F32)
            tstage = [sba(f"a_tst{i}", [128, 128], BF16) for i in range(4)]
            vstg = [sba(f"a_vst{i}", [128, 128], BF16) for i in range(4)]

            pstk = ExitStack()

            def sbp(name, shape, dt):
                return pstk.enter_context(nc.sbuf_tensor(name, shape, dt))

            xT = sbp("a_xT", [128, 16 * 1024], BF16)
            ropeC = sbp("a_ropeC", [128, 512], F32)
            ropeS = sbp("a_ropeS", [128, 512], F32)
            qsc = sbp("a_qsc", [128, 128], F32)
            rp1 = sbp("a_rp1", [128, 128], F32)
            rp2 = sbp("a_rp2", [128, 128], F32)
            nstage = [sbp("a_nst0", [128, 128], BF16),
                      sbp("a_nst1", [128, 128], BF16)]
            lnb2 = [sbp("a_lnb0", [128, D], BF16), sbp("a_lnb1", [128, D], BF16)]
            wq_og = [sbp("a_wq0", [128, 16 * 512], BF16),
                     sbp("a_wq1", [128, 16 * 512], BF16)]

            @block.sync
            def _(sync):
                sync.wait_ge(E2.h, jE)
                for j in range(8):
                    Lq.inc(sync.dma_start(
                        out=ropeC[:, 64 * j:64 * (j + 1)],
                        in_=rope_cos[128 * j:128 * (j + 1), :]), 16)
                    Lq.inc(sync.dma_start(
                        out=ropeS[:, 64 * j:64 * (j + 1)],
                        in_=rope_sin[128 * j:128 * (j + 1), :]), 16)
                Lq.inc(sync.dma_start(out=gain_bc[:],
                                      in_=grow.broadcast_to([128, NH_A])), 16)
                STs.inc(sync.wait_ge(Lq.h, Lq.n), 1)
            m_ac = STs.n

            @block.vector
            def _(vector):
                vector.wait_ge(STs.h, m_ac)
                DVE.wait(vector)
                DVE.inc(vector.tensor_scalar_mul(ngain[:], gain_bc[:], -1.0), 1)
                DVE.wait(vector)
                DVE.inc(vector.tensor_mul(sc_sb[:, 0:8], stat_nm[:, 0:8],
                                          stat_rs[:, 0:8]), 1)
            d_ng = DVE.n

            a_l2 = ACTs.n
            p_l2 = PE.n
            e_l2 = E2.n

            @block.scalar
            def _(scalar):
                scalar.wait_ge(DVE.h, d_ng)
                for j in range(8):
                    if j >= 2:
                        scalar.wait_ge(PE.h, p_l2 + 16 * (j - 1))
                    ACTs.inc(scalar.activation(out=lnb2[j % 2][:],
                                               in_=src_tile(j),
                                               func=AF.Identity,
                                               bias=sc_sb[:, j:j + 1],
                                               scale=stat_rs[:, j:j + 1]), 1)

            with nc.psum_tensor("a_pt0", [128, 128], BF16) as pa0, \
                 nc.psum_tensor("a_pt1", [128, 128], BF16) as pa1:
                psta = [pa0, pa1]

                @block.tensor
                def _(tensor):
                    for i in range(8 * 16):
                        j, k = divmod(i, 16)
                        if k == 0:
                            tensor.wait_ge(ACTs.h, a_l2 + j + 1)
                        if i >= 2:
                            tensor.wait_ge(E2.h, e_l2 + i - 1)
                        PE.inc(tensor.transpose(
                            psta[i % 2][:],
                            lnb2[j % 2][:, k * 128:(k + 1) * 128], idb[:]), 1)

                @block.vector
                def _(vector):
                    for i in range(8 * 16):
                        j, k = divmod(i, 16)
                        vector.wait_ge(PE.h, p_l2 + i + 1)
                        E2.inc(vector.tensor_copy(
                            xT[:, k * 1024 + j * 128:k * 1024 + (j + 1) * 128],
                            psta[i % 2][:]), 1)
            xT_ready = E2.n

            ga0 = [GS[0].n, GS[1].n]
            ga_cnt = {0: 0, 1: 0}
            pe_og_end = {}
            kv_mark = {}
            c_att = CC.n
            k_cnt = [0]
            v_cnt = [0]
            k_marks = {}
            v_marks = {}

            with ExitStack() as actx:
                pa = [actx.enter_context(nc.psum_tensor(f"a_pa{i}",
                                                        [128, 512], F32))
                      for i in range(4)]
                ptt = [actx.enter_context(nc.psum_tensor(f"a_ptt{i}",
                                                         [128, 128], BF16))
                       for i in range(2)]

                for og in range(12):
                    par = og % 2
                    ga_cnt[par] += 1
                    ga_need = ga0[par] + 16 * ga_cnt[par]

                    @block.gpsimd
                    def _(gpsimd):
                        if og >= 2:
                            gpsimd.wait_ge(PE.h, pe_og_end[og - 2])
                        else:
                            gpsimd.wait_ge(E2.h, xT_ready)
                        GS[par].inc(gpsimd.dma_start(
                            out=wq_og[par][:],
                            in_=wqkv[128 * og:128 * (og + 1), :]), 16)

                    for jg in range(2):
                        p_qk = PE.n

                        @block.tensor
                        def _(tensor):
                            tensor.wait_ge(GS[par].h, ga_need)
                            tensor.wait_ge(DVE.h, DVE.n)
                            tensor.wait_ge(ACTs.h, ACTs.n)
                            last = None
                            for j4 in range(4):
                                j = 4 * jg + j4
                                for k in range(16):
                                    last = tensor.matmul(
                                        pa[j4][:],
                                        xT[:, k * 1024 + 128 * j:
                                           k * 1024 + 128 * (j + 1)],
                                        wq_og[par][:, 512 * k:512 * (k + 1)],
                                        start=(k == 0), stop=(k == 15))
                            PE.inc(last, 1)

                        for j4 in range(4):
                            j = 4 * jg + j4
                            for hh in range(4):
                                h = (og % 4) * 4 + hh
                                kind = og // 4
                                chunk = pa[j4][:, 128 * hh:128 * (hh + 1)]
                                it = ((og * 2 + jg) * 4 + j4) * 4 + hh
                                rot = it % 2

                                if kind == 2:
                                    nv = v_cnt[0]
                                    v_cnt[0] += 1
                                    rv = nv % 4
                                    d_v = DVE.n

                                    @block.vector
                                    def _(vector):
                                        vector.wait_ge(PE.h, p_qk + 1)
                                        if nv >= 4:
                                            vector.wait_ge(STs.h,
                                                           v_marks[nv - 4])
                                        DVE.wait(vector)
                                        DVE.inc(vector.tensor_copy(
                                            vstg[rv][:], chunk), 1)

                                    @block.sync
                                    def _(sync):
                                        sync.wait_ge(DVE.h, d_v + 1)
                                        Lq.inc(sync.dma_start(
                                            out=v_src[h // 8]
                                            [:, ((h % 8) * 8 + j) * 128:
                                             ((h % 8) * 8 + j) * 128 + 128],
                                            in_=vstg[rv][:]), 16)
                                        STs.inc(sync.wait_ge(Lq.h, Lq.n), 1)
                                    v_marks[nv] = STs.n
                                    continue

                                d_n = DVE.n
                                a_n = ACTs.n

                                @block.scalar
                                def _(scalar):
                                    scalar.wait_ge(PE.h, p_qk + 1)
                                    scalar.wait_ge(DVE.h, d_n)
                                    ACTs.inc(scalar.activation(
                                        out=qsc[:], in_=chunk, func=AF.Square,
                                        accum_out=rsq[:, 0:1]), 1)
                                    ACTs.wait(scalar)
                                    ACTs.inc(scalar.activation(
                                        out=rsq[:, 1:2], in_=rsq[:, 0:1],
                                        func=AF.Sqrt, bias=cst[:, 2:3],
                                        scale=1.0), 1)

                                @block.vector
                                def _(vector):
                                    vector.wait_ge(ACTs.h, a_n + 2)
                                    DVE.inc(vector.reciprocal(rsq[:, 0:1],
                                                              rsq[:, 1:2]), 1)
                                    DVE.wait(vector)
                                    if h % 2 == 0:
                                        vector.tensor_scalar(
                                            rp1[:, 0:64], chunk[:, 0:64],
                                            rsq[:, 0:1], None, ALU.mult)
                                        DVE.inc(vector.tensor_scalar(
                                            rp1[:, 64:128], chunk[:, 64:128],
                                            rsq[:, 0:1], None, ALU.mult), 1)
                                        DVE.wait(vector)
                                        cj = ropeC[:, 64 * j:64 * (j + 1)]
                                        sj = ropeS[:, 64 * j:64 * (j + 1)]
                                        vector.tensor_mul(rp2[:, 0:64],
                                                          rp1[:, 0:64], cj)
                                        vector.tensor_mul(rp2[:, 64:128],
                                                          rp1[:, 64:128], sj)
                                        vector.tensor_mul(qsc[:, 0:64],
                                                          rp1[:, 0:64], sj)
                                        DVE.inc(vector.tensor_mul(
                                            qsc[:, 64:128],
                                            rp1[:, 64:128], cj), 1)
                                        DVE.wait(vector)
                                        vector.tensor_sub(
                                            nstage[rot][:, 0:64],
                                            rp2[:, 0:64], rp2[:, 64:128])
                                        DVE.inc(vector.tensor_add(
                                            nstage[rot][:, 64:128],
                                            qsc[:, 0:64], qsc[:, 64:128]), 1)
                                    else:
                                        DVE.inc(vector.tensor_scalar(
                                            nstage[rot][:], chunk,
                                            rsq[:, 0:1], None, ALU.mult), 1)

                                p_n = PE.n

                                @block.tensor
                                def _(tensor):
                                    tensor.wait_ge(DVE.h, DVE.n)
                                    PE.inc(tensor.transpose(
                                        ptt[rot][:], nstage[rot][:],
                                        idb[:]), 1)

                                if kind == 0:
                                    @block.vector
                                    def _(vector):
                                        vector.wait_ge(PE.h, p_n + 1)
                                        E2.inc(vector.tensor_copy(
                                            qT[:, h * 1024 + 128 * j:
                                               h * 1024 + 128 * (j + 1)],
                                            ptt[rot][:]), 1)
                                else:
                                    nk = k_cnt[0]
                                    k_cnt[0] += 1
                                    rk = nk % 4
                                    d_k = DVE.n

                                    @block.vector
                                    def _(vector):
                                        vector.wait_ge(PE.h, p_n + 1)
                                        if nk >= 4:
                                            vector.wait_ge(STs.h,
                                                           k_marks[nk - 4])
                                        DVE.inc(vector.tensor_copy(
                                            tstage[rk][:], ptt[rot][:]), 1)

                                    @block.sync
                                    def _(sync):
                                        sync.wait_ge(DVE.h, d_k + 1)
                                        Lq.inc(sync.dma_start(
                                            out=kt_src[h // 8]
                                            [128 * (h % 8):
                                             128 * (h % 8) + 128,
                                             128 * j:128 * (j + 1)],
                                            in_=tstage[rk][:]), 16)
                                        STs.inc(sync.wait_ge(Lq.h, Lq.n), 1)
                                    k_marks[nk] = STs.n

                    pe_og_end[og] = PE.n
                    kv_mark[og] = STs.n

                    if og in (5, 7, 9, 11):
                        @block.gpsimd
                        def _(gpsimd):
                            gpsimd.wait_ge(STs.h, kv_mark[og])
                            if og == 5:
                                CC.inc(gpsimd.collective_compute(
                                    "AllGather", ALU.bypass,
                                    replica_groups=[list(range(8))],
                                    ins=[kt_src[0]], outs=[kt_dst[0]]), 1)
                            elif og == 7:
                                CC.inc(gpsimd.collective_compute(
                                    "AllGather", ALU.bypass,
                                    replica_groups=[list(range(8))],
                                    ins=[kt_src[1]], outs=[kt_dst[1]]), 1)
                            elif og == 9:
                                CC.inc(gpsimd.collective_compute(
                                    "AllGather", ALU.bypass,
                                    replica_groups=[list(range(8))],
                                    ins=[v_src[0]], outs=[v_dst[0]]), 1)
                            else:
                                CC.inc(gpsimd.collective_compute(
                                    "AllGather", ALU.bypass,
                                    replica_groups=[list(range(8))],
                                    ins=[v_src[1]], outs=[v_dst[1]]), 1)

            pe_all = PE.n
            e_qT = E2.n

            pstk.close()
            cstk = ExitStack()

            def sbc(name, shape, dt):
                return cstk.enter_context(nc.sbuf_tensor(name, shape, dt))

            ktb = [sbc(f"a_ktb{i}", [128, 1024], BF16) for i in range(2)]
            vtb = [sbc(f"a_vtb{i}", [128, 1024], BF16) for i in range(2)]
            mask_rb = [sbc(f"a_mrb{i}", [128, 1024], BF16) for i in range(2)]
            pbuf = [sbc(f"a_pbuf{i}", [128, 1024], BF16) for i in range(2)]
            dinv_bc = sbc("a_dbc", [128, 1024], F32)
            wo_sb = sbc("a_wo", [128, 16 * 2048], BF16)
            dinv = scr[0:1, 0:1024]

            e_head = {}
            g_wo0 = Gq.n

            with ExitStack() as cctx:
                pSd = [cctx.enter_context(nc.psum_tensor(f"a_pS{i}",
                                                         [128, 1024], F32))
                       for i in range(2)]
                pO = [cctx.enter_context(nc.psum_tensor(f"a_pO{i}",
                                                        [128, 512], F32))
                      for i in range(2)]
                pden = [cctx.enter_context(nc.psum_tensor(f"a_pd{i}",
                                                          [1, 512], F32))
                        for i in range(2)]

                pe_r_done = {}
                a_exp = {}
                d_msk = {}
                p_S = {}
                m_kt = {}
                d_bc = [0]
                for h in range(NH_A):
                    need_cc = c_att + (3 if h < 8 else 4)

                    def emit_loads(r, h=h, need_cc=need_cc):
                        t_i = 8 * h + r
                        rot = r % 2
                        pe_pref = pe_r_done.get(t_i - 2, pe_all)
                        m_kt[t_i] = STs.n + 1

                        @block.sync
                        def _(sync):
                            sync.wait_ge(CC.h, need_cc)
                            sync.wait_ge(PE.h, pe_pref)
                            Lq.inc(sync.dma_start(
                                out=ktb[rot][:],
                                in_=kt_dst[h // 8]
                                [1024 * r + 128 * (h % 8):
                                 1024 * r + 128 * (h % 8) + 128, :]), 16)
                            Lq.inc(sync.dma_start(
                                out=vtb[rot][:],
                                in_=v_dst[h // 8]
                                [128 * r:128 * (r + 1),
                                 1024 * (h % 8):1024 * (h % 8) + 1024]), 16)
                            Lq.inc(sync.dma_start(
                                out=mask_rb[rot][:], in_=masks[r]), 16)
                            STs.inc(sync.wait_ge(Lq.h, Lq.n), 1)

                    def emit_pv(u, h=h):
                        r, ss = divmod(u, 8)
                        rot = r % 2
                        sr = u % 2
                        lo = 128 * ss
                        gu = 64 * h + u

                        @block.tensor
                        def _(tensor):
                            tensor.wait_ge(DVE.h, d_msk[gu])
                            first = (u == 0)
                            if lo < 512:
                                tensor.matmul(
                                    pden[0][:, lo:512], onesb_col[:],
                                    pbuf[sr][:, lo:512],
                                    start=first, stop=(u == 59))
                            tensor.matmul(
                                pden[1][:, max(0, lo - 512):512],
                                onesb_col[:],
                                pbuf[sr][:, max(512, lo):1024],
                                start=first, stop=(u == 63))
                            if lo < 512:
                                tensor.matmul(
                                    pO[0][:, lo:512],
                                    vtb[rot][:, 128 * ss:128 * (ss + 1)],
                                    pbuf[sr][:, lo:512],
                                    start=first, stop=(u == 59))
                            PE.inc(tensor.matmul(
                                pO[1][:, max(0, lo - 512):512],
                                vtb[rot][:, 128 * ss:128 * (ss + 1)],
                                pbuf[sr][:, max(512, lo):1024],
                                start=first, stop=(u == 63)), 1)
                        if ss == 7:
                            pe_r_done[8 * h + r] = PE.n

                    for r in range(8):
                        t_i = 8 * h + r
                        emit_loads(r)
                        for ss in range(8):
                            u = 8 * r + ss
                            gu = 64 * h + u
                            sr = u % 2
                            rot = r % 2
                            lo = 128 * ss

                            @block.tensor
                            def _(tensor):
                                if ss == 0:
                                    tensor.wait_ge(STs.h, m_kt[t_i])
                                if u == 0:
                                    if t_i == 0:
                                        tensor.wait_ge(E2.h, e_qT)
                                    else:
                                        tensor.wait_ge(DVE.h, d_bc[0])
                                if gu >= 2 and (gu - 2) in a_exp:
                                    tensor.wait_ge(ACTs.h, a_exp[gu - 2])
                                if lo < 512:
                                    tensor.matmul(
                                        pSd[sr][:, lo:512],
                                        ktb[rot][:, 128 * ss:128 * (ss + 1)],
                                        qT[:, h * 1024 + lo:h * 1024 + 512],
                                        start=True, stop=True)
                                PE.inc(tensor.matmul(
                                    pSd[sr][:, max(512, lo):1024],
                                    ktb[rot][:, 128 * ss:128 * (ss + 1)],
                                    qT[:, h * 1024 + max(512, lo):
                                       h * 1024 + 1024],
                                    start=True, stop=True), 1)
                            p_S[gu] = PE.n

                            @block.scalar
                            def _(scalar):
                                scalar.wait_ge(PE.h, p_S[gu])
                                ACTs.inc(scalar.activation(
                                    out=pbuf[sr][:, lo:1024],
                                    in_=pSd[sr][:, lo:1024], func=AF.Exp,
                                    bias=ngain[:, h:h + 1],
                                    scale=gain_bc[:, h:h + 1]), 1)
                            a_exp[gu] = ACTs.n

                            @block.vector
                            def _(vector):
                                vector.wait_ge(ACTs.h, a_exp[gu])
                                DVE.inc(vector.tensor_mul(
                                    pbuf[sr][:, lo:lo + 128],
                                    pbuf[sr][:, lo:lo + 128],
                                    mask_rb[rot][:, 128 * ss:
                                                  128 * (ss + 1)]), 1)
                            d_msk[gu] = DVE.n

                            if u >= 1:
                                emit_pv(u - 1)
                    emit_pv(63)

                    d_dn = DVE.n

                    @block.vector
                    def _(vector):
                        vector.wait_ge(PE.h, PE.n)
                        vector.tensor_copy(dinv[0:1, 0:512], pden[0][:])
                        DVE.inc(vector.tensor_copy(dinv[0:1, 512:1024],
                                                   pden[1][:]), 1)
                        DVE.wait(vector)
                        DVE.inc(vector.reciprocal(dinv[:], dinv[:]), 1)

                    p_b = PE.n

                    @block.tensor
                    def _(tensor):
                        tensor.wait_ge(DVE.h, d_dn + 2)
                        tensor.matmul(pSd[0][:, 0:512], of32g[:],
                                      dinv[0:1, 0:512],
                                      start=True, stop=True)
                        PE.inc(tensor.matmul(pSd[0][:, 512:1024], of32g[:],
                                             dinv[0:1, 512:1024],
                                             start=True, stop=True), 1)

                    @block.vector
                    def _(vector):
                        vector.wait_ge(PE.h, p_b + 1)
                        vector.tensor_copy(dinv_bc[:, 0:512],
                                           pSd[0][:, 0:512])
                        DVE.inc(vector.tensor_copy(dinv_bc[:, 512:1024],
                                                   pSd[0][:, 512:1024]), 1)
                        DVE.wait(vector)
                        vector.tensor_mul(qT[:, h * 1024:h * 1024 + 512],
                                          pO[0][:], dinv_bc[:, 0:512])
                        E2.inc(vector.tensor_mul(
                            qT[:, h * 1024 + 512:h * 1024 + 1024],
                            pO[1][:], dinv_bc[:, 512:1024]), 1)
                    d_bc[0] = DVE.n
                    e_head[h] = E2.n

                    if h % 4 == 3:
                        k4 = h // 4

                        @block.gpsimd
                        def _(gpsimd, k4=k4, eh=e_head[h]):
                            gpsimd.wait_ge(E2.h, eh)
                            Gq.inc(gpsimd.dma_start(
                                out=wo_sb[:, 4 * 2048 * k4:
                                          4 * 2048 * (k4 + 1)],
                                in_=wo_a[:, 4 * 2048 * k4:
                                         4 * 2048 * (k4 + 1)]), 16)

            oT_ready = E2.n
            g_wo = g_wo0

            with ExitStack() as wctx:
                pw = [wctx.enter_context(nc.psum_tensor(f"a_pw{i}",
                                                        [128, 512], F32))
                      for i in range(4)]

                for j in range(8):
                    p_w = PE.n

                    @block.tensor
                    def _(tensor):
                        if j == 0:
                            tensor.wait_ge(Gq.h, g_wo + 64)
                            tensor.wait_ge(E2.h, oT_ready)
                        else:
                            tensor.wait_ge(E2.h, E2.n)
                        last = None
                        for h in range(16):
                            for dd in range(4):
                                last = tensor.matmul(
                                    pw[dd][:],
                                    qT[:, h * 1024 + 128 * j:
                                       h * 1024 + 128 * (j + 1)],
                                    wo_sb[:, h * 2048 + 512 * dd:
                                          h * 2048 + 512 * (dd + 1)],
                                    start=(h == 0), stop=(h == 15))
                        PE.inc(last, 1)

                    @block.vector
                    def _(vector):
                        vector.wait_ge(PE.h, p_w + 1)
                        last = None
                        for dd in range(4):
                            last = vector.tensor_add(
                                xr[:, 2048 * j + 512 * dd:
                                   2048 * j + 512 * (dd + 1)],
                                pw[dd][:],
                                xr[:, 2048 * j + 512 * dd:
                                   2048 * j + 512 * (dd + 1)])
                        E2.inc(last, 1)

            cstk.close()
            astk.close()

        emit_ffn0(0, with_halo=(n_sub >= 2))
        if n_sub >= 2:
            emit_mamba()
        if n_sub >= 3:
            emit_ffn0(1, with_halo=False)
        hstk.close()
        if n_sub >= 4:
            emit_ffn1(0)
        if n_sub >= 5:
            emit_attn()
        if n_sub >= 6:
            emit_ffn1(1)

        @block.sync
        def _(sync):
            sync.wait_ge(E2.h, E2.n)
            for s in range(NSLOT):
                STs.inc(sync.dma_start(out=out[s * 128:(s + 1) * 128, :],
                                       in_=xr[:, s * D:(s + 1) * D]), 16)
            STs.wait(sync)

    return nc



def make_in_maps(inputs):
    f32 = np.float32
    bf = ml_dtypes.bfloat16
    x = np.asarray(inputs["L0_attn_u"]).reshape(8192, D)

    WinT_full = np.asarray(inputs["L0_attn_W_in"]).T
    WinT = np.ascontiguousarray(
        WinT_full[:, :8192].reshape(16, 128, 64, 128)
        .transpose(2, 1, 0, 3).reshape(64 * 128, 16 * 128))
    WinA = np.ascontiguousarray(
        WinT_full[:, 8192:8224].reshape(16, 128, 32)
        .transpose(1, 0, 2).reshape(128, 512))
    WoutT = np.ascontiguousarray(np.asarray(inputs["L0_attn_W_out"]).T)
    w0T = np.ascontiguousarray(np.asarray(inputs["L0_ffn_weight"]).T)
    WgT = np.ascontiguousarray(
        np.asarray(inputs["L1_ffn_gate_w"]).T.reshape(16, 128, 64, 128)
        .transpose(2, 1, 0, 3).reshape(64 * 128, 16 * 128))
    WuT = np.ascontiguousarray(
        np.asarray(inputs["L1_ffn_up_w"]).T.reshape(16, 128, 64, 128)
        .transpose(2, 1, 0, 3).reshape(64 * 128, 16 * 128))
    dwT = np.ascontiguousarray(np.asarray(inputs["L1_ffn_down_w"]).T)
    wqkv_t = np.ascontiguousarray(
        np.asarray(inputs["L1_attn_w_qkv"]).reshape(16, 128, 12, 512)
        .transpose(2, 1, 0, 3).reshape(12 * 128, 16 * 512))
    wo_t = np.ascontiguousarray(
        np.asarray(inputs["L1_attn_w_out"]).reshape(16, 128, 2048)
        .transpose(1, 0, 2).reshape(128, 16 * 2048))

    convw = np.ascontiguousarray(
        np.asarray(inputs["L0_attn_conv_w"]).reshape(48, 128, 4)
        .transpose(1, 0, 2).reshape(128, 192))
    convb = np.ascontiguousarray(
        np.asarray(inputs["L0_attn_conv_b"]).reshape(48, 128).T)

    ident_f = np.eye(128, dtype=f32)
    ident_b = np.eye(128, dtype=bf)
    ident_b64 = np.ascontiguousarray(np.tile(np.eye(64, dtype=bf), (2, 1)))
    ident_q = np.eye(128, dtype=ml_dtypes.float8_e4m3)
    triu_f = np.triu(np.ones((128, 128), dtype=f32))
    tril_i = np.ascontiguousarray(
        np.tril(np.ones((128, 128), dtype=np.int8), -1))

    half = 64
    inv_freq = (10000.0 ** (-np.arange(half, dtype=f32) / half)).astype(f32)
    theta = np.arange(8192, dtype=f32)[:, None] * inv_freq[None, :]
    cos_t = np.cos(theta).astype(f32)
    sin_t = np.sin(theta).astype(f32)

    tri_b = np.triu(np.ones((128, 128), dtype=f32)).astype(bf)
    one_t = np.ones((128, 128), dtype=bf)
    zero_t = np.zeros((128, 128), dtype=bf)

    in_maps = []
    for c in range(8):
        gs = [gtile(c, s) for s in range(NSLOT)]
        rows = np.concatenate([np.arange(g * 128, (g + 1) * 128) for g in gs])
        x0c = np.ascontiguousarray(x[rows])
        x0h = np.zeros((128, D), dtype=f32)
        halo_scale = np.ones((1, 24), dtype=f32)
        for s, g in enumerate(gs):
            if g == 0:
                halo_scale[0, 3 * s:3 * s + 3] = 0.0
            else:
                x0h[3 * s:3 * s + 3] = x[g * 128 - 3:g * 128]
        selb = np.zeros((64, 64), dtype=f32)
        for g in range(64):
            if g % 8 == (c + g // 8) % 8:
                selb[:, g] = 1.0
        mk = np.empty((8, 8, 128, 128), dtype=bf)
        for r in range(8):
            for s in range(NSLOT):
                kp = (r + s) % 8
                op = (c + s) % 8
                mk[r, s] = one_t if kp < op else (tri_b if kp == op else zero_t)
        mk = np.ascontiguousarray(mk.transpose(0, 2, 1, 3).reshape(8, 128, 1024))

        in_maps.append({
            "x0": x0c, "x0h": x0h, "halo_scale": halo_scale, "selb": selb,
            "masks": mk,
            "rope_cos": np.ascontiguousarray(cos_t[rows]),
            "rope_sin": np.ascontiguousarray(sin_t[rows]),
            "ident_f": ident_f, "ident_b": ident_b, "ident_b64": ident_b64,
            "ident_q": ident_q,
            "triu_f": triu_f, "tril_i": tril_i,
            "w0T": w0T, "b0row": np.asarray(inputs["L0_ffn_bias"]).reshape(1, D),
            "WinT": WinT, "WinA": WinA, "WoutT": WoutT,
            "convw": convw, "convb": convb,
            "zbF": np.ascontiguousarray(
                np.asarray(inputs["L0_attn_z_bias"]).reshape(16, 128).T),
            "Drow": np.asarray(inputs["L0_attn_D"]).reshape(1, NH_M),
            "wqkv": wqkv_t,
            "wo_a": wo_t,
            "grow": np.asarray(inputs["L1_attn_q_gain"]).reshape(1, NH_A),
            "WgT": WgT, "WuT": WuT, "dwT": dwT,
            "gs_in": np.asarray(inputs["L1_ffn_gate_scale"]).reshape(64, 128),
            "us_in": np.asarray(inputs["L1_ffn_up_scale"]).reshape(64, 128),
            "ds_in": np.asarray(inputs["L1_ffn_down_scale"]).reshape(16, 128),
        })
    return in_maps


def assemble(results):
    full = np.empty((1, 8192, D), dtype=np.float32)
    for c in range(8):
        o = results[c]["out"]
        for s in range(NSLOT):
            g = gtile(c, s)
            full[0, g * 128:(g + 1) * 128] = o[s * 128:(s + 1) * 128]
    return full


def kernel(**inputs):
    in_maps = make_in_maps(inputs)
    nc = build(N_SUB_DEFAULT)
    res = run_bass_kernel_spmd(nc, in_maps, core_ids=list(range(8)))
    return assemble(res.results)
